# version 10
# baseline (speedup 1.0000x reference)
import math
import numpy as np

import concourse.bass as bass
import concourse.mybir as mybir
from concourse.bass_utils import run_bass_kernel_spmd

F32 = mybir.dt.float32
BF16 = mybir.dt.bfloat16
I32 = mybir.dt.int32
AF = mybir.ActivationFunctionType
ALU = mybir.AluOpType
AX = mybir.AxisListType.X
ET = mybir.EngineType

NCO = 8
S_FULL = 8192
TS = 1024
D = 2048
NSUB = 8
SA = 1.0 / math.sqrt(128.0)
TWO_PI = 2.0 * math.pi
EPS_LN = 1e-5
EPS_RMS = 1e-6
DEBUG_TAPS = False
RG8 = [list(range(NCO))]


_TRAILER = {}


class Sem:

    def __init__(self, nc, name):
        self.h = nc.alloc_semaphore(name)
        self.n = 0

    def inc(self, inst, k=1):
        eng = str(inst.ins.engine) if hasattr(inst, "ins") else ""
        tr = _TRAILER.get(eng)
        if tr is not None:
            tr(self.h, k)
        else:
            inst.then_inc(self.h, k)
        self.n += k




_GUARD_OPS = {
    "vector": ["memset", "memzero", "tensor_copy", "tensor_tensor", "tensor_add",
               "tensor_sub", "tensor_mul", "tensor_max", "tensor_scalar",
               "tensor_single_scalar", "reduce_max", "reduce_sum", "reciprocal",
               "scalar_tensor_tensor"],
    "scalar": ["activation", "copy", "mul", "add", "memzero"],
}


def _ap_region(ap):
    try:
        name = ap.tensor.name
        off = int(ap.offset)
        span = 1
        for step, count in ap.ap:
            span += (int(count) - 1) * abs(int(step))
        return (name, off, off + span)
    except Exception:
        return ("?", 0, 1 << 60)


def _install_engine_guards(nc):
    import collections

    for engname in ("vector", "scalar"):
        eng = getattr(nc, engname)
        sem = nc.alloc_semaphore(f"grd_{engname}")
        state = {"cnt": 0, "recent": collections.deque(maxlen=8)}

        def mk(opname, eng=eng, sem=sem, state=state):
            orig = getattr(type(eng), opname)

            def wrapped(*args, **kwargs):
                aps = [a for a in list(args) + list(kwargs.values())
                       if isinstance(a, bass.AP)]
                if not aps:
                    inst = orig(eng, *args, **kwargs)
                    state["cnt"] += 1
                    inst.then_inc(sem, 1)
                    state["recent"].append((state["cnt"], [], []))
                    return inst
                regs = [_ap_region(a) for a in aps]
                writes = [regs[0]]
                acc = kwargs.get("accum_out")
                if isinstance(acc, bass.AP):
                    writes.append(_ap_region(acc))
                reads = regs[1:]

                def ovl(a, b):
                    return a[0] == b[0] and a[1] < b[2] and b[1] < a[2]

                need = 0
                for idx, w_prev, r_prev in state["recent"]:
                    hit = False
                    for r in regs:
                        if any(ovl(r, w) for w in w_prev):
                            hit = True
                    for w in writes:
                        if any(ovl(w, r) for r in r_prev):
                            hit = True
                    if hit:
                        need = max(need, idx)
                if need:
                    eng.wait_ge(sem, need)
                inst = orig(eng, *args, **kwargs)
                state["cnt"] += 1
                inst.then_inc(sem, 1)
                state["recent"].append((state["cnt"], writes, reads))
                return inst

            return wrapped

        for opname in _GUARD_OPS[engname]:
            setattr(eng, opname, mk(opname))

def build_program(race_detect=False, upto=99):
    nc = bass.Bass("TRN2", target_bir_lowering=False,
                   detect_race_conditions=race_detect)
    _install_engine_guards(nc)

    def di(name, shape):
        return nc.dram_tensor(name, shape, F32, kind="ExternalInput").ap()

    x_sl = di("x_sl", [TS, D])
    Wq, Wk, Wv, Wo = di("Wq", [D, D]), di("Wk", [D, D]), di("Wv", [D, D]), di("Wo", [D, D])
    bq, bk, bv, bo = di("bq", [D]), di("bk", [D]), di("bv", [D]), di("bo", [D])
    fwq, fwk, fwv, fwo = di("fwq", [D, D]), di("fwk", [D, 512]), di("fwv", [D, 512]), di("fwo", [D, D])
    rw = di("rw", [D, 8])
    wg, wu, wd = di("wg", [8, D, 1024]), di("wu", [8, D, 1024]), di("wd", [8, 1024, D])
    ln1w_r = di("ln1w_r", [128, D])
    ln2w_r = di("ln2w_r", [128, D])
    qnw_r = di("qnw_r", [128, D])
    knw_r = di("knw_r", [128, 512])
    rbias_r = di("rbias_r", [128, 8])
    pos_in = di("pos_in", [TS])
    iidx_in = di("iidx_in", [128, 32])

    y_out = nc.dram_tensor("y", [TS, D], F32, kind="ExternalOutput").ap()

    def dint(name, shape, dt=BF16):
        return nc.dram_tensor(name, shape, dt, kind="Internal").ap()

    agA_in = dint("agA_in", [TS, 2 * D])
    agA_out = nc.dram_tensor("agA_out", [NCO, TS, 2 * D], BF16, kind="Internal", addr_space="Shared").ap()
    agB_in = dint("agB_in", [256, S_FULL])
    agB_out = nc.dram_tensor("agB_out", [NCO, 256, S_FULL], BF16, kind="Internal", addr_space="Shared").ap()
    agC_in = dint("agC_in", [TS, 3072])
    agC_out = nc.dram_tensor("agC_out", [NCO, TS, 3072], BF16, kind="Internal", addr_space="Shared").ap()
    agD_in = dint("agD_in", [256, S_FULL])
    agD_out = nc.dram_tensor("agD_out", [NCO, 256, S_FULL], BF16, kind="Internal", addr_space="Shared").ap()
    lnx_d = dint("lnx_d", [TS, D])
    q_d = dint("q_d", [TS, D])
    qraw_d = dint("qraw_d", [TS, D])
    h_d = dint("h_d", [TS, D])
    h2_d = dint("h2_d", [TS, D])
    x2_d = dint("x2_d", [TS, D], F32)
    li2_d = dint("li2_d", [TS, D], F32)
    x1_d = dint("x1_d", [TS, D], F32)
    ctr_d = dint("ctr_d", [8, TS], F32)

    taps = {}
    if DEBUG_TAPS:
        for nm, shape, dt in [("tap_lnx", [TS, D], BF16), ("tap_qd", [TS, D], BF16),
                              ("tap_agA", [TS, 2 * D], BF16), ("tap_agB", [256, S_FULL], BF16),
                              ("tap_x2", [TS, D], F32), ("tap_h", [TS, D], BF16),
                              ("tap_agC", [TS, 3072], BF16), ("tap_agD", [256, S_FULL], BF16),
                              ("tap_x1", [TS, D], F32), ("tap_h2", [TS, D], BF16),
                              ("tap_ctr", [8, TS], F32), ("tap_li2", [TS, D], F32)]:
            taps[nm] = nc.dram_tensor(nm, shape, dt, kind="ExternalOutput").ap()

    def sbt(name, cols, dt=BF16, parts=128):
        return nc.alloc_sbuf_tensor(name, [parts, cols], dt)

    m_diag = sbt("m_diag", 2048)
    ones_b = sbt("ones_b", 128)
    onesf = sbt("onesf", 128, F32, parts=1)
    act_fm = sbt("act_fm", 16 * TS)
    wstage = sbt("wstage", 16 * TS)
    kvz = sbt("kvz", 32768)
    sha = sbt("sha", 8192)
    bias_row = sbt("bias_row", D, BF16, parts=1)
    idt = sbt("idt", 128, F32)
    rwb = sbt("rwb", 128)

    xt = sbt("xt", D, F32)
    ft = sbt("ft", D, F32)
    gt = sbt("gt", D, F32)
    rt = sbt("rt", 512, F32)
    wrep = sbt("wrep", D, BF16)
    rb_rep = sbt("rb_rep", 8, F32)
    linv = sbt("linv", 512, F32)
    csc = sbt("csc", 256, F32)
    ssc = sbt("ssc", 256, F32)
    crep = sbt("crep", 512, F32)
    srep = sbt("srep", 512, F32)
    stats = sbt("stats", 16, F32)
    ssqa = sbt("ssqa", NSUB, F32)
    cons = sbt("cons", 8, F32)
    rout = sbt("rout", 64, F32)
    poscol = sbt("poscol", NSUB, F32)
    iidx = sbt("iidx", 32, F32)
    cbc = sbt("cbc", TS, F32)
    sgb = sbt("sgb", 512, BF16)
    ki = sbt("ki", 32, I32)
    lrow = [crep[0:1, 0:512], srep[0:1, 0:512]]
    ctr8 = [crep[0:8, :], srep[0:8, :]]

    qslab = [sha[:, 0:512], sha[:, 512:1024]]
    pt = [sha[:, 1024:1536], sha[:, 1536:2048], sha[:, 3072:3584], sha[:, 3584:4096]]
    qsq = sha[:, 2048:2560]
    outsb = sha[:, 2560:3072]
    stg = [sha[:, 4096:6144], sha[:, 6144:8192]]
    lhsb = [sha[:, 0:2048], sha[:, 2048:4096]]
    wd_lo = sha[:, 0:8192]
    kslab = [kvz[:, 0:8192], kvz[:, 16384:24576]]
    vslab = [kvz[:, 8192:16384], kvz[:, 24576:32768]]
    macc = kvz[:, 0:16384]
    actb = kvz[:, 16384:24576]
    wd_hi = kvz[:, 24576:32768]

    ps = [nc.alloc_psum_tensor(f"ps{i}", [128, 512], F32) for i in range(8)]

    _raw_vcopy = type(nc.vector).tensor_copy
    _raw_scopy = type(nc.scalar).copy
    _TRAILER.clear()
    _TRAILER["EngineType.DVE"] = (
        lambda h, k: _raw_vcopy(nc.vector, cons[:, 7:8], cons[:, 0:1]).then_inc(h, k))
    _TRAILER["EngineType.Activation"] = (
        lambda h, k: _raw_scopy(nc.scalar, cons[:, 7:8], cons[:, 0:1]).then_inc(h, k))

    sG = Sem(nc, "sG")
    sSy = Sem(nc, "sSy")
    sT = Sem(nc, "sT")
    sA = Sem(nc, "sA")
    sV = Sem(nc, "sV")
    sCC = Sem(nc, "sCC")
    sO = Sem(nc, "sO")
    sqk = Sem(nc, "sqk")
    sxp = Sem(nc, "sxp")
    smk = Sem(nc, "smk")
    spv = Sem(nc, "spv")
    sbsq = Sem(nc, "sbsq")
    sbcs = Sem(nc, "sbcs")
    sm2 = Sem(nc, "sm2")
    sbe = Sem(nc, "sbe")
    snb = Sem(nc, "snb")
    soe = Sem(nc, "soe")
    sGRD = Sem(nc, "sGRD")


    def _cut():
        with nc.Block("fin") as blk:
            @blk.gpsimd
            def _(g):
                g.wait_ge(sO.h, sO.n)
        return nc

    with nc.Block("p0") as blk:
        @blk.gpsimd
        def _(g):
            g.memset(ones_b[:, :], 1.0)
            g.memset(onesf[:, :], 1.0)
            g.memset(cons[:, 0:1], 0.0)
            g.memset(cons[:, 1:2], EPS_LN)
            g.memset(cons[:, 2:3], EPS_RMS)
            g.memset(cons[:, 3:4], math.log(SA))
            inst = g.memset(m_diag[:, :], 1.0)
            sG.inc(inst, 16) if False else inst.then_inc(sGRD.h, 1)
            sGRD.n += 1
            g.wait_ge(sGRD.h, sGRD.n)
            for a in range(4):
                g.affine_select(out=m_diag[:, 512 * a:512 * (a + 1)],
                                in_=m_diag[:, 512 * a:512 * (a + 1)],
                                compare_op=ALU.is_ge, fill=0.0, base=-128 * a,
                                pattern=[[1, 512]], channel_multiplier=-1)
            inst = g.memset(idt[:, :], 0.0)
            inst.then_inc(sGRD.h, 1)
            sGRD.n += 1
            g.wait_ge(sGRD.h, sGRD.n)
            inst = g.affine_select(out=idt[:, :], in_=idt[:, :],
                                   compare_op=ALU.not_equal, fill=1.0, base=0,
                                   pattern=[[-1, 128]], channel_multiplier=1)
            sV.inc(inst)
            for s in range(NSUB):
                inst = g.dma_start(out=poscol[:, s:s + 1],
                                   in_=pos_in[s * 128:(s + 1) * 128])
                sG.inc(inst, 16)
            inst = g.dma_start(out=iidx[:, :], in_=iidx_in)
            sG.inc(inst, 16)
            inst = g.dma_start(out=rb_rep[:, :], in_=rbias_r)
            sG.inc(inst, 16)

        @blk.scalar
        def _(sc):
            sc.wait_ge(sG.h, sG.n)
            sc.wait_ge(sV.h, sV.n)
            inst = sc.activation(ft[:, 0:32], iidx[:, :], AF.Exp,
                                 bias=cons[:, 0:1], scale=-math.log(1e6) / 32.0)
            sA.inc(inst)

        @blk.vector
        def _(v):
            v.wait_ge(sA.h, sA.n)
            for s in range(NSUB):
                v.tensor_scalar(gt[:, 0:32], ft[:, 0:32], poscol[:, s:s + 1], None, ALU.mult)
                for dst, shift in ((csc, 0.5 * math.pi), (ssc, 0.0)):
                    v.tensor_scalar(gt[:, 32:64], gt[:, 0:32], shift, 1.0 / TWO_PI,
                                    ALU.add, ALU.mult)
                    v.tensor_copy(ki[:, 0:32], gt[:, 32:64])
                    v.tensor_copy(gt[:, 64:96], ki[:, 0:32])
                    v.tensor_scalar(gt[:, 96:128], gt[:, 64:96], -TWO_PI, shift,
                                    ALU.mult, ALU.add)
                    dcol = dst[:, 32 * s:32 * (s + 1)]
                    v.tensor_add(dcol, gt[:, 0:32], gt[:, 96:128])
                    v.tensor_scalar(gt[:, 32:64], dcol, math.pi, None, ALU.is_gt)
                    v.scalar_tensor_tensor(dcol, gt[:, 32:64], -TWO_PI, dcol,
                                           ALU.mult, ALU.add)
                    v.tensor_scalar(gt[:, 32:64], dcol, -math.pi, None, ALU.is_lt)
                    inst = v.scalar_tensor_tensor(dcol, gt[:, 32:64], TWO_PI, dcol,
                                                  ALU.mult, ALU.add)
            sV.inc(inst)

        @blk.scalar
        def _(sc):
            sc.wait_ge(sV.h, sV.n)
            sc.activation(csc[:, :], csc[:, :], AF.Sin, bias=cons[:, 0:1], scale=1.0)
            inst = sc.activation(ssc[:, :], ssc[:, :], AF.Sin, bias=cons[:, 0:1], scale=1.0)
            sA.inc(inst)

    if upto <= 1:
        return _cut()

    def emit_ln_phase(blk, load_fn, store_fn):
        g0, a0, v0 = sG.n, sA.n, sV.n

        @blk.gpsimd
        def _(g):
            for s in range(NSUB):
                g.wait_ge(sV.h, v0 + 2 * s)
                load_fn(g, s)
                g.wait_ge(sV.h, v0 + 2 * s + 2)
                store_fn(g, s)

        @blk.scalar
        def _(sc):
            for s in range(NSUB):
                sc.wait_ge(sG.h, g0 + 32 * s + 16)
                inst = sc.activation(ft[:, :], xt[:, :], AF.Square, bias=cons[:, 0:1],
                                     scale=1.0, accum_out=stats[:, 1:2])
                sA.inc(inst)
                sc.wait_ge(sV.h, v0 + 2 * s + 1)
                sc.activation(stats[:, 6:7], stats[:, 4:5], AF.Ln,
                              bias=cons[:, 1:2], scale=1.0)
                inst = sc.activation(stats[:, 5:6], stats[:, 6:7], AF.Exp,
                                     bias=cons[:, 0:1], scale=-0.5)
                sA.inc(inst)

        @blk.vector
        def _(v):
            for s in range(NSUB):
                v.wait_ge(sG.h, g0 + 32 * s + 16)
                v.reduce_sum(stats[:, 0:1], xt[:, :], axis=AX)
                v.wait_ge(sA.h, a0 + 2 * s + 1)
                v.tensor_scalar(stats[:, 2:3], stats[:, 0:1], 1.0 / D, None, ALU.mult)
                v.tensor_scalar(stats[:, 3:4], stats[:, 1:2], 1.0 / D, None, ALU.mult)
                v.tensor_mul(stats[:, 4:5], stats[:, 2:3], stats[:, 2:3])
                inst = v.tensor_sub(stats[:, 4:5], stats[:, 3:4], stats[:, 4:5])
                sV.inc(inst)
                v.wait_ge(sA.h, a0 + 2 * s + 2)
                v.wait_ge(sG.h, g0 + 32 * s)
                inst = v.tensor_scalar(stg[0][:, :], xt[:, :], stats[:, 2:3],
                                       stats[:, 5:6], ALU.subtract, ALU.mult)
                sV.inc(inst)

    with nc.Block("p1") as blk:
        def p1_load(g, s):
            sG.inc(g.dma_start(out=xt[:, :], in_=x_sl[s * 128:(s + 1) * 128, :]), 16)

        def p1_store(g, s):
            sG.inc(g.dma_start(out=lnx_d[s * 128:(s + 1) * 128, :], in_=stg[0][:, :]), 16)
        emit_ln_phase(blk, p1_load, p1_store)

    with nc.Block("p1t") as blk:
        @blk.sync
        def _(s):
            s.wait_ge(sG.h, sG.n)
            for ch in range(16):
                inst = s.dma_start(out=act_fm[:, ch * TS:(ch + 1) * TS],
                                   in_=lnx_d[:, ch * 128:(ch + 1) * 128], transpose=True)
                sSy.inc(inst, 16)

    if upto <= 2:
        return _cut()

    def emit_stream_proj(w_ap, b_ap, dst_fn, accum_ssq=False, name="proj"):
        for half in range(2):
            with nc.Block(f"{name}_{half}") as blk:
                @blk.gpsimd
                def _(g, half=half):
                    g.wait_ge(sT.h, sT.n)
                    for ch in range(16):
                        inst = g.dma_start(
                            out=wstage[:, ch * 1024:(ch + 1) * 1024],
                            in_=w_ap[ch * 128:(ch + 1) * 128, half * 1024:(half + 1) * 1024])
                        sG.inc(inst, 16)
                    if b_ap is not None:
                        inst = g.dma_start(out=bias_row[0:1, 0:1024],
                                           in_=b_ap[half * 1024:(half + 1) * 1024])
                        sG.inc(inst, 16)

                t0, a0, v0, g0 = sT.n, sA.n, sV.n, sG.n
                sy_snap = sSy.n
                soe_snap = soe.n

                @blk.tensor
                def _(t, t0=t0, v0=v0, g0=g0, sy_snap=sy_snap, soe_snap=soe_snap,
                      half=half):
                    t.wait_ge(sG.h, g0)
                    t.wait_ge(sSy.h, sy_snap)
                    t.wait_ge(sV.h, v0)
                    t.wait_ge(soe.h, soe_snap)
                    it = 0
                    for s in range(NSUB):
                        for ob in range(2):
                            bank = ps[it % 2]
                            if it >= 2:
                                t.wait_ge(sV.h, v0 + it - 1)
                            for ch in range(16):
                                inst = t.matmul(bank[:, :],
                                                act_fm[:, ch * TS + s * 128:ch * TS + (s + 1) * 128],
                                                wstage[:, ch * 1024 + ob * 512:ch * 1024 + (ob + 1) * 512],
                                                start=(ch == 0),
                                                stop=(b_ap is None and ch == 15))
                            if b_ap is not None:
                                inst = t.matmul(bank[:, :], ones_b[0:1, 0:128],
                                                bias_row[0:1, ob * 512:(ob + 1) * 512],
                                                start=False, stop=True)
                            sT.inc(inst)
                            it += 1

                if accum_ssq:
                    @blk.scalar
                    def _(sc, t0=t0, v0=v0):
                        it = 0
                        for s in range(NSUB):
                            for ob in range(2):
                                sc.wait_ge(sT.h, t0 + it + 1)
                                if it >= 1:
                                    sc.wait_ge(sV.h, v0 + it)
                                inst = sc.activation(qsq[:, :], ps[it % 2][:, :], AF.Square,
                                                     bias=cons[:, 0:1], scale=1.0,
                                                     accum_out=stats[:, 8:9])
                                sA.inc(inst)
                                it += 1

                    @blk.vector
                    def _(v, a0=a0, g0=g0, half=half):
                        it = 0
                        for s in range(NSUB):
                            for ob in range(2):
                                v.wait_ge(sA.h, a0 + it + 1)
                                if half == 0 and ob == 0:
                                    v.tensor_copy(ssqa[:, s:s + 1], stats[:, 8:9])
                                else:
                                    v.tensor_add(ssqa[:, s:s + 1], ssqa[:, s:s + 1],
                                                 stats[:, 8:9])
                                if s >= 2:
                                    v.wait_ge(sG.h, g0 + 16 * (s - 1))
                                inst = v.tensor_copy(stg[s % 2][:, ob * 512:(ob + 1) * 512],
                                                     ps[it % 2][:, :])
                                sV.inc(inst)
                                it += 1
                else:
                    @blk.vector
                    def _(v, t0=t0, g0=g0):
                        it = 0
                        for s in range(NSUB):
                            for ob in range(2):
                                v.wait_ge(sT.h, t0 + it + 1)
                                if s >= 2:
                                    v.wait_ge(sG.h, g0 + 16 * (s - 1))
                                inst = v.tensor_copy(stg[s % 2][:, ob * 512:(ob + 1) * 512],
                                                     ps[it % 2][:, :])
                                sV.inc(inst)
                                it += 1

                @blk.gpsimd
                def _(g, v0=v0, half=half):
                    for s in range(NSUB):
                        g.wait_ge(sV.h, v0 + 2 * (s + 1))
                        dst_fn(g, s, half)

    def mk_dst(dst_d, col_off):
        def f(g, s, half):
            inst = g.dma_start(
                out=dst_d[s * 128:(s + 1) * 128,
                          col_off + half * 1024:col_off + (half + 1) * 1024],
                in_=stg[s % 2][:, 0:1024])
            sG.inc(inst, 16)
        return f

    emit_stream_proj(Wq, bq, mk_dst(q_d, 0), name="p2q")
    emit_stream_proj(Wk, bk, mk_dst(agA_in, 0), name="p2k")
    emit_stream_proj(Wv, bv, mk_dst(agA_in, D), name="p2v")

    with nc.Block("p3") as blk:
        @blk.gpsimd
        def _(g):
            g.wait_ge(sG.h, sG.n)
            inst = g.collective_compute("AllGather", ALU.bypass, replica_groups=RG8,
                                        ins=[agA_in], outs=[agA_out])
            sCC.inc(inst)

    if upto <= 3:
        return _cut()

    def emit_attention(tag, kv_body_fn, q_in_switch, q_tr_fn, out_ap):
        slabs = [(hb, sl) for hb in range(2) for sl in range(16)]
        M_of = {(hb, sl): 4 * (sl + 1) for (hb, sl) in slabs}

        QK, EX, MK, PV = {}, {}, {}, {}
        SQk, SQq, CSk, CSq = {}, {}, {}, {}
        KP, M2, BE, NB, OE, QTR, OSTORE = {}, {}, {}, {}, {}, {}, {}
        cqk, cex, cmk, cpv = sqk.n, sxp.n, smk.n, spv.n
        csq, ccs, ckp = sbsq.n, sbcs.n, sV.n
        cm2, cbe, cnb, coe = sm2.n, sbe.n, snb.n, soe.n
        nk_tr = 16 * 16
        qtr_base = sSy.n + nk_tr
        cg_store = sG.n + 16 * 16
        for idx, (hb, sl) in enumerate(slabs):
            M = M_of[(hb, sl)]
            if sl == 0:
                for i in range(16):
                    csq += 1; SQk[(hb, i)] = csq
                    ccs += 1; CSk[(hb, i)] = ccs
                    ckp += 1; KP[(hb, i)] = ckp
            csq += 1; SQq[idx] = csq
            ccs += 1; CSq[idx] = ccs
            cm2 += 1; M2[idx] = cm2
            cbe += 1; BE[idx] = cbe
            cnb += 1; NB[idx] = cnb
            QTR[idx] = qtr_base + 16 * (idx + 1)
            for m in range(M):
                cqk += 1; QK[(idx, m)] = cqk
                cex += 1; EX[(idx, m)] = cex
                cpv += 1; PV[(idx, m)] = cpv
                if m >= 4 * sl:
                    cmk += 1; MK[(idx, m)] = cmk
            coe += 1; OE[idx] = coe
            cg_store += 16
            OSTORE[idx] = cg_store
        kv_ready_g = sG.n + 16 * 16
        khead_sy = {0: sSy.n + 16 * 8, 1: sSy.n + 16 * 16}
        v_snap = sV.n

        rg = nc.gpsimd.compute_val(nc.partition_id([ET.Pool]))
        rsy = nc.sync.compute_val(nc.partition_id([ET.SP]))
        qd_snap = sG.n
        for i in nc.Switch(engines=[nc.gpsimd, nc.sync], index=[rg, rsy], n=NCO):
            nc.gpsimd.wait_ge(sCC.h, sCC.n)
            nc.sync.wait_ge(sCC.h, sCC.n)
            for hb in range(2):
                kv_body_fn(i, hb)
            if q_in_switch:
                for idx, (hb, sl) in enumerate(slabs):
                    if idx >= 2:
                        nc.sync.wait_ge(sqk.h, QK[(idx - 1, 0)] - 1)
                    nc.sync.dma_start(out=qslab[idx % 2][:, :],
                                      in_=q_tr_fn(i, hb, sl),
                                      transpose=True).then_inc(sSy.h, 16)
        sSy.n += nk_tr + (16 * 32 if q_in_switch else 0)
        sG.n += 16 * 16

        with nc.Block(f"{tag}_att") as blk:
            if not q_in_switch:
                @blk.sync
                def _(s):
                    s.wait_ge(sG.h, qd_snap)
                    for idx, (hb, sl) in enumerate(slabs):
                        if idx >= 2:
                            s.wait_ge(sqk.h, QK[(idx - 1, 0)] - 1)
                        inst = s.dma_start(out=qslab[idx % 2][:, :],
                                           in_=q_tr_fn(None, hb, sl), transpose=True)
                        sSy.inc(inst, 16)

            @blk.tensor
            def _(t):
                t.wait_ge(sG.h, kv_ready_g)
                t.wait_ge(sV.h, v_snap)
                for idx, (hb, sl) in enumerate(slabs):
                    M = M_of[(hb, sl)]
                    if sl == 0:
                        for i in range(16):
                            t.wait_ge(sbsq.h, SQk[(hb, i)])
                            if i >= 1:
                                t.wait_ge(sV.h, KP[(hb, i - 1)])
                            elif idx > 0:
                                t.wait_ge(soe.h, OE[idx - 1])
                            inst = t.matmul(ps[6][:, :], ones_b[:, 0:128], qsq[:, :],
                                            start=True, stop=True)
                            sbcs.inc(inst)
                    t.wait_ge(sbsq.h, SQq[idx])
                    if sl == 0:
                        t.wait_ge(sV.h, KP[(hb, 15)])
                    if idx > 0:
                        t.wait_ge(soe.h, OE[idx - 1])
                    inst = t.matmul(ps[6][:, :], ones_b[:, 0:128], qsq[:, :],
                                    start=True, stop=True)
                    sbcs.inc(inst)

                    def emit_pv(mp, M=M, sl=sl, idx=idx, hb=hb, t=t):
                        apv = mp - 4 * sl
                        jp = 128 * apv if apv > 0 else 0
                        t.wait_ge(sxp.h, EX[(idx, mp)])
                        if apv >= 0:
                            t.wait_ge(smk.h, MK[(idx, mp)])
                        t.matmul(ps[4][:, jp:512],
                                 vslab[hb][:, 128 * mp:128 * (mp + 1)],
                                 pt[mp % 4][:, jp:512],
                                 start=(mp == 0), stop=(mp == M - 1))
                        inst = t.matmul(ps[5][:, jp:512], ones_b[:, 0:128],
                                        pt[mp % 4][:, jp:512],
                                        start=(mp == 0), stop=(mp == M - 1))
                        spv.inc(inst)

                    for m in range(M):
                        a = m - 4 * sl
                        j0 = 128 * a if a > 0 else 0
                        if m >= 4:
                            t.wait_ge(sxp.h, EX[(idx, m - 4)])
                        inst = t.matmul(ps[m % 4][:, j0:512],
                                        kslab[hb][:, 128 * m:128 * (m + 1)],
                                        qslab[idx % 2][:, j0:512], start=True, stop=True)
                        sqk.inc(inst)
                        if m >= 2:
                            emit_pv(m - 2)
                    emit_pv(M - 2)
                    emit_pv(M - 1)

            @blk.scalar
            def _(sc):
                for idx, (hb, sl) in enumerate(slabs):
                    M = M_of[(hb, sl)]
                    if sl == 0:
                        sc.wait_ge(sSy.h, khead_sy[hb])
                        for i in range(16):
                            if i >= 1:
                                sc.wait_ge(sbcs.h, CSk[(hb, i - 1)])
                            inst = sc.activation(qsq[:, :],
                                                 kslab[hb][:, 512 * i:512 * (i + 1)],
                                                 AF.Square, bias=cons[:, 0:1], scale=1.0)
                            sbsq.inc(inst)
                    sc.wait_ge(sSy.h, QTR[idx])
                    sc.wait_ge(sbcs.h, CSk[(hb, 15)] if sl == 0 else CSq[idx] - 1)
                    inst = sc.activation(qsq[:, :], qslab[idx % 2][:, :], AF.Square,
                                         bias=cons[:, 0:1], scale=1.0)
                    sbsq.inc(inst)
                    sc.wait_ge(sm2.h, M2[idx])
                    sc.activation(stats[:, 9:10], stats[:, 8:9], AF.Ln,
                                  bias=cons[:, 0:1], scale=1.0)
                    inst = sc.activation(stats[:, 10:11], stats[:, 9:10], AF.Exp,
                                         bias=cons[:, 3:4], scale=0.5)
                    sbe.inc(inst)
                    for m in range(M):
                        a = m - 4 * sl
                        j0 = 128 * a if a > 0 else 0
                        sc.wait_ge(sqk.h, QK[(idx, m)])
                        sc.wait_ge(snb.h, NB[idx])
                        if m >= 4:
                            sc.wait_ge(spv.h, PV[(idx, m - 4)])
                        elif idx > 0:
                            sc.wait_ge(spv.h, PV[(idx - 1, M_of[slabs[idx - 1]] - 1)])
                        inst = sc.activation(pt[m % 4][:, j0:512], ps[m % 4][:, j0:512],
                                             AF.Exp, bias=stats[:, 11:12], scale=SA)
                        sxp.inc(inst)

            @blk.vector
            def _(v):
                for idx, (hb, sl) in enumerate(slabs):
                    M = M_of[(hb, sl)]
                    if sl == 0:
                        for i in range(16):
                            v.wait_ge(sbcs.h, CSk[(hb, i)])
                            if i == 0:
                                inst = v.reduce_max(stats[:, 7:8], ps[6][:, :], axis=AX)
                            else:
                                v.reduce_max(stats[:, 12:13], ps[6][:, :], axis=AX)
                                inst = v.tensor_max(stats[:, 7:8], stats[:, 7:8],
                                                    stats[:, 12:13])
                            sV.inc(inst)
                    v.wait_ge(sbcs.h, CSq[idx])
                    v.reduce_max(stats[:, 12:13], ps[6][:, :], axis=AX)
                    inst = v.tensor_mul(stats[:, 8:9], stats[:, 12:13], stats[:, 7:8])
                    sm2.inc(inst)
                    v.wait_ge(sbe.h, BE[idx])
                    inst = v.tensor_scalar(stats[:, 11:12], stats[:, 10:11], -1.0,
                                           None, ALU.mult)
                    snb.inc(inst)
                    for m in range(M):
                        a = m - 4 * sl
                        if a >= 0:
                            j0 = 128 * a if a > 0 else 0
                            v.wait_ge(sxp.h, EX[(idx, m)])
                            inst = v.tensor_mul(pt[m % 4][:, j0:512],
                                                pt[m % 4][:, j0:512],
                                                m_diag[:, 512 * a + j0:512 * a + 512])
                            smk.inc(inst)
                    v.wait_ge(spv.h, PV[(idx, M - 1)])
                    if idx >= 1:
                        v.wait_ge(sG.h, OSTORE[idx - 1])
                    v.reciprocal(linv[:, :], ps[5][:, :])
                    inst = v.tensor_mul(outsb[:, :], ps[4][:, :], linv[:, :])
                    soe.inc(inst)

            @blk.gpsimd
            def _(g):
                for idx, (hb, sl) in enumerate(slabs):
                    g.wait_ge(soe.h, OE[idx])
                    inst = g.dma_start(
                        out=out_ap[128 * hb:128 * (hb + 1), 512 * sl:512 * (sl + 1)],
                        in_=outsb[:, :])
                    sG.inc(inst, 16)

    def kv1_body(i, hb):
        h = 2 * i + hb
        for r in range(NCO):
            nc.sync.dma_start(out=kslab[hb][:, 1024 * r:1024 * (r + 1)],
                              in_=agA_out[r, :, 128 * h:128 * (h + 1)],
                              transpose=True).then_inc(sSy.h, 16)
        for r in range(NCO):
            src3 = agA_out[r, :, D + 128 * h:D + 128 * (h + 1)].rearrange(
                "(mm p) d -> p mm d", p=128)
            dst3 = vslab[hb][:, 1024 * r:1024 * (r + 1)].rearrange(
                "p (mm d) -> p mm d", d=128)
            nc.gpsimd.dma_start(out=dst3, in_=src3).then_inc(sG.h, 16)

    q_flat = q_d.rearrange("a (b c) -> (a b) c", c=128)

    def q1_tr(i, hb, sl):
        row0 = 8192 * hb + 512 * sl
        return q_flat[row0:row0 + 512, :]

    emit_attention("b1", kv1_body, False, q1_tr, agB_in)

    if upto <= 4:
        return _cut()

    with nc.Block("p5ag") as blk:
        @blk.gpsimd
        def _(g):
            g.wait_ge(sG.h, sG.n)
            inst = g.collective_compute("AllGather", ALU.bypass, replica_groups=RG8,
                                        ins=[agB_in], outs=[agB_out])
            sCC.inc(inst)

    def emit_wo(tag, src_out, w_ap, b_ap, res_ap, dst_d, res_kind):
        nvi = 2 if res_kind == "ln_res" else 1
        with nc.Block(f"{tag}_wow") as blk:
            t_snap = sT.n

            @blk.gpsimd
            def _(g):
                g.wait_ge(sT.h, t_snap)
                for ch in range(16):
                    inst = g.dma_start(out=wstage[:, ch * 1024:(ch + 1) * 1024],
                                       in_=w_ap[ch * 128:(ch + 1) * 128, 0:1024])
                    sG.inc(inst, 16)
                for ch in range(16):
                    inst = g.dma_start(out=act_fm[:, ch * 1024:(ch + 1) * 1024],
                                       in_=w_ap[ch * 128:(ch + 1) * 128, 1024:2048])
                    sG.inc(inst, 16)
                if b_ap is not None:
                    inst = g.dma_start(out=bias_row[0:1, :], in_=b_ap[:])
                    sG.inc(inst, 16)

        if upto <= 4.5:
            return
        rg = nc.gpsimd.compute_val(nc.partition_id([ET.Pool]))
        t_base = sT.n
        L = {}
        gl = sG.n
        for s in range(NSUB):
            gl += 16
            L[s] = gl
        for i in nc.Switch(engines=[nc.gpsimd], index=[rg], n=NCO):
            nc.gpsimd.wait_ge(sCC.h, sCC.n)
            for s in range(NSUB):
                if s >= 2 and upto > 4.6:
                    nc.gpsimd.wait_ge(sT.h, t_base + 4 * (s - 1))
                col0 = TS * i + 128 * s
                src4 = src_out[:, :, col0:col0 + 128].rearrange(
                    "r (b p) i -> p r b i", p=128)
                dst4 = lhsb[s % 2].rearrange("p (r b i) -> p r b i", r=8, b=2)
                nc.gpsimd.dma_start(out=dst4, in_=src4).then_inc(sG.h, 16)
        sG.n += 16 * NSUB

        if upto <= 4.6:
            return
        with nc.Block(f"{tag}_wo") as blk:
            t0, a0, v0, g0 = sT.n, sA.n, sV.n, sG.n
            soe_snap = soe.n

            @blk.tensor
            def _(t):
                t.wait_ge(soe.h, soe_snap)
                t.wait_ge(sV.h, v0)
                for s in range(NSUB):
                    t.wait_ge(sG.h, L[s])
                    if upto > 4.65:
                        t.wait_ge(sV.h, v0 + s * nvi)
                    for ob in range(4):
                        wsrc = wstage if ob < 2 else act_fm
                        o2 = ob % 2
                        for ch in range(16):
                            inst = t.matmul(ps[ob][:, :], lhsb[s % 2][:, 128 * ch:128 * (ch + 1)],
                                            wsrc[:, ch * 1024 + o2 * 512:ch * 1024 + (o2 + 1) * 512],
                                            start=(ch == 0), stop=(b_ap is None and ch == 15))
                        if b_ap is not None:
                            inst = t.matmul(ps[ob][:, :], ones_b[0:1, 0:128],
                                            bias_row[0:1, 512 * ob:512 * (ob + 1)],
                                            start=False, stop=True)
                        sT.inc(inst)

            if upto <= 4.65:
                return
            sy0 = sSy.n

            @blk.sync
            def _(sy):
                for s in range(NSUB):
                    sy.wait_ge(sV.h, v0 + s * nvi)
                    inst = sy.dma_start(out=xt[:, :], in_=res_ap[128 * s:128 * (s + 1), :])
                    sSy.inc(inst, 16)
                    sy.wait_ge(sV.h, v0 + (s + 1) * nvi)
                    inst = sy.dma_start(out=dst_d[128 * s:(s + 1) * 128, :], in_=gt[:, :])
                    sSy.inc(inst, 16)

            if res_kind == "ln_res":
                @blk.scalar
                def _(sc):
                    for s in range(NSUB):
                        sc.wait_ge(sT.h, t0 + 4 * s + 4)
                        sc.wait_ge(sV.h, v0 + s * nvi)
                        for ob in range(4):
                            sq_src = ps[ob][:, :] if upto > 4.8 else xt[:, 512 * ob:512 * (ob + 1)]
                            inst = sc.activation(ft[:, 512 * ob:512 * (ob + 1)], sq_src,
                                                 AF.Square, bias=cons[:, 0:1], scale=1.0,
                                                 accum_out=stats[:, ob:ob + 1])
                        sA.inc(inst)
                        sc.wait_ge(sV.h, v0 + s * nvi + 1)
                        sc.activation(stats[:, 6:7], stats[:, 4:5], AF.Ln,
                                      bias=cons[:, 1:2], scale=1.0)
                        inst = sc.activation(stats[:, 5:6], stats[:, 6:7], AF.Exp,
                                             bias=cons[:, 0:1], scale=-0.5)
                        sA.inc(inst)

            @blk.vector
            def _(v):
                for s in range(NSUB):
                    v.wait_ge(sT.h, t0 + 4 * s + 4)
                    v.wait_ge(sSy.h, sy0 + 16 * (2 * s + 1))
                    if s >= 1:
                        v.wait_ge(sSy.h, sy0 + 16 * 2 * s)
                    if res_kind == "ln_res":
                        v.wait_ge(sA.h, a0 + s * 2 + 1)
                        if upto > 4.75:
                            for ob in range(4):
                                if ob == 0:
                                    v.reduce_sum(stats[:, 12:13], ps[ob][:, :], axis=AX)
                                else:
                                    v.reduce_sum(stats[:, 13:14], ps[ob][:, :], axis=AX)
                                    v.tensor_add(stats[:, 12:13], stats[:, 12:13],
                                                 stats[:, 13:14])
                            v.tensor_add(stats[:, 14:15], stats[:, 0:1], stats[:, 1:2])
                            v.tensor_add(stats[:, 15:16], stats[:, 2:3], stats[:, 3:4])
                            v.tensor_add(stats[:, 14:15], stats[:, 14:15], stats[:, 15:16])
                            v.tensor_scalar(stats[:, 2:3], stats[:, 12:13], 1.0 / D, None, ALU.mult)
                            v.tensor_scalar(stats[:, 3:4], stats[:, 14:15], 1.0 / D, None, ALU.mult)
                            v.tensor_mul(stats[:, 4:5], stats[:, 2:3], stats[:, 2:3])
                            inst = v.tensor_sub(stats[:, 4:5], stats[:, 3:4], stats[:, 4:5])
                        else:
                            inst = v.tensor_copy(stats[:, 4:5], cons[:, 1:2])
                        sV.inc(inst)
                        v.wait_ge(sA.h, a0 + s * 2 + 2)
                        for ob in range(4):
                            if upto > 4.77:
                                v.tensor_scalar(ft[:, 512 * ob:512 * (ob + 1)], ps[ob][:, :],
                                                stats[:, 2:3], stats[:, 5:6],
                                                ALU.subtract, ALU.mult)
                            else:
                                v.tensor_copy(ft[:, 512 * ob:512 * (ob + 1)], ps[ob][:, :])
                        inst = v.tensor_add(gt[:, :], xt[:, :], ft[:, :])
                        sV.inc(inst)
                    else:
                        for ob in range(4):
                            v.tensor_copy(ft[:, 512 * ob:512 * (ob + 1)], ps[ob][:, :])
                        inst = v.tensor_add(gt[:, :], xt[:, :], ft[:, :])
                        sV.inc(inst)

    if upto <= 4.3:
        return _cut()
    emit_wo("b1", agB_out, Wo, bo, x_sl, x2_d,
            "add" if upto <= 4.7 else "ln_res")
    if upto <= 4.9:
        return _cut()

    if upto <= 5:
        return _cut()

    with nc.Block("p6w") as blk:
        @blk.gpsimd
        def _(g):
            sG.inc(g.dma_start(out=wrep[:, :], in_=ln1w_r), 16)

    with nc.Block("p6") as blk:
        g0, a0, v0 = sG.n, sA.n, sV.n

        @blk.gpsimd
        def _(g):
            for s in range(NSUB):
                g.wait_ge(sV.h, v0 + 3 * s)
                inst = g.dma_start(out=xt[:, :], in_=x2_d[128 * s:(s + 1) * 128, :])
                sG.inc(inst, 16)
                g.wait_ge(sV.h, v0 + 3 * s + 2)
                inst = g.dma_start(out=li2_d[128 * s:(s + 1) * 128, :], in_=gt[:, :])
                sG.inc(inst, 16)
                g.wait_ge(sV.h, v0 + 3 * s + 3)
                inst = g.dma_start(out=h_d[128 * s:(s + 1) * 128, :], in_=stg[0][:, :])
                sG.inc(inst, 16)

        @blk.scalar
        def _(sc):
            for s in range(NSUB):
                sc.wait_ge(sG.h, g0 + 48 * s + 16)
                inst = sc.activation(ft[:, :], xt[:, :], AF.Square, bias=cons[:, 0:1],
                                     scale=1.0, accum_out=stats[:, 1:2])
                sA.inc(inst)
                sc.wait_ge(sV.h, v0 + 3 * s + 1)
                sc.activation(stats[:, 6:7], stats[:, 4:5], AF.Ln,
                              bias=cons[:, 1:2], scale=1.0)
                inst = sc.activation(stats[:, 5:6], stats[:, 6:7], AF.Exp,
                                     bias=cons[:, 0:1], scale=-0.5)
                sA.inc(inst)
                sc.wait_ge(sV.h, v0 + 3 * s + 2)
                sc.activation(stats[:, 8:9], stats[:, 7:8], AF.Ln,
                              bias=cons[:, 2:3], scale=1.0)
                inst = sc.activation(stats[:, 9:10], stats[:, 8:9], AF.Exp,
                                     bias=cons[:, 0:1], scale=-0.5)
                sA.inc(inst)

        @blk.vector
        def _(v):
            for s in range(NSUB):
                v.wait_ge(sG.h, g0 + 48 * s + 16)
                v.reduce_sum(stats[:, 0:1], xt[:, :], axis=AX)
                v.wait_ge(sA.h, a0 + 3 * s + 1)
                v.tensor_scalar(stats[:, 2:3], stats[:, 0:1], 1.0 / D, None, ALU.mult)
                v.tensor_scalar(stats[:, 3:4], stats[:, 1:2], 1.0 / D, None, ALU.mult)
                v.tensor_mul(stats[:, 4:5], stats[:, 2:3], stats[:, 2:3])
                inst = v.tensor_sub(stats[:, 4:5], stats[:, 3:4], stats[:, 4:5])
                sV.inc(inst)
                v.wait_ge(sA.h, a0 + 3 * s + 2)
                v.tensor_mul(stats[:, 7:8], stats[:, 5:6], stats[:, 5:6])
                v.tensor_mul(stats[:, 7:8], stats[:, 7:8], stats[:, 4:5])
                if s >= 1:
                    v.wait_ge(sG.h, g0 + 48 * s - 16)
                inst = v.tensor_scalar(gt[:, :], xt[:, :], stats[:, 2:3],
                                       stats[:, 5:6], ALU.subtract, ALU.mult)
                sV.inc(inst)
                v.wait_ge(sA.h, a0 + 3 * s + 3)
                if s >= 1:
                    v.wait_ge(sG.h, g0 + 48 * s)
                v.tensor_scalar(ft[:, :], gt[:, :], stats[:, 9:10], None, ALU.mult)
                inst = v.tensor_mul(stg[0][:, :], ft[:, :], wrep[:, :])
                sV.inc(inst)

    with nc.Block("p6t") as blk:
        @blk.sync
        def _(s):
            s.wait_ge(sG.h, sG.n)
            for ch in range(16):
                inst = s.dma_start(out=act_fm[:, ch * TS:(ch + 1) * TS],
                                   in_=h_d[:, ch * 128:(ch + 1) * 128], transpose=True)
                sSy.inc(inst, 16)

    if upto <= 6:
        return _cut()

    emit_stream_proj(fwq, None, mk_dst(qraw_d, 0), accum_ssq=True, name="p7q")

    def emit_kv512(w_ap, dst_col, do_knorm, name):
        with nc.Block(name) as blk:
            @blk.gpsimd
            def _(g):
                g.wait_ge(sT.h, sT.n)
                for ch in range(16):
                    inst = g.dma_start(out=wstage[:, ch * 512:(ch + 1) * 512],
                                       in_=w_ap[ch * 128:(ch + 1) * 128, :])
                    sG.inc(inst, 16)
                if do_knorm:
                    inst = g.dma_start(out=wrep[:, 0:512], in_=knw_r)
                    sG.inc(inst, 16)

            t0, a0, v0, g0 = sT.n, sA.n, sV.n, sG.n

            @blk.tensor
            def _(t):
                t.wait_ge(sG.h, g0)
                t.wait_ge(sSy.h, sSy.n)
                t.wait_ge(sV.h, v0)
                for s in range(NSUB):
                    if s >= 2:
                        t.wait_ge(sV.h, v0 + s - 1)
                    for ch in range(16):
                        inst = t.matmul(ps[s % 2][:, :],
                                        act_fm[:, ch * TS + s * 128:ch * TS + (s + 1) * 128],
                                        wstage[:, ch * 512:(ch + 1) * 512],
                                        start=(ch == 0), stop=(ch == 15))
                    sT.inc(inst)

            if do_knorm:
                @blk.scalar
                def _(sc):
                    for s in range(NSUB):
                        sc.wait_ge(sT.h, t0 + s + 1)
                        if s >= 1:
                            sc.wait_ge(sV.h, v0 + s)
                        sc.activation(qsq[:, :], ps[s % 2][:, :], AF.Square,
                                      bias=cons[:, 0:1], scale=1.0,
                                      accum_out=stats[:, 0:1])
                        sc.activation(stats[:, 6:7], stats[:, 0:1], AF.Ln,
                                      bias=cons[:, 2:3], scale=1.0 / 512.0)
                        inst = sc.activation(stats[:, 5:6], stats[:, 6:7], AF.Exp,
                                             bias=cons[:, 0:1], scale=-0.5)
                        sA.inc(inst)

                @blk.vector
                def _(v):
                    for s in range(NSUB):
                        v.wait_ge(sA.h, a0 + s + 1)
                        if s >= 1:
                            v.wait_ge(sG.h, g0 + 16 * s)
                        v.tensor_scalar(rt[:, :], ps[s % 2][:, :], stats[:, 5:6],
                                        None, ALU.mult)
                        v.tensor_mul(rt[:, :], rt[:, :], wrep[:, 0:512])
                        for hh in range(4):
                            v.tensor_copy(crep[:, 32 * hh:32 * hh + 32],
                                          csc[:, 32 * s:32 * s + 32])
                            v.tensor_copy(srep[:, 32 * hh:32 * hh + 32],
                                          ssc[:, 32 * s:32 * s + 32])
                        a_ap = rt[:, :].rearrange("p (h d) -> p h d", d=128)[:, :, 0:32]
                        b_ap2 = rt[:, :].rearrange("p (h d) -> p h d", d=128)[:, :, 32:64]
                        c_ap = crep[:, 0:128].rearrange("p (h d) -> p h d", d=32)
                        s_ap = srep[:, 0:128].rearrange("p (h d) -> p h d", d=32)
                        t1 = gt[:, 0:128].rearrange("p (h d) -> p h d", d=32)
                        t2 = gt[:, 128:256].rearrange("p (h d) -> p h d", d=32)
                        t3 = gt[:, 256:384].rearrange("p (h d) -> p h d", d=32)
                        t4 = gt[:, 384:512].rearrange("p (h d) -> p h d", d=32)
                        v.tensor_mul(t1, a_ap, c_ap)
                        v.tensor_mul(t2, b_ap2, s_ap)
                        v.tensor_mul(t3, b_ap2, c_ap)
                        v.tensor_mul(t4, a_ap, s_ap)
                        v.tensor_sub(a_ap, t1, t2)
                        v.tensor_add(b_ap2, t3, t4)
                        inst = v.tensor_copy(stg[0][:, 0:512], rt[:, :])
                        sV.inc(inst)
            else:
                @blk.vector
                def _(v):
                    for s in range(NSUB):
                        v.wait_ge(sT.h, t0 + s + 1)
                        if s >= 1:
                            v.wait_ge(sG.h, g0 + 16 * s)
                        inst = v.tensor_copy(stg[0][:, 0:512], ps[s % 2][:, :])
                        sV.inc(inst)

            @blk.gpsimd
            def _(g):
                for s in range(NSUB):
                    g.wait_ge(sV.h, v0 + s + 1)
                    inst = g.dma_start(
                        out=agC_in[128 * s:128 * (s + 1), dst_col:dst_col + 512],
                        in_=stg[0][:, 0:512])
                    sG.inc(inst, 16)

    emit_kv512(fwk, 2048, True, "p7k")
    emit_kv512(fwv, 2560, False, "p7v")

    with nc.Block("p7qp") as blk:
        @blk.gpsimd
        def _(g):
            sG.inc(g.dma_start(out=wrep[:, :], in_=qnw_r), 16)

        g0, a0, v0 = sG.n, sA.n, sV.n

        @blk.gpsimd
        def _(g):
            for s in range(NSUB):
                g.wait_ge(sV.h, v0 + 2 * s)
                inst = g.dma_start(out=stg[0][:, :], in_=qraw_d[128 * s:128 * (s + 1), :])
                sG.inc(inst, 16)
                g.wait_ge(sV.h, v0 + 2 * s + 2)
                inst = g.dma_start(out=agC_in[128 * s:128 * (s + 1), 0:2048],
                                   in_=stg[1][:, :])
                sG.inc(inst, 16)

        @blk.scalar
        def _(sc):
            for s in range(NSUB):
                sc.wait_ge(sG.h, g0 + 32 * s + 16)
                sc.activation(stats[:, 6:7], ssqa[:, s:s + 1], AF.Ln,
                              bias=cons[:, 2:3], scale=1.0 / D)
                inst = sc.activation(stats[:, 5:6], stats[:, 6:7], AF.Exp,
                                     bias=cons[:, 0:1], scale=-0.5)
                sA.inc(inst)

        @blk.vector
        def _(v):
            for s in range(NSUB):
                v.wait_ge(sG.h, g0 + 32 * s + 16)
                v.wait_ge(sA.h, a0 + s + 1)
                v.tensor_scalar(ft[:, :], stg[0][:, :], stats[:, 5:6], None, ALU.mult)
                inst = v.tensor_mul(ft[:, :], ft[:, :], wrep[:, :])
                sV.inc(inst)
                for hh in range(16):
                    v.tensor_copy(crep[:, 32 * hh:32 * hh + 32], csc[:, 32 * s:32 * s + 32])
                    v.tensor_copy(srep[:, 32 * hh:32 * hh + 32], ssc[:, 32 * s:32 * s + 32])
                a_ap = ft[:, :].rearrange("p (h d) -> p h d", d=128)[:, :, 0:32]
                b_ap2 = ft[:, :].rearrange("p (h d) -> p h d", d=128)[:, :, 32:64]
                c_ap = crep[:, :].rearrange("p (h d) -> p h d", d=32)
                s_ap = srep[:, :].rearrange("p (h d) -> p h d", d=32)
                t1 = gt[:, 0:512].rearrange("p (h d) -> p h d", d=32)
                t2 = gt[:, 512:1024].rearrange("p (h d) -> p h d", d=32)
                t3 = gt[:, 1024:1536].rearrange("p (h d) -> p h d", d=32)
                t4 = gt[:, 1536:2048].rearrange("p (h d) -> p h d", d=32)
                v.tensor_mul(t1, a_ap, c_ap)
                v.tensor_mul(t2, b_ap2, s_ap)
                v.tensor_mul(t3, b_ap2, c_ap)
                v.tensor_mul(t4, a_ap, s_ap)
                v.tensor_sub(a_ap, t1, t2)
                v.tensor_add(b_ap2, t3, t4)
                inst = v.tensor_copy(stg[1][:, :], ft[:, :])
                sV.inc(inst)

    with nc.Block("p8ag") as blk:
        @blk.gpsimd
        def _(g):
            g.wait_ge(sG.h, sG.n)
            inst = g.collective_compute("AllGather", ALU.bypass, replica_groups=RG8,
                                        ins=[agC_in], outs=[agC_out])
            sCC.inc(inst)

    if upto <= 7:
        return _cut()

    def kv2_body(i, hb):
        gidx = i // 2
        for r in range(NCO):
            nc.sync.dma_start(out=kslab[hb][:, 1024 * r:1024 * (r + 1)],
                              in_=agC_out[r, :, 2048 + 128 * gidx:2048 + 128 * (gidx + 1)],
                              transpose=True).then_inc(sSy.h, 16)
        for r in range(NCO):
            src3 = agC_out[r, :, 2560 + 128 * gidx:2560 + 128 * (gidx + 1)].rearrange(
                "(mm p) d -> p mm d", p=128)
            dst3 = vslab[hb][:, 1024 * r:1024 * (r + 1)].rearrange(
                "p (mm d) -> p mm d", d=128)
            nc.gpsimd.dma_start(out=dst3, in_=src3).then_inc(sG.h, 16)

    def q2_tr(i, hb, sl):
        h = 2 * i + hb
        r = sl // 2
        row0 = 512 * (sl % 2)
        return agC_out[r, row0:row0 + 512, 128 * h:128 * (h + 1)]

    emit_attention("b2", kv2_body, True, q2_tr, agD_in)

    if upto <= 8:
        return _cut()

    with nc.Block("p9ag") as blk:
        @blk.gpsimd
        def _(g):
            g.wait_ge(sG.h, sG.n)
            inst = g.collective_compute("AllGather", ALU.bypass, replica_groups=RG8,
                                        ins=[agD_in], outs=[agD_out])
            sCC.inc(inst)

    emit_wo("b2", agD_out, fwo, None, li2_d, x1_d, "add")

    if upto <= 9:
        return _cut()

    with nc.Block("p9w") as blk:
        @blk.gpsimd
        def _(g):
            sG.inc(g.dma_start(out=wrep[:, :], in_=ln2w_r), 16)
            src3 = rw.rearrange("(c p) e -> p c e", p=128)
            dst3 = rwb[:, :].rearrange("p (c e) -> p c e", e=8)
            sG.inc(g.dma_start(out=dst3, in_=src3), 16)

    with nc.Block("p9h") as blk:
        g0, a0, v0 = sG.n, sA.n, sV.n

        @blk.gpsimd
        def _(g):
            for s in range(NSUB):
                g.wait_ge(sV.h, v0 + s)
                inst = g.dma_start(out=xt[:, :], in_=x1_d[128 * s:(s + 1) * 128, :])
                sG.inc(inst, 16)
                g.wait_ge(sV.h, v0 + s + 1)
                inst = g.dma_start(out=h2_d[128 * s:(s + 1) * 128, :], in_=stg[0][:, :])
                sG.inc(inst, 16)

        @blk.scalar
        def _(sc):
            for s in range(NSUB):
                sc.wait_ge(sG.h, g0 + 32 * s + 16)
                sc.activation(ft[:, :], xt[:, :], AF.Square, bias=cons[:, 0:1],
                              scale=1.0, accum_out=stats[:, 0:1])
                sc.activation(stats[:, 6:7], stats[:, 0:1], AF.Ln,
                              bias=cons[:, 2:3], scale=1.0 / D)
                inst = sc.activation(stats[:, 5:6], stats[:, 6:7], AF.Exp,
                                     bias=cons[:, 0:1], scale=-0.5)
                sA.inc(inst)

        @blk.vector
        def _(v):
            for s in range(NSUB):
                v.wait_ge(sA.h, a0 + s + 1)
                if s >= 1:
                    v.wait_ge(sG.h, g0 + 32 * s)
                v.tensor_scalar(gt[:, :], xt[:, :], stats[:, 5:6], None, ALU.mult)
                inst = v.tensor_mul(stg[0][:, :], gt[:, :], wrep[:, :])
                sV.inc(inst)

    with nc.Block("p9t") as blk:
        @blk.sync
        def _(s):
            s.wait_ge(sG.h, sG.n)
            for ch in range(16):
                inst = s.dma_start(out=act_fm[:, ch * TS:(ch + 1) * TS],
                                   in_=h2_d[:, ch * 128:(ch + 1) * 128], transpose=True)
                sSy.inc(inst, 16)

    with nc.Block("p9r") as blk:
        t0, a0, v0 = sT.n, sA.n, sV.n

        @blk.tensor
        def _(t):
            t.wait_ge(sSy.h, sSy.n)
            t.wait_ge(sG.h, sG.n)
            t.wait_ge(sV.h, v0)
            for s in range(NSUB):
                if s >= 2:
                    t.wait_ge(sA.h, a0 + s - 1)
                for ch in range(16):
                    inst = t.matmul(ps[s % 2][:, 0:8],
                                    act_fm[:, ch * TS + s * 128:ch * TS + (s + 1) * 128],
                                    rwb[:, 8 * ch:8 * (ch + 1)],
                                    start=(ch == 0), stop=(ch == 15))
                sT.inc(inst)
                t.wait_ge(sV.h, v0 + 2 * s + 1)
                inst = t.transpose(ps[2][0:8, 0:128], rout[:, 0:8], idt[:, :])
                sT.inc(inst)

        @blk.scalar
        def _(sc):
            for s in range(NSUB):
                sc.wait_ge(sT.h, t0 + 2 * s + 1)
                inst = sc.activation(rout[:, 8:16], ps[s % 2][:, 0:8], AF.Exp,
                                     bias=cons[:, 0:1], scale=-1.0)
                sA.inc(inst)

        @blk.vector
        def _(v):
            for s in range(NSUB):
                v.wait_ge(sA.h, a0 + s + 1)
                v.tensor_scalar(rout[:, 8:16], rout[:, 8:16], 1.0, None, ALU.add)
                v.reciprocal(rout[:, 16:24], rout[:, 8:16])
                v.tensor_add(rout[:, 24:32], rout[:, 16:24], rb_rep[:, :])
                v.reduce_max(stats[:, 0:1], rout[:, 24:32], axis=AX)
                v.tensor_scalar(rout[:, 32:40], rout[:, 24:32], stats[:, 0:1],
                                None, ALU.is_equal)
                v.scalar_tensor_tensor(rout[:, 40:48], rout[:, 32:40], -1e30,
                                       rout[:, 24:32], ALU.mult, ALU.add)
                v.reduce_max(stats[:, 1:2], rout[:, 40:48], axis=AX)
                v.tensor_scalar(rout[:, 48:56], rout[:, 40:48], stats[:, 1:2],
                                None, ALU.is_equal)
                v.tensor_add(rout[:, 32:40], rout[:, 32:40], rout[:, 48:56])
                v.tensor_mul(rout[:, 56:64], rout[:, 16:24], rout[:, 32:40])
                v.reduce_sum(stats[:, 2:3], rout[:, 56:64], axis=AX)
                v.reciprocal(stats[:, 3:4], stats[:, 2:3])
                inst = v.tensor_scalar(rout[:, 0:8], rout[:, 56:64], stats[:, 3:4],
                                       None, ALU.mult)
                sV.inc(inst)
                v.wait_ge(sT.h, t0 + 2 * s + 2)
                dstc = ctr8[0] if s < 4 else ctr8[1]
                inst = v.tensor_copy(dstc[:, 128 * (s % 4):128 * (s % 4 + 1)],
                                     ps[2][0:8, 0:128])
                sV.inc(inst)

        @blk.gpsimd
        def _(g):
            g.wait_ge(sV.h, sV.n)
            sG.inc(g.dma_start(out=ctr_d[:, 0:512], in_=ctr8[0]), 16)
            sG.inc(g.dma_start(out=ctr_d[:, 512:1024], in_=ctr8[1]), 16)

    if upto <= 10:
        return _cut()

    with nc.Block("p10") as blk:
        t0, a0, v0, g0 = sT.n, sA.n, sV.n, sG.n
        soe_snap, spv_snap, sqk_snap = soe.n, spv.n, sqk.n
        GW, TW, VW, AW = {}, {}, {}, {}
        g_cnt, t_cnt, v_cnt, a_cnt = g0, t0, v0, a0
        for e in range(8):
            g_cnt += 32
            GW[(e, 'ctr')] = g_cnt
            t_cnt += 2
            TW[(e, 'cbc')] = t_cnt
            v_cnt += 1
            VW[(e, 'cbc')] = v_cnt
            for ih in range(2):
                g_cnt += 16 * 32
                GW[(e, ih)] = g_cnt
                for pr in range(8):
                    t_cnt += 2
                    TW[(e, ih, pr)] = t_cnt
                    a_cnt += 1
                    AW[(e, ih, pr)] = a_cnt
                    v_cnt += 1
                    VW[(e, ih, pr)] = v_cnt
            g_cnt += 16 * 8
            GW[(e, 'wd')] = g_cnt
            for dt_i in range(32):
                t_cnt += 1
                TW[(e, 'd', dt_i)] = t_cnt
                v_cnt += 1
                VW[(e, 'd', dt_i)] = v_cnt

        @blk.gpsimd
        def _(g):
            g.wait_ge(soe.h, soe_snap)
            g.wait_ge(spv.h, spv_snap)
            g.wait_ge(sqk.h, sqk_snap)
            g.wait_ge(sT.h, t0)
            g.wait_ge(sV.h, v0)
            g.wait_ge(sA.h, a0)
            for e in range(8):
                if e >= 1:
                    g.wait_ge(sT.h, TW[(e - 1, 'd', 31)])
                inst = g.dma_start(out=lrow[0], in_=ctr_d[e, 0:512])
                sG.inc(inst, 16)
                inst = g.dma_start(out=lrow[1], in_=ctr_d[e, 512:1024])
                sG.inc(inst, 16)
                for ih in range(2):
                    if e >= 1 or ih >= 1:
                        pe, pih = (e, 0) if ih == 1 else (e - 1, 1)
                        g.wait_ge(sT.h, TW[(pe, pih, 7)])
                    for ch in range(16):
                        inst = g.dma_start(out=wstage[:, ch * 512:(ch + 1) * 512],
                                           in_=wg[e, ch * 128:(ch + 1) * 128,
                                                  512 * ih:512 * (ih + 1)])
                        sG.inc(inst, 16)
                    for ch in range(16):
                        inst = g.dma_start(out=wstage[:, 8192 + ch * 512:8192 + (ch + 1) * 512],
                                           in_=wu[e, ch * 128:(ch + 1) * 128,
                                                  512 * ih:512 * (ih + 1)])
                        sG.inc(inst, 16)
                for chi in range(8):
                    dst = wd_lo if chi < 4 else wd_hi
                    co = (chi % 4) * 2048
                    inst = g.dma_start(out=dst[:, co:co + 2048],
                                       in_=wd[e, chi * 128:(chi + 1) * 128, :])
                    sG.inc(inst, 16)

        @blk.tensor
        def _(t):
            t.wait_ge(sV.h, v0)
            for e in range(8):
                t.wait_ge(sG.h, GW[(e, 'ctr')])
                if e >= 1:
                    t.wait_ge(sV.h, VW[(e - 1, 'd', 31)])
                t.matmul(ps[4][:, :], onesf[0:1, 0:128], lrow[0],
                         start=True, stop=True)
                inst = t.matmul(ps[5][:, :], onesf[0:1, 0:128], lrow[1],
                                start=True, stop=True)
                sT.inc(inst, 2)
                for ih in range(2):
                    t.wait_ge(sG.h, GW[(e, ih)])
                    for pr in range(8):
                        ic, th = pr // 2, pr % 2
                        bg, bu = (0, 1) if pr % 2 == 0 else (2, 3)
                        if pr >= 2:
                            t.wait_ge(sV.h, VW[(e, ih, pr - 2)])
                        elif ih == 1:
                            t.wait_ge(sV.h, VW[(e, 0, 6 + pr)])
                        elif e >= 1:
                            t.wait_ge(sV.h, VW[(e - 1, 1, 6 + pr)])
                        for ch in range(16):
                            inst = t.matmul(ps[bg][:, :],
                                            wstage[:, ch * 512 + ic * 128:ch * 512 + (ic + 1) * 128],
                                            act_fm[:, ch * TS + th * 512:ch * TS + (th + 1) * 512],
                                            start=(ch == 0), stop=(ch == 15))
                        sT.inc(inst)
                        for ch in range(16):
                            inst = t.matmul(ps[bu][:, :],
                                            wstage[:, 8192 + ch * 512 + ic * 128:8192 + ch * 512 + (ic + 1) * 128],
                                            act_fm[:, ch * TS + th * 512:ch * TS + (th + 1) * 512],
                                            start=(ch == 0), stop=(ch == 15))
                        sT.inc(inst)
                t.wait_ge(sG.h, GW[(e, 'wd')])
                t.wait_ge(sV.h, VW[(e, 1, 7)])
                dt_i = 0
                for s in range(NSUB):
                    for ob in range(4):
                        bank = ps[dt_i % 2]
                        if dt_i >= 2:
                            t.wait_ge(sV.h, VW[(e, 'd', dt_i - 2)])
                        for chi in range(8):
                            wsrc = wd_lo if chi < 4 else wd_hi
                            co = (chi % 4) * 2048
                            inst = t.matmul(bank[:, :],
                                            actb[:, chi * 1024 + s * 128:chi * 1024 + (s + 1) * 128],
                                            wsrc[:, co + ob * 512:co + (ob + 1) * 512],
                                            start=(chi == 0), stop=(chi == 7))
                        sT.inc(inst)
                        dt_i += 1

        @blk.scalar
        def _(sc):
            for e in range(8):
                for ih in range(2):
                    for pr in range(8):
                        bg = 0 if pr % 2 == 0 else 2
                        sc.wait_ge(sT.h, TW[(e, ih, pr)] - 1)
                        if pr >= 1:
                            sc.wait_ge(sV.h, VW[(e, ih, pr - 1)])
                        elif ih == 1:
                            sc.wait_ge(sV.h, VW[(e, 0, 7)])
                        elif e >= 1:
                            sc.wait_ge(sV.h, VW[(e - 1, 1, 7)])
                        inst = sc.activation(sgb[:, :], ps[bg][:, :], AF.Silu,
                                             bias=cons[:, 0:1], scale=1.0)
                        sA.inc(inst)

        @blk.vector
        def _(v):
            for e in range(8):
                v.wait_ge(sT.h, TW[(e, 'cbc')])
                v.tensor_copy(cbc[:, 0:512], ps[4][:, :])
                inst = v.tensor_copy(cbc[:, 512:1024], ps[5][:, :])
                sV.inc(inst)
                for ih in range(2):
                    for pr in range(8):
                        ic, th = pr // 2, pr % 2
                        bu = 1 if pr % 2 == 0 else 3
                        gI = ih * 4 + ic
                        v.wait_ge(sT.h, TW[(e, ih, pr)])
                        v.wait_ge(sA.h, AW[(e, ih, pr)])
                        v.tensor_mul(rt[:, :], ps[bu][:, :], cbc[:, 512 * th:512 * (th + 1)])
                        inst = v.tensor_mul(
                            actb[:, gI * 1024 + th * 512:gI * 1024 + (th + 1) * 512],
                            sgb[:, :], rt[:, :])
                        sV.inc(inst)
                dt_i = 0
                for s in range(NSUB):
                    for ob in range(4):
                        v.wait_ge(sT.h, TW[(e, 'd', dt_i)])
                        mslice = macc[:, s * 2048 + ob * 512:s * 2048 + (ob + 1) * 512]
                        if e == 0:
                            inst = v.tensor_copy(mslice, ps[dt_i % 2][:, :])
                        else:
                            inst = v.tensor_add(mslice, mslice, ps[dt_i % 2][:, :])
                        sV.inc(inst)
                        dt_i += 1

    if upto <= 11:
        return _cut()

    with nc.Block("p11") as blk:
        g0, a0, v0, o0 = sG.n, sA.n, sV.n, sO.n

        @blk.gpsimd
        def _(g):
            for s in range(NSUB):
                g.wait_ge(sV.h, v0 + 3 * s)
                inst = g.dma_start(out=xt[:, :], in_=x1_d[128 * s:(s + 1) * 128, :])
                sG.inc(inst, 16)
                inst = g.dma_start(out=gt[:, :], in_=x2_d[128 * s:(s + 1) * 128, :])
                sG.inc(inst, 16)
                g.wait_ge(sV.h, v0 + 3 * s + 3)
                inst = g.dma_start(out=y_out[128 * s:(s + 1) * 128, :], in_=ft[:, :])
                sO.inc(inst, 16)

        @blk.scalar
        def _(sc):
            for s in range(NSUB):
                sc.wait_ge(sV.h, v0 + 3 * s + 1)
                inst = sc.activation(xt[:, :], ft[:, :], AF.Square, bias=cons[:, 0:1],
                                     scale=1.0, accum_out=stats[:, 1:2])
                sA.inc(inst)
                sc.wait_ge(sV.h, v0 + 3 * s + 2)
                sc.activation(stats[:, 6:7], stats[:, 4:5], AF.Ln,
                              bias=cons[:, 1:2], scale=1.0)
                inst = sc.activation(stats[:, 5:6], stats[:, 6:7], AF.Exp,
                                     bias=cons[:, 0:1], scale=-0.5)
                sA.inc(inst)

        @blk.vector
        def _(v):
            for s in range(NSUB):
                v.wait_ge(sG.h, g0 + 32 * (s + 1))
                if s >= 1:
                    v.wait_ge(sO.h, o0 + 16 * s)
                inst = v.tensor_add(ft[:, :], xt[:, :], macc[:, 2048 * s:2048 * (s + 1)])
                sV.inc(inst)
                v.reduce_sum(stats[:, 0:1], ft[:, :], axis=AX)
                v.wait_ge(sA.h, a0 + 2 * s + 1)
                v.tensor_scalar(stats[:, 2:3], stats[:, 0:1], 1.0 / D, None, ALU.mult)
                v.tensor_scalar(stats[:, 3:4], stats[:, 1:2], 1.0 / D, None, ALU.mult)
                v.tensor_mul(stats[:, 4:5], stats[:, 2:3], stats[:, 2:3])
                inst = v.tensor_sub(stats[:, 4:5], stats[:, 3:4], stats[:, 4:5])
                sV.inc(inst)
                v.wait_ge(sA.h, a0 + 2 * s + 2)
                v.tensor_scalar(xt[:, :], ft[:, :], stats[:, 2:3], stats[:, 5:6],
                                ALU.subtract, ALU.mult)
                inst = v.tensor_add(ft[:, :], gt[:, :], xt[:, :])
                sV.inc(inst)

    if DEBUG_TAPS:
        with nc.Block("taps") as blk:
            @blk.gpsimd
            def _(g):
                g.wait_ge(sO.h, sO.n)
                for nm, src in [("tap_lnx", lnx_d), ("tap_qd", q_d), ("tap_agA", agA_in),
                                ("tap_agB", agB_in), ("tap_x2", x2_d), ("tap_h", h_d),
                                ("tap_agC", agC_in), ("tap_agD", agD_in),
                                ("tap_x1", x1_d), ("tap_h2", h2_d),
                                ("tap_ctr", ctr_d), ("tap_li2", li2_d)]:
                    inst = g.dma_start(out=taps[nm], in_=src)
                    sO.inc(inst, 16)

    with nc.Block("fin") as blk:
        @blk.gpsimd
        def _(g):
            g.wait_ge(sO.h, sO.n)
    return nc


_NC_CACHE = [None]


def kernel(**inputs):
    i = {k: np.ascontiguousarray(np.asarray(v, dtype=np.float32))
         for k, v in inputs.items()}
    x = i["L0_attn_x"].reshape(S_FULL, D)
    if _NC_CACHE[0] is None:
        _NC_CACHE[0] = build_program()
    nc = _NC_CACHE[0]

    shared = {
        "Wq": i["L0_attn_Wq"], "Wk": i["L0_attn_Wk"], "Wv": i["L0_attn_Wv"],
        "Wo": i["L0_attn_Wo"], "bq": i["L0_attn_bq"], "bk": i["L0_attn_bk"],
        "bv": i["L0_attn_bv"], "bo": i["L0_attn_bo"],
        "fwq": i["L0_ffn_wq"], "fwk": i["L0_ffn_wk"], "fwv": i["L0_ffn_wv"],
        "fwo": i["L0_ffn_wo"], "rw": i["L0_ffn_router_w"],
        "wg": i["L0_ffn_wg"], "wu": i["L0_ffn_wu"], "wd": i["L0_ffn_wd"],
        "ln1w_r": np.ascontiguousarray(np.tile(i["L0_ffn_ln1_w"][None, :], (128, 1))),
        "ln2w_r": np.ascontiguousarray(np.tile(i["L0_ffn_ln2_w"][None, :], (128, 1))),
        "qnw_r": np.ascontiguousarray(np.tile(i["L0_ffn_qnorm_w"][None, :], (128, 1))),
        "knw_r": np.ascontiguousarray(np.tile(i["L0_ffn_knorm_w"][None, :], (128, 1))),
        "rbias_r": np.ascontiguousarray(np.tile(i["L0_ffn_router_bias"][None, :], (128, 1))),
        "iidx_in": np.ascontiguousarray(
            np.tile(np.arange(32, dtype=np.float32)[None, :], (128, 1))),
    }
    in_maps = []
    for c in range(NCO):
        m = dict(shared)
        m["x_sl"] = np.ascontiguousarray(x[c * TS:(c + 1) * TS])
        m["pos_in"] = np.arange(c * TS, (c + 1) * TS, dtype=np.float32)
        in_maps.append(m)

    res = run_bass_kernel_spmd(nc, in_maps, core_ids=list(range(NCO)))
    y = np.concatenate([res.results[c]["y"] for c in range(NCO)], axis=0)
    return y.reshape(1, S_FULL, D)


# version 11
# speedup vs baseline: 1.1014x; 1.1014x over previous
import math
import numpy as np

import concourse.bass as bass
import concourse.mybir as mybir
from concourse.bass_utils import run_bass_kernel_spmd

F32 = mybir.dt.float32
BF16 = mybir.dt.bfloat16
I32 = mybir.dt.int32
AF = mybir.ActivationFunctionType
ALU = mybir.AluOpType
AX = mybir.AxisListType.X
ET = mybir.EngineType

NCO = 8
S_FULL = 8192
TS = 1024
D = 2048
NSUB = 8
SA = 1.0 / math.sqrt(128.0)
TWO_PI = 2.0 * math.pi
EPS_LN = 1e-5
EPS_RMS = 1e-6
DEBUG_TAPS = False
RG8 = [list(range(NCO))]


_TRAILER = {}


class Sem:

    def __init__(self, nc, name):
        self.h = nc.alloc_semaphore(name)
        self.n = 0

    def inc(self, inst, k=1):
        eng = str(inst.ins.engine) if hasattr(inst, "ins") else ""
        tr = _TRAILER.get(eng)
        if tr is not None:
            tr(self.h, k)
        else:
            inst.then_inc(self.h, k)
        self.n += k




_GUARD_OPS = {
    "vector": ["memset", "memzero", "tensor_copy", "tensor_tensor", "tensor_add",
               "tensor_sub", "tensor_mul", "tensor_max", "tensor_scalar",
               "tensor_single_scalar", "reduce_max", "reduce_sum", "reciprocal",
               "scalar_tensor_tensor"],
    "scalar": ["activation", "copy", "mul", "add", "memzero"],
}


def _ap_region(ap):
    try:
        name = ap.tensor.name
        off = int(ap.offset)
        span = 1
        for step, count in ap.ap:
            span += (int(count) - 1) * abs(int(step))
        return (name, off, off + span)
    except Exception:
        return ("?", 0, 1 << 60)


def _install_engine_guards(nc):
    import collections

    for engname in ("vector", "scalar"):
        eng = getattr(nc, engname)
        sem = nc.alloc_semaphore(f"grd_{engname}")
        state = {"cnt": 0, "recent": collections.deque(maxlen=8)}

        def mk(opname, eng=eng, sem=sem, state=state):
            orig = getattr(type(eng), opname)

            def wrapped(*args, **kwargs):
                aps = [a for a in list(args) + list(kwargs.values())
                       if isinstance(a, bass.AP)]
                if not aps:
                    inst = orig(eng, *args, **kwargs)
                    state["cnt"] += 1
                    inst.then_inc(sem, 1)
                    state["recent"].append((state["cnt"], [], []))
                    return inst
                regs = [_ap_region(a) for a in aps]
                writes = [regs[0]]
                acc = kwargs.get("accum_out")
                if isinstance(acc, bass.AP):
                    writes.append(_ap_region(acc))
                reads = regs[1:]

                def ovl(a, b):
                    return a[0] == b[0] and a[1] < b[2] and b[1] < a[2]

                need = 0
                for idx, w_prev, r_prev in state["recent"]:
                    hit = False
                    for r in regs:
                        if any(ovl(r, w) for w in w_prev):
                            hit = True
                    for w in writes:
                        if any(ovl(w, r) for r in r_prev):
                            hit = True
                    if hit:
                        need = max(need, idx)
                if need:
                    eng.wait_ge(sem, need)
                inst = orig(eng, *args, **kwargs)
                state["cnt"] += 1
                inst.then_inc(sem, 1)
                state["recent"].append((state["cnt"], writes, reads))
                return inst

            return wrapped

        for opname in _GUARD_OPS[engname]:
            setattr(eng, opname, mk(opname))

def build_program(race_detect=False, upto=99):
    nc = bass.Bass("TRN2", target_bir_lowering=False,
                   detect_race_conditions=race_detect)
    _install_engine_guards(nc)

    def di(name, shape):
        return nc.dram_tensor(name, shape, F32, kind="ExternalInput").ap()

    x_sl = di("x_sl", [TS, D])
    Wq, Wk, Wv, Wo = di("Wq", [D, D]), di("Wk", [D, D]), di("Wv", [D, D]), di("Wo", [D, D])
    bq, bk, bv, bo = di("bq", [D]), di("bk", [D]), di("bv", [D]), di("bo", [D])
    fwq, fwk, fwv, fwo = di("fwq", [D, D]), di("fwk", [D, 512]), di("fwv", [D, 512]), di("fwo", [D, D])
    rw = di("rw", [D, 8])
    wg, wu, wd = di("wg", [8, D, 1024]), di("wu", [8, D, 1024]), di("wd", [8, 1024, D])
    ln1w_r = di("ln1w_r", [128, D])
    ln2w_r = di("ln2w_r", [128, D])
    qnw_r = di("qnw_r", [128, D])
    knw_r = di("knw_r", [128, 512])
    rbias_r = di("rbias_r", [128, 8])
    pos_in = di("pos_in", [TS])
    iidx_in = di("iidx_in", [128, 32])

    y_out = nc.dram_tensor("y", [TS, D], F32, kind="ExternalOutput").ap()

    def dint(name, shape, dt=BF16):
        return nc.dram_tensor(name, shape, dt, kind="Internal").ap()

    agA_in = dint("agA_in", [TS, 2 * D])
    agA_out = nc.dram_tensor("agA_out", [NCO, TS, 2 * D], BF16, kind="Internal", addr_space="Shared").ap()
    agB_in = dint("agB_in", [256, S_FULL])
    agB_out = nc.dram_tensor("agB_out", [NCO, 256, S_FULL], BF16, kind="Internal", addr_space="Shared").ap()
    agC_in = dint("agC_in", [TS, 3072])
    agC_out = nc.dram_tensor("agC_out", [NCO, TS, 3072], BF16, kind="Internal", addr_space="Shared").ap()
    agD_in = dint("agD_in", [256, S_FULL])
    agD_out = nc.dram_tensor("agD_out", [NCO, 256, S_FULL], BF16, kind="Internal", addr_space="Shared").ap()
    lnx_d = dint("lnx_d", [TS, D])
    q_d = dint("q_d", [TS, D])
    qraw_d = dint("qraw_d", [TS, D])
    h_d = dint("h_d", [TS, D])
    h2_d = dint("h2_d", [TS, D])
    x2_d = dint("x2_d", [TS, D], F32)
    li2_d = dint("li2_d", [TS, D], F32)
    x1_d = dint("x1_d", [TS, D], F32)
    ctr_d = dint("ctr_d", [8, TS], F32)

    taps = {}
    if DEBUG_TAPS:
        for nm, shape, dt in [("tap_lnx", [TS, D], BF16), ("tap_qd", [TS, D], BF16),
                              ("tap_agA", [TS, 2 * D], BF16), ("tap_agB", [256, S_FULL], BF16),
                              ("tap_x2", [TS, D], F32), ("tap_h", [TS, D], BF16),
                              ("tap_agC", [TS, 3072], BF16), ("tap_agD", [256, S_FULL], BF16),
                              ("tap_x1", [TS, D], F32), ("tap_h2", [TS, D], BF16),
                              ("tap_ctr", [8, TS], F32), ("tap_li2", [TS, D], F32)]:
            taps[nm] = nc.dram_tensor(nm, shape, dt, kind="ExternalOutput").ap()

    def sbt(name, cols, dt=BF16, parts=128):
        return nc.alloc_sbuf_tensor(name, [parts, cols], dt)

    m_diag = sbt("m_diag", 2048)
    ones_b = sbt("ones_b", 128)
    onesf = sbt("onesf", 128, F32, parts=1)
    act_fm = sbt("act_fm", 16 * TS)
    wstage = sbt("wstage", 16 * TS)
    kvz = sbt("kvz", 32768)
    sha = sbt("sha", 8192)
    bias_row = sbt("bias_row", D, BF16, parts=1)
    idt = sbt("idt", 128, F32)
    rwb = sbt("rwb", 128)

    xt = sbt("xt", D, F32)
    ft = sbt("ft", D, F32)
    gt = sbt("gt", D, F32)
    rt = sbt("rt", 512, F32)
    wrep = sbt("wrep", D, BF16)
    rb_rep = sbt("rb_rep", 8, F32)
    linv = sbt("linv", 512, F32)
    csc = sbt("csc", 256, F32)
    ssc = sbt("ssc", 256, F32)
    crep = sbt("crep", 512, F32)
    srep = sbt("srep", 512, F32)
    stats = sbt("stats", 16, F32)
    ssqa = sbt("ssqa", NSUB, F32)
    cons = sbt("cons", 8, F32)
    rout = sbt("rout", 64, F32)
    poscol = sbt("poscol", NSUB, F32)
    iidx = sbt("iidx", 32, F32)
    cbc = sbt("cbc", TS, F32)
    sgb = sbt("sgb", 512, BF16)
    ki = sbt("ki", 32, I32)
    lrow = [crep[0:1, 0:512], srep[0:1, 0:512]]
    ctr8 = [crep[0:8, :], srep[0:8, :]]

    qslab = [sha[:, 0:512], sha[:, 512:1024]]
    pt = [sha[:, 1024:1536], sha[:, 1536:2048], sha[:, 3072:3584], sha[:, 3584:4096]]
    qsq = sha[:, 2048:2560]
    outsb = sha[:, 2560:3072]
    stg = [sha[:, 4096:6144], sha[:, 6144:8192]]
    lhsb = [sha[:, 0:2048], sha[:, 2048:4096]]
    wd_lo = sha[:, 0:8192]
    kslab = [kvz[:, 0:8192], kvz[:, 16384:24576]]
    vslab = [kvz[:, 8192:16384], kvz[:, 24576:32768]]
    macc = kvz[:, 0:16384]
    actb = kvz[:, 16384:24576]
    wd_hi = kvz[:, 24576:32768]

    ps = [nc.alloc_psum_tensor(f"ps{i}", [128, 512], F32) for i in range(8)]

    _raw_vcopy = type(nc.vector).tensor_copy
    _raw_scopy = type(nc.scalar).copy
    _TRAILER.clear()
    _TRAILER["EngineType.DVE"] = (
        lambda h, k: _raw_vcopy(nc.vector, cons[:, 7:8], cons[:, 0:1]).then_inc(h, k))
    _TRAILER["EngineType.Activation"] = (
        lambda h, k: _raw_scopy(nc.scalar, cons[:, 7:8], cons[:, 0:1]).then_inc(h, k))

    sG = Sem(nc, "sG")
    sSy = Sem(nc, "sSy")
    sT = Sem(nc, "sT")
    sA = Sem(nc, "sA")
    sV = Sem(nc, "sV")
    sCC = Sem(nc, "sCC")
    sO = Sem(nc, "sO")
    sqk = Sem(nc, "sqk")
    sxp = Sem(nc, "sxp")
    smk = Sem(nc, "smk")
    spv = Sem(nc, "spv")
    sbsq = Sem(nc, "sbsq")
    sbcs = Sem(nc, "sbcs")
    sm2 = Sem(nc, "sm2")
    sbe = Sem(nc, "sbe")
    snb = Sem(nc, "snb")
    soe = Sem(nc, "soe")
    sGRD = Sem(nc, "sGRD")


    def _cut():
        with nc.Block("fin") as blk:
            @blk.gpsimd
            def _(g):
                g.wait_ge(sO.h, sO.n)
        return nc

    with nc.Block("p0") as blk:
        @blk.gpsimd
        def _(g):
            g.memset(ones_b[:, :], 1.0)
            g.memset(onesf[:, :], 1.0)
            g.memset(cons[:, 0:1], 0.0)
            g.memset(cons[:, 1:2], EPS_LN)
            g.memset(cons[:, 2:3], EPS_RMS)
            g.memset(cons[:, 3:4], math.log(SA))
            inst = g.memset(m_diag[:, :], 1.0)
            sG.inc(inst, 16) if False else inst.then_inc(sGRD.h, 1)
            sGRD.n += 1
            g.wait_ge(sGRD.h, sGRD.n)
            for a in range(4):
                g.affine_select(out=m_diag[:, 512 * a:512 * (a + 1)],
                                in_=m_diag[:, 512 * a:512 * (a + 1)],
                                compare_op=ALU.is_ge, fill=0.0, base=-128 * a,
                                pattern=[[1, 512]], channel_multiplier=-1)
            inst = g.memset(idt[:, :], 0.0)
            inst.then_inc(sGRD.h, 1)
            sGRD.n += 1
            g.wait_ge(sGRD.h, sGRD.n)
            inst = g.affine_select(out=idt[:, :], in_=idt[:, :],
                                   compare_op=ALU.not_equal, fill=1.0, base=0,
                                   pattern=[[-1, 128]], channel_multiplier=1)
            sV.inc(inst)
            for s in range(NSUB):
                inst = g.dma_start(out=poscol[:, s:s + 1],
                                   in_=pos_in[s * 128:(s + 1) * 128])
                sG.inc(inst, 16)
            inst = g.dma_start(out=iidx[:, :], in_=iidx_in)
            sG.inc(inst, 16)
            inst = g.dma_start(out=rb_rep[:, :], in_=rbias_r)
            sG.inc(inst, 16)

        @blk.scalar
        def _(sc):
            sc.wait_ge(sG.h, sG.n)
            sc.wait_ge(sV.h, sV.n)
            inst = sc.activation(ft[:, 0:32], iidx[:, :], AF.Exp,
                                 bias=cons[:, 0:1], scale=-math.log(1e6) / 32.0)
            sA.inc(inst)

        @blk.vector
        def _(v):
            v.wait_ge(sA.h, sA.n)
            for s in range(NSUB):
                v.tensor_scalar(gt[:, 0:32], ft[:, 0:32], poscol[:, s:s + 1], None, ALU.mult)
                for dst, shift in ((csc, 0.5 * math.pi), (ssc, 0.0)):
                    v.tensor_scalar(gt[:, 32:64], gt[:, 0:32], shift, 1.0 / TWO_PI,
                                    ALU.add, ALU.mult)
                    v.tensor_copy(ki[:, 0:32], gt[:, 32:64])
                    v.tensor_copy(gt[:, 64:96], ki[:, 0:32])
                    v.tensor_scalar(gt[:, 96:128], gt[:, 64:96], -TWO_PI, shift,
                                    ALU.mult, ALU.add)
                    dcol = dst[:, 32 * s:32 * (s + 1)]
                    v.tensor_add(dcol, gt[:, 0:32], gt[:, 96:128])
                    v.tensor_scalar(gt[:, 32:64], dcol, math.pi, None, ALU.is_gt)
                    v.scalar_tensor_tensor(dcol, gt[:, 32:64], -TWO_PI, dcol,
                                           ALU.mult, ALU.add)
                    v.tensor_scalar(gt[:, 32:64], dcol, -math.pi, None, ALU.is_lt)
                    inst = v.scalar_tensor_tensor(dcol, gt[:, 32:64], TWO_PI, dcol,
                                                  ALU.mult, ALU.add)
            sV.inc(inst)

        @blk.scalar
        def _(sc):
            sc.wait_ge(sV.h, sV.n)
            sc.activation(csc[:, :], csc[:, :], AF.Sin, bias=cons[:, 0:1], scale=1.0)
            inst = sc.activation(ssc[:, :], ssc[:, :], AF.Sin, bias=cons[:, 0:1], scale=1.0)
            sA.inc(inst)

    if upto <= 1:
        return _cut()

    def emit_ln_phase(blk, load_fn, store_fn):
        g0, a0, v0 = sG.n, sA.n, sV.n

        @blk.gpsimd
        def _(g):
            for s in range(NSUB):
                g.wait_ge(sV.h, v0 + 2 * s)
                load_fn(g, s)
                g.wait_ge(sV.h, v0 + 2 * s + 2)
                store_fn(g, s)

        @blk.scalar
        def _(sc):
            for s in range(NSUB):
                sc.wait_ge(sG.h, g0 + 32 * s + 16)
                inst = sc.activation(ft[:, :], xt[:, :], AF.Square, bias=cons[:, 0:1],
                                     scale=1.0, accum_out=stats[:, 1:2])
                sA.inc(inst)
                sc.wait_ge(sV.h, v0 + 2 * s + 1)
                sc.activation(stats[:, 6:7], stats[:, 4:5], AF.Ln,
                              bias=cons[:, 1:2], scale=1.0)
                inst = sc.activation(stats[:, 5:6], stats[:, 6:7], AF.Exp,
                                     bias=cons[:, 0:1], scale=-0.5)
                sA.inc(inst)

        @blk.vector
        def _(v):
            for s in range(NSUB):
                v.wait_ge(sG.h, g0 + 32 * s + 16)
                v.reduce_sum(stats[:, 0:1], xt[:, :], axis=AX)
                v.wait_ge(sA.h, a0 + 2 * s + 1)
                v.tensor_scalar(stats[:, 2:3], stats[:, 0:1], 1.0 / D, None, ALU.mult)
                v.tensor_scalar(stats[:, 3:4], stats[:, 1:2], 1.0 / D, None, ALU.mult)
                v.tensor_mul(stats[:, 4:5], stats[:, 2:3], stats[:, 2:3])
                inst = v.tensor_sub(stats[:, 4:5], stats[:, 3:4], stats[:, 4:5])
                sV.inc(inst)
                v.wait_ge(sA.h, a0 + 2 * s + 2)
                v.wait_ge(sG.h, g0 + 32 * s)
                inst = v.tensor_scalar(stg[0][:, :], xt[:, :], stats[:, 2:3],
                                       stats[:, 5:6], ALU.subtract, ALU.mult)
                sV.inc(inst)

    with nc.Block("p1") as blk:
        def p1_load(g, s):
            sG.inc(g.dma_start(out=xt[:, :], in_=x_sl[s * 128:(s + 1) * 128, :]), 16)

        def p1_store(g, s):
            sG.inc(g.dma_start(out=lnx_d[s * 128:(s + 1) * 128, :], in_=stg[0][:, :]), 16)
        emit_ln_phase(blk, p1_load, p1_store)

    with nc.Block("p1t") as blk:
        @blk.sync
        def _(s):
            s.wait_ge(sG.h, sG.n)
            for ch in range(16):
                inst = s.dma_start(out=act_fm[:, ch * TS:(ch + 1) * TS],
                                   in_=lnx_d[:, ch * 128:(ch + 1) * 128], transpose=True)
                sSy.inc(inst, 16)

    if upto <= 2:
        return _cut()

    stgq = [sha[:, 4096 + i * 512:4096 + (i + 1) * 512] for i in range(8)]

    def emit_stream_projs(specs, accum_ssq=False, name="proj"):
        with nc.Block(name) as blk:
            t0, a0, v0, g0 = sT.n, sA.n, sV.n, sG.n
            sy0, soe_snap = sSy.n, soe.n
            nq = 4 * len(specs)
            WL, TQ, ST = {}, {}, {}
            g_cnt = g0
            for kq in range(nq):
                g_cnt += 16 * (17 if specs[kq // 4][1] is not None else 16)
                WL[kq] = g_cnt
                TQ[kq] = t0 + 8 * (kq + 1)
            for it in range(8 * nq):
                ST[it] = sy0 + 16 * (it + 1)

            @blk.gpsimd
            def _(g):
                g.wait_ge(sT.h, t0)
                g.wait_ge(soe.h, soe_snap)
                for kq in range(nq):
                    w_ap, b_ap, _ = specs[kq // 4]
                    qu = kq % 4
                    if kq >= 2:
                        g.wait_ge(sT.h, TQ[kq - 2])
                    qb = (kq % 2) * 8192
                    for ch in range(16):
                        inst = g.dma_start(
                            out=wstage[:, qb + ch * 512:qb + (ch + 1) * 512],
                            in_=w_ap[ch * 128:(ch + 1) * 128, qu * 512:(qu + 1) * 512])
                        sG.inc(inst, 16)
                    if b_ap is not None:
                        inst = g.dma_start(out=bias_row[0:1, qu * 512:(qu + 1) * 512],
                                           in_=b_ap[qu * 512:(qu + 1) * 512])
                        sG.inc(inst, 16)

            @blk.tensor
            def _(t):
                t.wait_ge(sSy.h, sy0)
                t.wait_ge(sV.h, v0)
                t.wait_ge(soe.h, soe_snap)
                it = 0
                for kq in range(nq):
                    b_ap = specs[kq // 4][1]
                    qu = kq % 4
                    t.wait_ge(sG.h, WL[kq])
                    qb = (kq % 2) * 8192
                    for s in range(NSUB):
                        bank = ps[it % 2]
                        if it >= 2:
                            t.wait_ge(sV.h, v0 + it - 1)
                        for ch in range(16):
                            inst = t.matmul(bank[:, :],
                                            act_fm[:, ch * TS + s * 128:ch * TS + (s + 1) * 128],
                                            wstage[:, qb + ch * 512:qb + (ch + 1) * 512],
                                            start=(ch == 0),
                                            stop=(b_ap is None and ch == 15))
                        if b_ap is not None:
                            inst = t.matmul(bank[:, :], ones_b[0:1, 0:128],
                                            bias_row[0:1, qu * 512:(qu + 1) * 512],
                                            start=False, stop=True)
                        sT.inc(inst)
                        it += 1

            if accum_ssq:
                @blk.scalar
                def _(sc):
                    for it in range(8 * nq):
                        sc.wait_ge(sT.h, t0 + it + 1)
                        if it >= 1:
                            sc.wait_ge(sV.h, v0 + it)
                        inst = sc.activation(qsq[:, :], ps[it % 2][:, :], AF.Square,
                                             bias=cons[:, 0:1], scale=1.0,
                                             accum_out=stats[:, 8:9])
                        sA.inc(inst)

            @blk.vector
            def _(v):
                for it in range(8 * nq):
                    kq, s = it // 8, it % 8
                    if accum_ssq:
                        v.wait_ge(sA.h, a0 + it + 1)
                        if kq == 0:
                            v.tensor_copy(ssqa[:, s:s + 1], stats[:, 8:9])
                        else:
                            v.tensor_add(ssqa[:, s:s + 1], ssqa[:, s:s + 1],
                                         stats[:, 8:9])
                    else:
                        v.wait_ge(sT.h, t0 + it + 1)
                    if it >= 8:
                        v.wait_ge(sSy.h, ST[it - 8])
                    inst = v.tensor_copy(stgq[it % 8][:, :], ps[it % 2][:, :])
                    sV.inc(inst)

            @blk.sync
            def _(sy):
                for it in range(8 * nq):
                    kq, s = it // 8, it % 8
                    dst_fn = specs[kq // 4][2]
                    sy.wait_ge(sV.h, v0 + it + 1)
                    dst_fn(sy, s, kq % 4, it % 8)

    def mk_dst(dst_d, col_off):
        def f(eng, s, qu, slot):
            inst = eng.dma_start(
                out=dst_d[s * 128:(s + 1) * 128,
                          col_off + qu * 512:col_off + (qu + 1) * 512],
                in_=stgq[slot][:, :])
            sSy.inc(inst, 16)
        return f

    emit_stream_projs([(Wq, bq, mk_dst(q_d, 0)),
                       (Wk, bk, mk_dst(agA_in, 0)),
                       (Wv, bv, mk_dst(agA_in, D))], name="p2qkv")

    with nc.Block("p3") as blk:
        @blk.gpsimd
        def _(g):
            g.wait_ge(sG.h, sG.n)
            g.wait_ge(sSy.h, sSy.n)
            inst = g.collective_compute("AllGather", ALU.bypass, replica_groups=RG8,
                                        ins=[agA_in], outs=[agA_out])
            sCC.inc(inst)

    if upto <= 3:
        return _cut()

    def emit_attention(tag, kv_body_fn, q_in_switch, q_tr_fn, out_ap):
        slabs = [(hb, sl) for hb in range(2) for sl in range(16)]
        M_of = {(hb, sl): 4 * (sl + 1) for (hb, sl) in slabs}

        QK, EX, MK, PV = {}, {}, {}, {}
        SQk, SQq, CSk, CSq = {}, {}, {}, {}
        KP, M2, BE, NB, OE, QTR, OSTORE = {}, {}, {}, {}, {}, {}, {}
        cqk, cex, cmk, cpv = sqk.n, sxp.n, smk.n, spv.n
        csq, ccs, ckp = sbsq.n, sbcs.n, sV.n
        cm2, cbe, cnb, coe = sm2.n, sbe.n, snb.n, soe.n
        nk_tr = 16 * 16
        qtr_base = sSy.n + nk_tr
        cg_store = sG.n + 16 * 16
        for idx, (hb, sl) in enumerate(slabs):
            M = M_of[(hb, sl)]
            if sl == 0:
                for i in range(16):
                    csq += 1; SQk[(hb, i)] = csq
                    ccs += 1; CSk[(hb, i)] = ccs
                    ckp += 1; KP[(hb, i)] = ckp
            csq += 1; SQq[idx] = csq
            ccs += 1; CSq[idx] = ccs
            cm2 += 1; M2[idx] = cm2
            cbe += 1; BE[idx] = cbe
            cnb += 1; NB[idx] = cnb
            QTR[idx] = qtr_base + 16 * (idx + 1)
            for m in range(M):
                cqk += 1; QK[(idx, m)] = cqk
                cex += 1; EX[(idx, m)] = cex
                cpv += 1; PV[(idx, m)] = cpv
                if m >= 4 * sl:
                    cmk += 1; MK[(idx, m)] = cmk
            coe += 1; OE[idx] = coe
            cg_store += 16
            OSTORE[idx] = cg_store
        kv_ready_g = sG.n + 16 * 16
        khead_sy = {0: sSy.n + 16 * 8, 1: sSy.n + 16 * 16}
        v_snap = sV.n

        rg = nc.gpsimd.compute_val(nc.partition_id([ET.Pool]))
        rsy = nc.sync.compute_val(nc.partition_id([ET.SP]))
        qd_snap = sG.n
        for i in nc.Switch(engines=[nc.gpsimd, nc.sync], index=[rg, rsy], n=NCO):
            nc.gpsimd.wait_ge(sCC.h, sCC.n)
            nc.sync.wait_ge(sCC.h, sCC.n)
            for hb in range(2):
                kv_body_fn(i, hb)
            if q_in_switch:
                for idx, (hb, sl) in enumerate(slabs):
                    if idx >= 2:
                        nc.sync.wait_ge(sqk.h, QK[(idx - 1, 0)] - 1)
                    nc.sync.dma_start(out=qslab[idx % 2][:, :],
                                      in_=q_tr_fn(i, hb, sl),
                                      transpose=True).then_inc(sSy.h, 16)
        sSy.n += nk_tr + (16 * 32 if q_in_switch else 0)
        sG.n += 16 * 16

        with nc.Block(f"{tag}_att") as blk:
            if not q_in_switch:
                @blk.sync
                def _(s):
                    s.wait_ge(sG.h, qd_snap)
                    for idx, (hb, sl) in enumerate(slabs):
                        if idx >= 2:
                            s.wait_ge(sqk.h, QK[(idx - 1, 0)] - 1)
                        inst = s.dma_start(out=qslab[idx % 2][:, :],
                                           in_=q_tr_fn(None, hb, sl), transpose=True)
                        sSy.inc(inst, 16)

            @blk.tensor
            def _(t):
                t.wait_ge(sG.h, kv_ready_g)
                t.wait_ge(sV.h, v_snap)
                for idx, (hb, sl) in enumerate(slabs):
                    M = M_of[(hb, sl)]
                    if sl == 0:
                        for i in range(16):
                            t.wait_ge(sbsq.h, SQk[(hb, i)])
                            if i >= 1:
                                t.wait_ge(sV.h, KP[(hb, i - 1)])
                            elif idx > 0:
                                t.wait_ge(soe.h, OE[idx - 1])
                            inst = t.matmul(ps[6][:, :], ones_b[:, 0:128], qsq[:, :],
                                            start=True, stop=True)
                            sbcs.inc(inst)
                    t.wait_ge(sbsq.h, SQq[idx])
                    if sl == 0:
                        t.wait_ge(sV.h, KP[(hb, 15)])
                    if idx > 0:
                        t.wait_ge(soe.h, OE[idx - 1])
                    inst = t.matmul(ps[6][:, :], ones_b[:, 0:128], qsq[:, :],
                                    start=True, stop=True)
                    sbcs.inc(inst)

                    def emit_pv(mp, M=M, sl=sl, idx=idx, hb=hb, t=t):
                        apv = mp - 4 * sl
                        jp = 128 * apv if apv > 0 else 0
                        t.wait_ge(sxp.h, EX[(idx, mp)])
                        if apv >= 0:
                            t.wait_ge(smk.h, MK[(idx, mp)])
                        t.matmul(ps[4][:, jp:512],
                                 vslab[hb][:, 128 * mp:128 * (mp + 1)],
                                 pt[mp % 4][:, jp:512],
                                 start=(mp == 0), stop=(mp == M - 1))
                        inst = t.matmul(ps[5][:, jp:512], ones_b[:, 0:128],
                                        pt[mp % 4][:, jp:512],
                                        start=(mp == 0), stop=(mp == M - 1))
                        spv.inc(inst)

                    for m in range(M):
                        a = m - 4 * sl
                        j0 = 128 * a if a > 0 else 0
                        if m >= 4:
                            t.wait_ge(sxp.h, EX[(idx, m - 4)])
                        inst = t.matmul(ps[m % 4][:, j0:512],
                                        kslab[hb][:, 128 * m:128 * (m + 1)],
                                        qslab[idx % 2][:, j0:512], start=True, stop=True)
                        sqk.inc(inst)
                        if m >= 2:
                            emit_pv(m - 2)
                    emit_pv(M - 2)
                    emit_pv(M - 1)

            @blk.scalar
            def _(sc):
                for idx, (hb, sl) in enumerate(slabs):
                    M = M_of[(hb, sl)]
                    if sl == 0:
                        sc.wait_ge(sSy.h, khead_sy[hb])
                        for i in range(16):
                            if i >= 1:
                                sc.wait_ge(sbcs.h, CSk[(hb, i - 1)])
                            inst = sc.activation(qsq[:, :],
                                                 kslab[hb][:, 512 * i:512 * (i + 1)],
                                                 AF.Square, bias=cons[:, 0:1], scale=1.0)
                            sbsq.inc(inst)
                    sc.wait_ge(sSy.h, QTR[idx])
                    sc.wait_ge(sbcs.h, CSk[(hb, 15)] if sl == 0 else CSq[idx] - 1)
                    inst = sc.activation(qsq[:, :], qslab[idx % 2][:, :], AF.Square,
                                         bias=cons[:, 0:1], scale=1.0)
                    sbsq.inc(inst)
                    sc.wait_ge(sm2.h, M2[idx])
                    sc.activation(stats[:, 9:10], stats[:, 8:9], AF.Ln,
                                  bias=cons[:, 0:1], scale=1.0)
                    inst = sc.activation(stats[:, 10:11], stats[:, 9:10], AF.Exp,
                                         bias=cons[:, 3:4], scale=0.5)
                    sbe.inc(inst)
                    for m in range(M):
                        a = m - 4 * sl
                        j0 = 128 * a if a > 0 else 0
                        sc.wait_ge(sqk.h, QK[(idx, m)])
                        sc.wait_ge(snb.h, NB[idx])
                        if m >= 4:
                            sc.wait_ge(spv.h, PV[(idx, m - 4)])
                        elif idx > 0:
                            sc.wait_ge(spv.h, PV[(idx - 1, M_of[slabs[idx - 1]] - 1)])
                        inst = sc.activation(pt[m % 4][:, j0:512], ps[m % 4][:, j0:512],
                                             AF.Exp, bias=stats[:, 11:12], scale=SA)
                        sxp.inc(inst)

            @blk.vector
            def _(v):
                for idx, (hb, sl) in enumerate(slabs):
                    M = M_of[(hb, sl)]
                    if sl == 0:
                        for i in range(16):
                            v.wait_ge(sbcs.h, CSk[(hb, i)])
                            if i == 0:
                                inst = v.reduce_max(stats[:, 7:8], ps[6][:, :], axis=AX)
                            else:
                                v.reduce_max(stats[:, 12:13], ps[6][:, :], axis=AX)
                                inst = v.tensor_max(stats[:, 7:8], stats[:, 7:8],
                                                    stats[:, 12:13])
                            sV.inc(inst)
                    v.wait_ge(sbcs.h, CSq[idx])
                    v.reduce_max(stats[:, 12:13], ps[6][:, :], axis=AX)
                    inst = v.tensor_mul(stats[:, 8:9], stats[:, 12:13], stats[:, 7:8])
                    sm2.inc(inst)
                    v.wait_ge(sbe.h, BE[idx])
                    inst = v.tensor_scalar(stats[:, 11:12], stats[:, 10:11], -1.0,
                                           None, ALU.mult)
                    snb.inc(inst)
                    for m in range(M):
                        a = m - 4 * sl
                        if a >= 0:
                            j0 = 128 * a if a > 0 else 0
                            v.wait_ge(sxp.h, EX[(idx, m)])
                            inst = v.tensor_mul(pt[m % 4][:, j0:512],
                                                pt[m % 4][:, j0:512],
                                                m_diag[:, 512 * a + j0:512 * a + 512])
                            smk.inc(inst)
                    v.wait_ge(spv.h, PV[(idx, M - 1)])
                    if idx >= 1:
                        v.wait_ge(sG.h, OSTORE[idx - 1])
                    v.reciprocal(linv[:, :], ps[5][:, :])
                    inst = v.tensor_mul(outsb[:, :], ps[4][:, :], linv[:, :])
                    soe.inc(inst)

            @blk.gpsimd
            def _(g):
                for idx, (hb, sl) in enumerate(slabs):
                    g.wait_ge(soe.h, OE[idx])
                    inst = g.dma_start(
                        out=out_ap[128 * hb:128 * (hb + 1), 512 * sl:512 * (sl + 1)],
                        in_=outsb[:, :])
                    sG.inc(inst, 16)

    def kv1_body(i, hb):
        h = 2 * i + hb
        for r in range(NCO):
            nc.sync.dma_start(out=kslab[hb][:, 1024 * r:1024 * (r + 1)],
                              in_=agA_out[r, :, 128 * h:128 * (h + 1)],
                              transpose=True).then_inc(sSy.h, 16)
        for r in range(NCO):
            src3 = agA_out[r, :, D + 128 * h:D + 128 * (h + 1)].rearrange(
                "(mm p) d -> p mm d", p=128)
            dst3 = vslab[hb][:, 1024 * r:1024 * (r + 1)].rearrange(
                "p (mm d) -> p mm d", d=128)
            nc.gpsimd.dma_start(out=dst3, in_=src3).then_inc(sG.h, 16)

    q_flat = q_d.rearrange("a (b c) -> (a b) c", c=128)

    def q1_tr(i, hb, sl):
        row0 = 8192 * hb + 512 * sl
        return q_flat[row0:row0 + 512, :]

    emit_attention("b1", kv1_body, False, q1_tr, agB_in)

    if upto <= 4:
        return _cut()

    with nc.Block("p5ag") as blk:
        @blk.gpsimd
        def _(g):
            g.wait_ge(sG.h, sG.n)
            inst = g.collective_compute("AllGather", ALU.bypass, replica_groups=RG8,
                                        ins=[agB_in], outs=[agB_out])
            sCC.inc(inst)

    def emit_wo(tag, src_out, w_ap, b_ap, res_ap, dst_d, res_kind):
        nvi = 2 if res_kind == "ln_res" else 1
        with nc.Block(f"{tag}_wow") as blk:
            t_snap = sT.n

            @blk.gpsimd
            def _(g):
                g.wait_ge(sT.h, t_snap)
                for ch in range(16):
                    inst = g.dma_start(out=wstage[:, ch * 1024:(ch + 1) * 1024],
                                       in_=w_ap[ch * 128:(ch + 1) * 128, 0:1024])
                    sG.inc(inst, 16)
                for ch in range(16):
                    inst = g.dma_start(out=act_fm[:, ch * 1024:(ch + 1) * 1024],
                                       in_=w_ap[ch * 128:(ch + 1) * 128, 1024:2048])
                    sG.inc(inst, 16)
                if b_ap is not None:
                    inst = g.dma_start(out=bias_row[0:1, :], in_=b_ap[:])
                    sG.inc(inst, 16)

        if upto <= 4.5:
            return
        rg = nc.gpsimd.compute_val(nc.partition_id([ET.Pool]))
        t_base = sT.n
        L = {}
        gl = sG.n
        for s in range(NSUB):
            gl += 16
            L[s] = gl
        for i in nc.Switch(engines=[nc.gpsimd], index=[rg], n=NCO):
            nc.gpsimd.wait_ge(sCC.h, sCC.n)
            for s in range(NSUB):
                if s >= 2 and upto > 4.6:
                    nc.gpsimd.wait_ge(sT.h, t_base + 4 * (s - 1))
                col0 = TS * i + 128 * s
                src4 = src_out[:, :, col0:col0 + 128].rearrange(
                    "r (b p) i -> p r b i", p=128)
                dst4 = lhsb[s % 2].rearrange("p (r b i) -> p r b i", r=8, b=2)
                nc.gpsimd.dma_start(out=dst4, in_=src4).then_inc(sG.h, 16)
        sG.n += 16 * NSUB

        if upto <= 4.6:
            return
        with nc.Block(f"{tag}_wo") as blk:
            t0, a0, v0, g0 = sT.n, sA.n, sV.n, sG.n
            soe_snap = soe.n

            @blk.tensor
            def _(t):
                t.wait_ge(soe.h, soe_snap)
                t.wait_ge(sV.h, v0)
                for s in range(NSUB):
                    t.wait_ge(sG.h, L[s])
                    if upto > 4.65:
                        t.wait_ge(sV.h, v0 + s * nvi)
                    for ob in range(4):
                        wsrc = wstage if ob < 2 else act_fm
                        o2 = ob % 2
                        for ch in range(16):
                            inst = t.matmul(ps[ob][:, :], lhsb[s % 2][:, 128 * ch:128 * (ch + 1)],
                                            wsrc[:, ch * 1024 + o2 * 512:ch * 1024 + (o2 + 1) * 512],
                                            start=(ch == 0), stop=(b_ap is None and ch == 15))
                        if b_ap is not None:
                            inst = t.matmul(ps[ob][:, :], ones_b[0:1, 0:128],
                                            bias_row[0:1, 512 * ob:512 * (ob + 1)],
                                            start=False, stop=True)
                        sT.inc(inst)

            if upto <= 4.65:
                return
            sy0 = sSy.n

            @blk.sync
            def _(sy):
                for s in range(NSUB):
                    sy.wait_ge(sV.h, v0 + s * nvi)
                    inst = sy.dma_start(out=xt[:, :], in_=res_ap[128 * s:128 * (s + 1), :])
                    sSy.inc(inst, 16)
                    sy.wait_ge(sV.h, v0 + (s + 1) * nvi)
                    inst = sy.dma_start(out=dst_d[128 * s:(s + 1) * 128, :], in_=gt[:, :])
                    sSy.inc(inst, 16)

            if res_kind == "ln_res":
                @blk.scalar
                def _(sc):
                    for s in range(NSUB):
                        sc.wait_ge(sT.h, t0 + 4 * s + 4)
                        sc.wait_ge(sV.h, v0 + s * nvi)
                        for ob in range(4):
                            sq_src = ps[ob][:, :] if upto > 4.8 else xt[:, 512 * ob:512 * (ob + 1)]
                            inst = sc.activation(ft[:, 512 * ob:512 * (ob + 1)], sq_src,
                                                 AF.Square, bias=cons[:, 0:1], scale=1.0,
                                                 accum_out=stats[:, ob:ob + 1])
                        sA.inc(inst)
                        sc.wait_ge(sV.h, v0 + s * nvi + 1)
                        sc.activation(stats[:, 6:7], stats[:, 4:5], AF.Ln,
                                      bias=cons[:, 1:2], scale=1.0)
                        inst = sc.activation(stats[:, 5:6], stats[:, 6:7], AF.Exp,
                                             bias=cons[:, 0:1], scale=-0.5)
                        sA.inc(inst)

            @blk.vector
            def _(v):
                for s in range(NSUB):
                    v.wait_ge(sT.h, t0 + 4 * s + 4)
                    v.wait_ge(sSy.h, sy0 + 16 * (2 * s + 1))
                    if s >= 1:
                        v.wait_ge(sSy.h, sy0 + 16 * 2 * s)
                    if res_kind == "ln_res":
                        v.wait_ge(sA.h, a0 + s * 2 + 1)
                        if upto > 4.75:
                            for ob in range(4):
                                if ob == 0:
                                    v.reduce_sum(stats[:, 12:13], ps[ob][:, :], axis=AX)
                                else:
                                    v.reduce_sum(stats[:, 13:14], ps[ob][:, :], axis=AX)
                                    v.tensor_add(stats[:, 12:13], stats[:, 12:13],
                                                 stats[:, 13:14])
                            v.tensor_add(stats[:, 14:15], stats[:, 0:1], stats[:, 1:2])
                            v.tensor_add(stats[:, 15:16], stats[:, 2:3], stats[:, 3:4])
                            v.tensor_add(stats[:, 14:15], stats[:, 14:15], stats[:, 15:16])
                            v.tensor_scalar(stats[:, 2:3], stats[:, 12:13], 1.0 / D, None, ALU.mult)
                            v.tensor_scalar(stats[:, 3:4], stats[:, 14:15], 1.0 / D, None, ALU.mult)
                            v.tensor_mul(stats[:, 4:5], stats[:, 2:3], stats[:, 2:3])
                            inst = v.tensor_sub(stats[:, 4:5], stats[:, 3:4], stats[:, 4:5])
                        else:
                            inst = v.tensor_copy(stats[:, 4:5], cons[:, 1:2])
                        sV.inc(inst)
                        v.wait_ge(sA.h, a0 + s * 2 + 2)
                        for ob in range(4):
                            if upto > 4.77:
                                v.tensor_scalar(ft[:, 512 * ob:512 * (ob + 1)], ps[ob][:, :],
                                                stats[:, 2:3], stats[:, 5:6],
                                                ALU.subtract, ALU.mult)
                            else:
                                v.tensor_copy(ft[:, 512 * ob:512 * (ob + 1)], ps[ob][:, :])
                        inst = v.tensor_add(gt[:, :], xt[:, :], ft[:, :])
                        sV.inc(inst)
                    else:
                        for ob in range(4):
                            v.tensor_copy(ft[:, 512 * ob:512 * (ob + 1)], ps[ob][:, :])
                        inst = v.tensor_add(gt[:, :], xt[:, :], ft[:, :])
                        sV.inc(inst)

    if upto <= 4.3:
        return _cut()
    emit_wo("b1", agB_out, Wo, bo, x_sl, x2_d,
            "add" if upto <= 4.7 else "ln_res")
    if upto <= 4.9:
        return _cut()

    if upto <= 5:
        return _cut()

    with nc.Block("p6w") as blk:
        @blk.gpsimd
        def _(g):
            sG.inc(g.dma_start(out=wrep[:, :], in_=ln1w_r), 16)

    with nc.Block("p6") as blk:
        g0, a0, v0 = sG.n, sA.n, sV.n

        @blk.gpsimd
        def _(g):
            for s in range(NSUB):
                g.wait_ge(sV.h, v0 + 3 * s)
                inst = g.dma_start(out=xt[:, :], in_=x2_d[128 * s:(s + 1) * 128, :])
                sG.inc(inst, 16)
                g.wait_ge(sV.h, v0 + 3 * s + 2)
                inst = g.dma_start(out=li2_d[128 * s:(s + 1) * 128, :], in_=gt[:, :])
                sG.inc(inst, 16)
                g.wait_ge(sV.h, v0 + 3 * s + 3)
                inst = g.dma_start(out=h_d[128 * s:(s + 1) * 128, :], in_=stg[0][:, :])
                sG.inc(inst, 16)

        @blk.scalar
        def _(sc):
            for s in range(NSUB):
                sc.wait_ge(sG.h, g0 + 48 * s + 16)
                inst = sc.activation(ft[:, :], xt[:, :], AF.Square, bias=cons[:, 0:1],
                                     scale=1.0, accum_out=stats[:, 1:2])
                sA.inc(inst)
                sc.wait_ge(sV.h, v0 + 3 * s + 1)
                sc.activation(stats[:, 6:7], stats[:, 4:5], AF.Ln,
                              bias=cons[:, 1:2], scale=1.0)
                inst = sc.activation(stats[:, 5:6], stats[:, 6:7], AF.Exp,
                                     bias=cons[:, 0:1], scale=-0.5)
                sA.inc(inst)
                sc.wait_ge(sV.h, v0 + 3 * s + 2)
                sc.activation(stats[:, 8:9], stats[:, 7:8], AF.Ln,
                              bias=cons[:, 2:3], scale=1.0)
                inst = sc.activation(stats[:, 9:10], stats[:, 8:9], AF.Exp,
                                     bias=cons[:, 0:1], scale=-0.5)
                sA.inc(inst)

        @blk.vector
        def _(v):
            for s in range(NSUB):
                v.wait_ge(sG.h, g0 + 48 * s + 16)
                v.reduce_sum(stats[:, 0:1], xt[:, :], axis=AX)
                v.wait_ge(sA.h, a0 + 3 * s + 1)
                v.tensor_scalar(stats[:, 2:3], stats[:, 0:1], 1.0 / D, None, ALU.mult)
                v.tensor_scalar(stats[:, 3:4], stats[:, 1:2], 1.0 / D, None, ALU.mult)
                v.tensor_mul(stats[:, 4:5], stats[:, 2:3], stats[:, 2:3])
                inst = v.tensor_sub(stats[:, 4:5], stats[:, 3:4], stats[:, 4:5])
                sV.inc(inst)
                v.wait_ge(sA.h, a0 + 3 * s + 2)
                v.tensor_mul(stats[:, 7:8], stats[:, 5:6], stats[:, 5:6])
                v.tensor_mul(stats[:, 7:8], stats[:, 7:8], stats[:, 4:5])
                if s >= 1:
                    v.wait_ge(sG.h, g0 + 48 * s - 16)
                inst = v.tensor_scalar(gt[:, :], xt[:, :], stats[:, 2:3],
                                       stats[:, 5:6], ALU.subtract, ALU.mult)
                sV.inc(inst)
                v.wait_ge(sA.h, a0 + 3 * s + 3)
                if s >= 1:
                    v.wait_ge(sG.h, g0 + 48 * s)
                v.tensor_scalar(ft[:, :], gt[:, :], stats[:, 9:10], None, ALU.mult)
                inst = v.tensor_mul(stg[0][:, :], ft[:, :], wrep[:, :])
                sV.inc(inst)

    with nc.Block("p6t") as blk:
        @blk.sync
        def _(s):
            s.wait_ge(sG.h, sG.n)
            for ch in range(16):
                inst = s.dma_start(out=act_fm[:, ch * TS:(ch + 1) * TS],
                                   in_=h_d[:, ch * 128:(ch + 1) * 128], transpose=True)
                sSy.inc(inst, 16)

    if upto <= 6:
        return _cut()

    emit_stream_projs([(fwq, None, mk_dst(qraw_d, 0))], accum_ssq=True, name="p7q")

    def emit_kv512(w_ap, dst_col, do_knorm, name):
        with nc.Block(name) as blk:
            @blk.gpsimd
            def _(g):
                g.wait_ge(sT.h, sT.n)
                for ch in range(16):
                    inst = g.dma_start(out=wstage[:, ch * 512:(ch + 1) * 512],
                                       in_=w_ap[ch * 128:(ch + 1) * 128, :])
                    sG.inc(inst, 16)
                if do_knorm:
                    inst = g.dma_start(out=wrep[:, 0:512], in_=knw_r)
                    sG.inc(inst, 16)

            t0, a0, v0, g0 = sT.n, sA.n, sV.n, sG.n

            @blk.tensor
            def _(t):
                t.wait_ge(sG.h, g0)
                t.wait_ge(sSy.h, sSy.n)
                t.wait_ge(sV.h, v0)
                for s in range(NSUB):
                    if s >= 2:
                        t.wait_ge(sV.h, v0 + s - 1)
                    for ch in range(16):
                        inst = t.matmul(ps[s % 2][:, :],
                                        act_fm[:, ch * TS + s * 128:ch * TS + (s + 1) * 128],
                                        wstage[:, ch * 512:(ch + 1) * 512],
                                        start=(ch == 0), stop=(ch == 15))
                    sT.inc(inst)

            if do_knorm:
                @blk.scalar
                def _(sc):
                    for s in range(NSUB):
                        sc.wait_ge(sT.h, t0 + s + 1)
                        if s >= 1:
                            sc.wait_ge(sV.h, v0 + s)
                        sc.activation(qsq[:, :], ps[s % 2][:, :], AF.Square,
                                      bias=cons[:, 0:1], scale=1.0,
                                      accum_out=stats[:, 0:1])
                        sc.activation(stats[:, 6:7], stats[:, 0:1], AF.Ln,
                                      bias=cons[:, 2:3], scale=1.0 / 512.0)
                        inst = sc.activation(stats[:, 5:6], stats[:, 6:7], AF.Exp,
                                             bias=cons[:, 0:1], scale=-0.5)
                        sA.inc(inst)

                @blk.vector
                def _(v):
                    for s in range(NSUB):
                        v.wait_ge(sA.h, a0 + s + 1)
                        if s >= 1:
                            v.wait_ge(sG.h, g0 + 16 * s)
                        v.tensor_scalar(rt[:, :], ps[s % 2][:, :], stats[:, 5:6],
                                        None, ALU.mult)
                        v.tensor_mul(rt[:, :], rt[:, :], wrep[:, 0:512])
                        for hh in range(4):
                            v.tensor_copy(crep[:, 32 * hh:32 * hh + 32],
                                          csc[:, 32 * s:32 * s + 32])
                            v.tensor_copy(srep[:, 32 * hh:32 * hh + 32],
                                          ssc[:, 32 * s:32 * s + 32])
                        a_ap = rt[:, :].rearrange("p (h d) -> p h d", d=128)[:, :, 0:32]
                        b_ap2 = rt[:, :].rearrange("p (h d) -> p h d", d=128)[:, :, 32:64]
                        c_ap = crep[:, 0:128].rearrange("p (h d) -> p h d", d=32)
                        s_ap = srep[:, 0:128].rearrange("p (h d) -> p h d", d=32)
                        t1 = gt[:, 0:128].rearrange("p (h d) -> p h d", d=32)
                        t2 = gt[:, 128:256].rearrange("p (h d) -> p h d", d=32)
                        t3 = gt[:, 256:384].rearrange("p (h d) -> p h d", d=32)
                        t4 = gt[:, 384:512].rearrange("p (h d) -> p h d", d=32)
                        v.tensor_mul(t1, a_ap, c_ap)
                        v.tensor_mul(t2, b_ap2, s_ap)
                        v.tensor_mul(t3, b_ap2, c_ap)
                        v.tensor_mul(t4, a_ap, s_ap)
                        v.tensor_sub(a_ap, t1, t2)
                        v.tensor_add(b_ap2, t3, t4)
                        inst = v.tensor_copy(stg[0][:, 0:512], rt[:, :])
                        sV.inc(inst)
            else:
                @blk.vector
                def _(v):
                    for s in range(NSUB):
                        v.wait_ge(sT.h, t0 + s + 1)
                        if s >= 1:
                            v.wait_ge(sG.h, g0 + 16 * s)
                        inst = v.tensor_copy(stg[0][:, 0:512], ps[s % 2][:, :])
                        sV.inc(inst)

            @blk.gpsimd
            def _(g):
                for s in range(NSUB):
                    g.wait_ge(sV.h, v0 + s + 1)
                    inst = g.dma_start(
                        out=agC_in[128 * s:128 * (s + 1), dst_col:dst_col + 512],
                        in_=stg[0][:, 0:512])
                    sG.inc(inst, 16)

    emit_kv512(fwk, 2048, True, "p7k")
    emit_kv512(fwv, 2560, False, "p7v")

    with nc.Block("p7qp") as blk:
        @blk.gpsimd
        def _(g):
            sG.inc(g.dma_start(out=wrep[:, :], in_=qnw_r), 16)

        g0, a0, v0 = sG.n, sA.n, sV.n

        @blk.gpsimd
        def _(g):
            for s in range(NSUB):
                g.wait_ge(sV.h, v0 + 2 * s)
                inst = g.dma_start(out=stg[0][:, :], in_=qraw_d[128 * s:128 * (s + 1), :])
                sG.inc(inst, 16)
                g.wait_ge(sV.h, v0 + 2 * s + 2)
                inst = g.dma_start(out=agC_in[128 * s:128 * (s + 1), 0:2048],
                                   in_=stg[1][:, :])
                sG.inc(inst, 16)

        @blk.scalar
        def _(sc):
            for s in range(NSUB):
                sc.wait_ge(sG.h, g0 + 32 * s + 16)
                sc.activation(stats[:, 6:7], ssqa[:, s:s + 1], AF.Ln,
                              bias=cons[:, 2:3], scale=1.0 / D)
                inst = sc.activation(stats[:, 5:6], stats[:, 6:7], AF.Exp,
                                     bias=cons[:, 0:1], scale=-0.5)
                sA.inc(inst)

        @blk.vector
        def _(v):
            for s in range(NSUB):
                v.wait_ge(sG.h, g0 + 32 * s + 16)
                v.wait_ge(sA.h, a0 + s + 1)
                v.tensor_scalar(ft[:, :], stg[0][:, :], stats[:, 5:6], None, ALU.mult)
                inst = v.tensor_mul(ft[:, :], ft[:, :], wrep[:, :])
                sV.inc(inst)
                for hh in range(16):
                    v.tensor_copy(crep[:, 32 * hh:32 * hh + 32], csc[:, 32 * s:32 * s + 32])
                    v.tensor_copy(srep[:, 32 * hh:32 * hh + 32], ssc[:, 32 * s:32 * s + 32])
                a_ap = ft[:, :].rearrange("p (h d) -> p h d", d=128)[:, :, 0:32]
                b_ap2 = ft[:, :].rearrange("p (h d) -> p h d", d=128)[:, :, 32:64]
                c_ap = crep[:, :].rearrange("p (h d) -> p h d", d=32)
                s_ap = srep[:, :].rearrange("p (h d) -> p h d", d=32)
                t1 = gt[:, 0:512].rearrange("p (h d) -> p h d", d=32)
                t2 = gt[:, 512:1024].rearrange("p (h d) -> p h d", d=32)
                t3 = gt[:, 1024:1536].rearrange("p (h d) -> p h d", d=32)
                t4 = gt[:, 1536:2048].rearrange("p (h d) -> p h d", d=32)
                v.tensor_mul(t1, a_ap, c_ap)
                v.tensor_mul(t2, b_ap2, s_ap)
                v.tensor_mul(t3, b_ap2, c_ap)
                v.tensor_mul(t4, a_ap, s_ap)
                v.tensor_sub(a_ap, t1, t2)
                v.tensor_add(b_ap2, t3, t4)
                inst = v.tensor_copy(stg[1][:, :], ft[:, :])
                sV.inc(inst)

    with nc.Block("p8ag") as blk:
        @blk.gpsimd
        def _(g):
            g.wait_ge(sG.h, sG.n)
            inst = g.collective_compute("AllGather", ALU.bypass, replica_groups=RG8,
                                        ins=[agC_in], outs=[agC_out])
            sCC.inc(inst)

    if upto <= 7:
        return _cut()

    def kv2_body(i, hb):
        gidx = i // 2
        for r in range(NCO):
            nc.sync.dma_start(out=kslab[hb][:, 1024 * r:1024 * (r + 1)],
                              in_=agC_out[r, :, 2048 + 128 * gidx:2048 + 128 * (gidx + 1)],
                              transpose=True).then_inc(sSy.h, 16)
        for r in range(NCO):
            src3 = agC_out[r, :, 2560 + 128 * gidx:2560 + 128 * (gidx + 1)].rearrange(
                "(mm p) d -> p mm d", p=128)
            dst3 = vslab[hb][:, 1024 * r:1024 * (r + 1)].rearrange(
                "p (mm d) -> p mm d", d=128)
            nc.gpsimd.dma_start(out=dst3, in_=src3).then_inc(sG.h, 16)

    def q2_tr(i, hb, sl):
        h = 2 * i + hb
        r = sl // 2
        row0 = 512 * (sl % 2)
        return agC_out[r, row0:row0 + 512, 128 * h:128 * (h + 1)]

    emit_attention("b2", kv2_body, True, q2_tr, agD_in)

    if upto <= 8:
        return _cut()

    with nc.Block("p9ag") as blk:
        @blk.gpsimd
        def _(g):
            g.wait_ge(sG.h, sG.n)
            inst = g.collective_compute("AllGather", ALU.bypass, replica_groups=RG8,
                                        ins=[agD_in], outs=[agD_out])
            sCC.inc(inst)

    emit_wo("b2", agD_out, fwo, None, li2_d, x1_d, "add")

    if upto <= 9:
        return _cut()

    with nc.Block("p9w") as blk:
        @blk.gpsimd
        def _(g):
            sG.inc(g.dma_start(out=wrep[:, :], in_=ln2w_r), 16)
            src3 = rw.rearrange("(c p) e -> p c e", p=128)
            dst3 = rwb[:, :].rearrange("p (c e) -> p c e", e=8)
            sG.inc(g.dma_start(out=dst3, in_=src3), 16)

    with nc.Block("p9h") as blk:
        g0, a0, v0 = sG.n, sA.n, sV.n

        @blk.gpsimd
        def _(g):
            for s in range(NSUB):
                g.wait_ge(sV.h, v0 + s)
                inst = g.dma_start(out=xt[:, :], in_=x1_d[128 * s:(s + 1) * 128, :])
                sG.inc(inst, 16)
                g.wait_ge(sV.h, v0 + s + 1)
                inst = g.dma_start(out=h2_d[128 * s:(s + 1) * 128, :], in_=stg[0][:, :])
                sG.inc(inst, 16)

        @blk.scalar
        def _(sc):
            for s in range(NSUB):
                sc.wait_ge(sG.h, g0 + 32 * s + 16)
                sc.activation(ft[:, :], xt[:, :], AF.Square, bias=cons[:, 0:1],
                              scale=1.0, accum_out=stats[:, 0:1])
                sc.activation(stats[:, 6:7], stats[:, 0:1], AF.Ln,
                              bias=cons[:, 2:3], scale=1.0 / D)
                inst = sc.activation(stats[:, 5:6], stats[:, 6:7], AF.Exp,
                                     bias=cons[:, 0:1], scale=-0.5)
                sA.inc(inst)

        @blk.vector
        def _(v):
            for s in range(NSUB):
                v.wait_ge(sA.h, a0 + s + 1)
                if s >= 1:
                    v.wait_ge(sG.h, g0 + 32 * s)
                v.tensor_scalar(gt[:, :], xt[:, :], stats[:, 5:6], None, ALU.mult)
                inst = v.tensor_mul(stg[0][:, :], gt[:, :], wrep[:, :])
                sV.inc(inst)

    with nc.Block("p9t") as blk:
        @blk.sync
        def _(s):
            s.wait_ge(sG.h, sG.n)
            for ch in range(16):
                inst = s.dma_start(out=act_fm[:, ch * TS:(ch + 1) * TS],
                                   in_=h2_d[:, ch * 128:(ch + 1) * 128], transpose=True)
                sSy.inc(inst, 16)

    with nc.Block("p9r") as blk:
        t0, a0, v0 = sT.n, sA.n, sV.n

        @blk.tensor
        def _(t):
            t.wait_ge(sSy.h, sSy.n)
            t.wait_ge(sG.h, sG.n)
            t.wait_ge(sV.h, v0)
            for s in range(NSUB):
                if s >= 2:
                    t.wait_ge(sA.h, a0 + s - 1)
                for ch in range(16):
                    inst = t.matmul(ps[s % 2][:, 0:8],
                                    act_fm[:, ch * TS + s * 128:ch * TS + (s + 1) * 128],
                                    rwb[:, 8 * ch:8 * (ch + 1)],
                                    start=(ch == 0), stop=(ch == 15))
                sT.inc(inst)
                t.wait_ge(sV.h, v0 + 2 * s + 1)
                inst = t.transpose(ps[2][0:8, 0:128], rout[:, 0:8], idt[:, :])
                sT.inc(inst)

        @blk.scalar
        def _(sc):
            for s in range(NSUB):
                sc.wait_ge(sT.h, t0 + 2 * s + 1)
                inst = sc.activation(rout[:, 8:16], ps[s % 2][:, 0:8], AF.Exp,
                                     bias=cons[:, 0:1], scale=-1.0)
                sA.inc(inst)

        @blk.vector
        def _(v):
            for s in range(NSUB):
                v.wait_ge(sA.h, a0 + s + 1)
                v.tensor_scalar(rout[:, 8:16], rout[:, 8:16], 1.0, None, ALU.add)
                v.reciprocal(rout[:, 16:24], rout[:, 8:16])
                v.tensor_add(rout[:, 24:32], rout[:, 16:24], rb_rep[:, :])
                v.reduce_max(stats[:, 0:1], rout[:, 24:32], axis=AX)
                v.tensor_scalar(rout[:, 32:40], rout[:, 24:32], stats[:, 0:1],
                                None, ALU.is_equal)
                v.scalar_tensor_tensor(rout[:, 40:48], rout[:, 32:40], -1e30,
                                       rout[:, 24:32], ALU.mult, ALU.add)
                v.reduce_max(stats[:, 1:2], rout[:, 40:48], axis=AX)
                v.tensor_scalar(rout[:, 48:56], rout[:, 40:48], stats[:, 1:2],
                                None, ALU.is_equal)
                v.tensor_add(rout[:, 32:40], rout[:, 32:40], rout[:, 48:56])
                v.tensor_mul(rout[:, 56:64], rout[:, 16:24], rout[:, 32:40])
                v.reduce_sum(stats[:, 2:3], rout[:, 56:64], axis=AX)
                v.reciprocal(stats[:, 3:4], stats[:, 2:3])
                inst = v.tensor_scalar(rout[:, 0:8], rout[:, 56:64], stats[:, 3:4],
                                       None, ALU.mult)
                sV.inc(inst)
                v.wait_ge(sT.h, t0 + 2 * s + 2)
                dstc = ctr8[0] if s < 4 else ctr8[1]
                inst = v.tensor_copy(dstc[:, 128 * (s % 4):128 * (s % 4 + 1)],
                                     ps[2][0:8, 0:128])
                sV.inc(inst)

        @blk.gpsimd
        def _(g):
            g.wait_ge(sV.h, sV.n)
            sG.inc(g.dma_start(out=ctr_d[:, 0:512], in_=ctr8[0]), 16)
            sG.inc(g.dma_start(out=ctr_d[:, 512:1024], in_=ctr8[1]), 16)

    if upto <= 10:
        return _cut()

    with nc.Block("p10") as blk:
        t0, a0, v0, g0 = sT.n, sA.n, sV.n, sG.n
        soe_snap, spv_snap, sqk_snap = soe.n, spv.n, sqk.n
        GW, TW, VW, AW = {}, {}, {}, {}
        g_cnt, t_cnt, v_cnt, a_cnt = g0, t0, v0, a0
        for e in range(8):
            g_cnt += 32
            GW[(e, 'ctr')] = g_cnt
            for qu in range(4):
                g_cnt += 16 * 32
                GW[(e, qu)] = g_cnt
                if qu == 2:
                    g_cnt += 16 * 4
                    GW[(e, 'wdlo')] = g_cnt
            g_cnt += 16 * 4
            GW[(e, 'wd')] = g_cnt
            t_cnt += 2
            TW[(e, 'cbc')] = t_cnt
            v_cnt += 1
            VW[(e, 'cbc')] = v_cnt
            for qu in range(4):
                for pr in range(4):
                    t_cnt += 2
                    TW[(e, qu, pr)] = t_cnt
                    a_cnt += 1
                    AW[(e, qu, pr)] = a_cnt
                    v_cnt += 1
                    VW[(e, qu, pr)] = v_cnt
            for dt_i in range(32):
                t_cnt += 1
                TW[(e, 'd', dt_i)] = t_cnt
                v_cnt += 1
                VW[(e, 'd', dt_i)] = v_cnt

        def ld_quarter(g, e, qu):
            qb = (qu % 2) * 8192
            for ch in range(16):
                inst = g.dma_start(out=wstage[:, qb + ch * 256:qb + (ch + 1) * 256],
                                   in_=wg[e, ch * 128:(ch + 1) * 128,
                                          256 * qu:256 * (qu + 1)])
                sG.inc(inst, 16)
            for ch in range(16):
                inst = g.dma_start(
                    out=wstage[:, qb + 4096 + ch * 256:qb + 4096 + (ch + 1) * 256],
                    in_=wu[e, ch * 128:(ch + 1) * 128, 256 * qu:256 * (qu + 1)])
                sG.inc(inst, 16)

        def ld_wd(g, e, lo):
            for chi in (range(4) if lo else range(4, 8)):
                dst = wd_lo if chi < 4 else wd_hi
                co = (chi % 4) * 2048
                inst = g.dma_start(out=dst[:, co:co + 2048],
                                   in_=wd[e, chi * 128:(chi + 1) * 128, :])
                sG.inc(inst, 16)

        @blk.gpsimd
        def _(g):
            g.wait_ge(soe.h, soe_snap)
            g.wait_ge(spv.h, spv_snap)
            g.wait_ge(sqk.h, sqk_snap)
            g.wait_ge(sT.h, t0)
            g.wait_ge(sV.h, v0)
            g.wait_ge(sA.h, a0)
            for e in range(8):
                if e >= 1:
                    g.wait_ge(sT.h, TW[(e - 1, 'cbc')])
                inst = g.dma_start(out=lrow[0], in_=ctr_d[e, 0:512])
                sG.inc(inst, 16)
                inst = g.dma_start(out=lrow[1], in_=ctr_d[e, 512:1024])
                sG.inc(inst, 16)
                for qu in range(4):
                    if qu >= 2:
                        g.wait_ge(sT.h, TW[(e, qu - 2, 3)])
                    elif e >= 1:
                        g.wait_ge(sT.h, TW[(e - 1, qu + 2, 3)])
                    ld_quarter(g, e, qu)
                    if qu == 2:
                        ld_wd(g, e, True)
                ld_wd(g, e, False)

        @blk.tensor
        def _(t):
            t.wait_ge(sV.h, v0)
            for e in range(8):
                t.wait_ge(sG.h, GW[(e, 'ctr')])
                if e >= 1:
                    t.wait_ge(sV.h, VW[(e - 1, 'cbc')])
                t.matmul(ps[4][:, :], onesf[0:1, 0:128], lrow[0],
                         start=True, stop=True)
                inst = t.matmul(ps[5][:, :], onesf[0:1, 0:128], lrow[1],
                                start=True, stop=True)
                sT.inc(inst, 2)
                for qu in range(4):
                    t.wait_ge(sG.h, GW[(e, qu)])
                    qb = (qu % 2) * 8192
                    for pr in range(4):
                        ic2, th = pr // 2, pr % 2
                        bg, bu = (0, 1) if th == 0 else (2, 3)
                        if pr >= 2:
                            t.wait_ge(sV.h, VW[(e, qu, pr - 2)])
                        elif qu >= 1:
                            t.wait_ge(sV.h, VW[(e, qu - 1, 2 + pr)])
                        elif e >= 1:
                            t.wait_ge(sV.h, VW[(e - 1, 'd', 31)])
                        for ch in range(16):
                            inst = t.matmul(ps[bg][:, :],
                                            wstage[:, qb + ch * 256 + ic2 * 128:qb + ch * 256 + (ic2 + 1) * 128],
                                            act_fm[:, ch * TS + th * 512:ch * TS + (th + 1) * 512],
                                            start=(ch == 0), stop=(ch == 15))
                        sT.inc(inst)
                        for ch in range(16):
                            inst = t.matmul(ps[bu][:, :],
                                            wstage[:, qb + 4096 + ch * 256 + ic2 * 128:qb + 4096 + ch * 256 + (ic2 + 1) * 128],
                                            act_fm[:, ch * TS + th * 512:ch * TS + (th + 1) * 512],
                                            start=(ch == 0), stop=(ch == 15))
                        sT.inc(inst)
                t.wait_ge(sG.h, GW[(e, 'wd')])
                t.wait_ge(sV.h, VW[(e, 3, 3)])
                dt_i = 0
                for s in range(NSUB):
                    for ob in range(4):
                        bank = ps[dt_i % 2]
                        if dt_i >= 2:
                            t.wait_ge(sV.h, VW[(e, 'd', dt_i - 2)])
                        for chi in range(8):
                            wsrc = wd_lo if chi < 4 else wd_hi
                            co = (chi % 4) * 2048
                            inst = t.matmul(bank[:, :],
                                            actb[:, chi * 1024 + s * 128:chi * 1024 + (s + 1) * 128],
                                            wsrc[:, co + ob * 512:co + (ob + 1) * 512],
                                            start=(chi == 0), stop=(chi == 7))
                        sT.inc(inst)
                        dt_i += 1

        @blk.scalar
        def _(sc):
            for e in range(8):
                for qu in range(4):
                    for pr in range(4):
                        bg = 0 if pr % 2 == 0 else 2
                        sc.wait_ge(sT.h, TW[(e, qu, pr)] - 1)
                        if pr >= 1:
                            sc.wait_ge(sV.h, VW[(e, qu, pr - 1)])
                        elif qu >= 1:
                            sc.wait_ge(sV.h, VW[(e, qu - 1, 3)])
                        elif e >= 1:
                            sc.wait_ge(sV.h, VW[(e - 1, 3, 3)])
                        inst = sc.activation(sgb[:, :], ps[bg][:, :], AF.Silu,
                                             bias=cons[:, 0:1], scale=1.0)
                        sA.inc(inst)

        @blk.vector
        def _(v):
            for e in range(8):
                v.wait_ge(sT.h, TW[(e, 'cbc')])
                v.tensor_copy(cbc[:, 0:512], ps[4][:, :])
                inst = v.tensor_copy(cbc[:, 512:1024], ps[5][:, :])
                sV.inc(inst)
                for qu in range(4):
                    for pr in range(4):
                        ic2, th = pr // 2, pr % 2
                        bu = 1 if th == 0 else 3
                        gI = qu * 2 + ic2
                        v.wait_ge(sT.h, TW[(e, qu, pr)])
                        v.wait_ge(sA.h, AW[(e, qu, pr)])
                        v.tensor_mul(rt[:, :], ps[bu][:, :], cbc[:, 512 * th:512 * (th + 1)])
                        inst = v.tensor_mul(
                            actb[:, gI * 1024 + th * 512:gI * 1024 + (th + 1) * 512],
                            sgb[:, :], rt[:, :])
                        sV.inc(inst)
                dt_i = 0
                for s in range(NSUB):
                    for ob in range(4):
                        v.wait_ge(sT.h, TW[(e, 'd', dt_i)])
                        mslice = macc[:, s * 2048 + ob * 512:s * 2048 + (ob + 1) * 512]
                        if e == 0:
                            inst = v.tensor_copy(mslice, ps[dt_i % 2][:, :])
                        else:
                            inst = v.tensor_add(mslice, mslice, ps[dt_i % 2][:, :])
                        sV.inc(inst)
                        dt_i += 1

    if upto <= 11:
        return _cut()

    with nc.Block("p11") as blk:
        g0, a0, v0, o0 = sG.n, sA.n, sV.n, sO.n

        @blk.gpsimd
        def _(g):
            for s in range(NSUB):
                g.wait_ge(sV.h, v0 + 3 * s)
                inst = g.dma_start(out=xt[:, :], in_=x1_d[128 * s:(s + 1) * 128, :])
                sG.inc(inst, 16)
                inst = g.dma_start(out=gt[:, :], in_=x2_d[128 * s:(s + 1) * 128, :])
                sG.inc(inst, 16)
                g.wait_ge(sV.h, v0 + 3 * s + 3)
                inst = g.dma_start(out=y_out[128 * s:(s + 1) * 128, :], in_=ft[:, :])
                sO.inc(inst, 16)

        @blk.scalar
        def _(sc):
            for s in range(NSUB):
                sc.wait_ge(sV.h, v0 + 3 * s + 1)
                inst = sc.activation(xt[:, :], ft[:, :], AF.Square, bias=cons[:, 0:1],
                                     scale=1.0, accum_out=stats[:, 1:2])
                sA.inc(inst)
                sc.wait_ge(sV.h, v0 + 3 * s + 2)
                sc.activation(stats[:, 6:7], stats[:, 4:5], AF.Ln,
                              bias=cons[:, 1:2], scale=1.0)
                inst = sc.activation(stats[:, 5:6], stats[:, 6:7], AF.Exp,
                                     bias=cons[:, 0:1], scale=-0.5)
                sA.inc(inst)

        @blk.vector
        def _(v):
            for s in range(NSUB):
                v.wait_ge(sG.h, g0 + 32 * (s + 1))
                if s >= 1:
                    v.wait_ge(sO.h, o0 + 16 * s)
                inst = v.tensor_add(ft[:, :], xt[:, :], macc[:, 2048 * s:2048 * (s + 1)])
                sV.inc(inst)
                v.reduce_sum(stats[:, 0:1], ft[:, :], axis=AX)
                v.wait_ge(sA.h, a0 + 2 * s + 1)
                v.tensor_scalar(stats[:, 2:3], stats[:, 0:1], 1.0 / D, None, ALU.mult)
                v.tensor_scalar(stats[:, 3:4], stats[:, 1:2], 1.0 / D, None, ALU.mult)
                v.tensor_mul(stats[:, 4:5], stats[:, 2:3], stats[:, 2:3])
                inst = v.tensor_sub(stats[:, 4:5], stats[:, 3:4], stats[:, 4:5])
                sV.inc(inst)
                v.wait_ge(sA.h, a0 + 2 * s + 2)
                v.tensor_scalar(xt[:, :], ft[:, :], stats[:, 2:3], stats[:, 5:6],
                                ALU.subtract, ALU.mult)
                inst = v.tensor_add(ft[:, :], gt[:, :], xt[:, :])
                sV.inc(inst)

    if DEBUG_TAPS:
        with nc.Block("taps") as blk:
            @blk.gpsimd
            def _(g):
                g.wait_ge(sO.h, sO.n)
                for nm, src in [("tap_lnx", lnx_d), ("tap_qd", q_d), ("tap_agA", agA_in),
                                ("tap_agB", agB_in), ("tap_x2", x2_d), ("tap_h", h_d),
                                ("tap_agC", agC_in), ("tap_agD", agD_in),
                                ("tap_x1", x1_d), ("tap_h2", h2_d),
                                ("tap_ctr", ctr_d), ("tap_li2", li2_d)]:
                    inst = g.dma_start(out=taps[nm], in_=src)
                    sO.inc(inst, 16)

    with nc.Block("fin") as blk:
        @blk.gpsimd
        def _(g):
            g.wait_ge(sO.h, sO.n)
    return nc


_NC_CACHE = [None]


def kernel(**inputs):
    i = {k: np.ascontiguousarray(np.asarray(v, dtype=np.float32))
         for k, v in inputs.items()}
    x = i["L0_attn_x"].reshape(S_FULL, D)
    if _NC_CACHE[0] is None:
        _NC_CACHE[0] = build_program()
    nc = _NC_CACHE[0]

    shared = {
        "Wq": i["L0_attn_Wq"], "Wk": i["L0_attn_Wk"], "Wv": i["L0_attn_Wv"],
        "Wo": i["L0_attn_Wo"], "bq": i["L0_attn_bq"], "bk": i["L0_attn_bk"],
        "bv": i["L0_attn_bv"], "bo": i["L0_attn_bo"],
        "fwq": i["L0_ffn_wq"], "fwk": i["L0_ffn_wk"], "fwv": i["L0_ffn_wv"],
        "fwo": i["L0_ffn_wo"], "rw": i["L0_ffn_router_w"],
        "wg": i["L0_ffn_wg"], "wu": i["L0_ffn_wu"], "wd": i["L0_ffn_wd"],
        "ln1w_r": np.ascontiguousarray(np.tile(i["L0_ffn_ln1_w"][None, :], (128, 1))),
        "ln2w_r": np.ascontiguousarray(np.tile(i["L0_ffn_ln2_w"][None, :], (128, 1))),
        "qnw_r": np.ascontiguousarray(np.tile(i["L0_ffn_qnorm_w"][None, :], (128, 1))),
        "knw_r": np.ascontiguousarray(np.tile(i["L0_ffn_knorm_w"][None, :], (128, 1))),
        "rbias_r": np.ascontiguousarray(np.tile(i["L0_ffn_router_bias"][None, :], (128, 1))),
        "iidx_in": np.ascontiguousarray(
            np.tile(np.arange(32, dtype=np.float32)[None, :], (128, 1))),
    }
    in_maps = []
    for c in range(NCO):
        m = dict(shared)
        m["x_sl"] = np.ascontiguousarray(x[c * TS:(c + 1) * TS])
        m["pos_in"] = np.arange(c * TS, (c + 1) * TS, dtype=np.float32)
        in_maps.append(m)

    res = run_bass_kernel_spmd(nc, in_maps, core_ids=list(range(NCO)))
    y = np.concatenate([res.results[c]["y"] for c in range(NCO)], axis=0)
    return y.reshape(1, S_FULL, D)


# version 12
# speedup vs baseline: 1.1198x; 1.0167x over previous
import math
import numpy as np

import concourse.bass as bass
import concourse.mybir as mybir
from concourse.bass_utils import run_bass_kernel_spmd

F32 = mybir.dt.float32
BF16 = mybir.dt.bfloat16
I32 = mybir.dt.int32
AF = mybir.ActivationFunctionType
ALU = mybir.AluOpType
AX = mybir.AxisListType.X
ET = mybir.EngineType

NCO = 8
S_FULL = 8192
TS = 1024
D = 2048
NSUB = 8
SA = 1.0 / math.sqrt(128.0)
TWO_PI = 2.0 * math.pi
EPS_LN = 1e-5
EPS_RMS = 1e-6
DEBUG_TAPS = False
RG8 = [list(range(NCO))]


_TRAILER = {}


class Sem:

    def __init__(self, nc, name):
        self.h = nc.alloc_semaphore(name)
        self.n = 0

    def inc(self, inst, k=1):
        eng = str(inst.ins.engine) if hasattr(inst, "ins") else ""
        tr = _TRAILER.get(eng)
        if tr is not None:
            tr(self.h, k)
        else:
            inst.then_inc(self.h, k)
        self.n += k




_GUARD_OPS = {
    "vector": ["memset", "memzero", "tensor_copy", "tensor_tensor", "tensor_add",
               "tensor_sub", "tensor_mul", "tensor_max", "tensor_scalar",
               "tensor_single_scalar", "reduce_max", "reduce_sum", "reciprocal",
               "scalar_tensor_tensor"],
    "scalar": ["activation", "copy", "mul", "add", "memzero"],
}


def _ap_region(ap):
    try:
        name = ap.tensor.name
        off = int(ap.offset)
        span = 1
        for step, count in ap.ap:
            span += (int(count) - 1) * abs(int(step))
        return (name, off, off + span)
    except Exception:
        return ("?", 0, 1 << 60)


def _install_engine_guards(nc):
    import collections

    for engname in ("vector", "scalar"):
        eng = getattr(nc, engname)
        sem = nc.alloc_semaphore(f"grd_{engname}")
        state = {"cnt": 0, "recent": collections.deque(maxlen=8)}

        def mk(opname, eng=eng, sem=sem, state=state):
            orig = getattr(type(eng), opname)

            def wrapped(*args, **kwargs):
                aps = [a for a in list(args) + list(kwargs.values())
                       if isinstance(a, bass.AP)]
                if not aps:
                    inst = orig(eng, *args, **kwargs)
                    state["cnt"] += 1
                    inst.then_inc(sem, 1)
                    state["recent"].append((state["cnt"], [], []))
                    return inst
                regs = [_ap_region(a) for a in aps]
                writes = [regs[0]]
                acc = kwargs.get("accum_out")
                if isinstance(acc, bass.AP):
                    writes.append(_ap_region(acc))
                reads = regs[1:]

                def ovl(a, b):
                    return a[0] == b[0] and a[1] < b[2] and b[1] < a[2]

                need = 0
                for idx, w_prev, r_prev in state["recent"]:
                    hit = False
                    for r in regs:
                        if any(ovl(r, w) for w in w_prev):
                            hit = True
                    for w in writes:
                        if any(ovl(w, r) for r in r_prev):
                            hit = True
                    if hit:
                        need = max(need, idx)
                if need:
                    eng.wait_ge(sem, need)
                inst = orig(eng, *args, **kwargs)
                state["cnt"] += 1
                inst.then_inc(sem, 1)
                state["recent"].append((state["cnt"], writes, reads))
                return inst

            return wrapped

        for opname in _GUARD_OPS[engname]:
            setattr(eng, opname, mk(opname))

def build_program(race_detect=False, upto=99):
    nc = bass.Bass("TRN2", target_bir_lowering=False,
                   detect_race_conditions=race_detect)
    _install_engine_guards(nc)

    def di(name, shape):
        return nc.dram_tensor(name, shape, F32, kind="ExternalInput").ap()

    x_sl = di("x_sl", [TS, D])
    Wq, Wk, Wv, Wo = di("Wq", [D, D]), di("Wk", [D, D]), di("Wv", [D, D]), di("Wo", [D, D])
    bq, bk, bv, bo = di("bq", [D]), di("bk", [D]), di("bv", [D]), di("bo", [D])
    fwq, fwk, fwv, fwo = di("fwq", [D, D]), di("fwk", [D, 512]), di("fwv", [D, 512]), di("fwo", [D, D])
    rw = di("rw", [D, 8])
    wg, wu, wd = di("wg", [8, D, 1024]), di("wu", [8, D, 1024]), di("wd", [8, 1024, D])
    ln1w_r = di("ln1w_r", [128, D])
    ln2w_r = di("ln2w_r", [128, D])
    qnw_r = di("qnw_r", [128, D])
    knw_r = di("knw_r", [128, 512])
    rbias_r = di("rbias_r", [128, 8])
    pos_in = di("pos_in", [TS])
    iidx_in = di("iidx_in", [128, 32])

    y_out = nc.dram_tensor("y", [TS, D], F32, kind="ExternalOutput").ap()

    def dint(name, shape, dt=BF16):
        return nc.dram_tensor(name, shape, dt, kind="Internal").ap()

    agA_in = dint("agA_in", [TS, 2 * D])
    agA_out = nc.dram_tensor("agA_out", [NCO, TS, 2 * D], BF16, kind="Internal", addr_space="Shared").ap()
    agB_in = dint("agB_in", [256, S_FULL])
    agB_out = nc.dram_tensor("agB_out", [NCO, 256, S_FULL], BF16, kind="Internal", addr_space="Shared").ap()
    agCq_in = dint("agCq_in", [TS, 2048])
    agCq_out = nc.dram_tensor("agCq_out", [NCO, TS, 2048], BF16, kind="Internal", addr_space="Shared").ap()
    agCkv_in = dint("agCkv_in", [TS, 1024])
    agCkv_out = nc.dram_tensor("agCkv_out", [NCO, TS, 1024], BF16, kind="Internal", addr_space="Shared").ap()
    agD_in = dint("agD_in", [256, S_FULL])
    agD_out = nc.dram_tensor("agD_out", [NCO, 256, S_FULL], BF16, kind="Internal", addr_space="Shared").ap()
    lnx_d = dint("lnx_d", [TS, D])
    q_d = dint("q_d", [TS, D])
    qraw_d = dint("qraw_d", [TS, D])
    h_d = dint("h_d", [TS, D])
    h2_d = dint("h2_d", [TS, D])
    x2_d = dint("x2_d", [TS, D], F32)
    li2_d = dint("li2_d", [TS, D], F32)
    x1_d = dint("x1_d", [TS, D], F32)
    ctr_d = dint("ctr_d", [8, TS], F32)

    taps = {}
    if DEBUG_TAPS:
        for nm, shape, dt in [("tap_lnx", [TS, D], BF16), ("tap_qd", [TS, D], BF16),
                              ("tap_agA", [TS, 2 * D], BF16), ("tap_agB", [256, S_FULL], BF16),
                              ("tap_x2", [TS, D], F32), ("tap_h", [TS, D], BF16),
                              ("tap_agC", [TS, 2048], BF16), ("tap_agD", [256, S_FULL], BF16),
                              ("tap_x1", [TS, D], F32), ("tap_h2", [TS, D], BF16),
                              ("tap_ctr", [8, TS], F32), ("tap_li2", [TS, D], F32)]:
            taps[nm] = nc.dram_tensor(nm, shape, dt, kind="ExternalOutput").ap()

    def sbt(name, cols, dt=BF16, parts=128):
        return nc.alloc_sbuf_tensor(name, [parts, cols], dt)

    m_diag = sbt("m_diag", 2048)
    ones_b = sbt("ones_b", 128)
    onesf = sbt("onesf", 128, F32, parts=1)
    act_fm = sbt("act_fm", 16 * TS)
    wstage = sbt("wstage", 16 * TS)
    kvz = sbt("kvz", 32768)
    sha = sbt("sha", 8192)
    bias_row = sbt("bias_row", D, BF16, parts=1)
    idt = sbt("idt", 128, F32)
    rwb = sbt("rwb", 128)

    xt = sbt("xt", D, F32)
    ft = sbt("ft", D, F32)
    gt = sbt("gt", D, F32)
    rt = sbt("rt", 512, F32)
    wrep = sbt("wrep", D, BF16)
    rb_rep = sbt("rb_rep", 8, F32)
    linv = sbt("linv", 512, F32)
    csc = sbt("csc", 256, F32)
    ssc = sbt("ssc", 256, F32)
    crep = sbt("crep", 512, F32)
    srep = sbt("srep", 512, F32)
    stats = sbt("stats", 16, F32)
    ssqa = sbt("ssqa", NSUB, F32)
    cons = sbt("cons", 8, F32)
    rout = sbt("rout", 64, F32)
    poscol = sbt("poscol", NSUB, F32)
    iidx = sbt("iidx", 32, F32)
    cbc = sbt("cbc", TS, F32)
    sgb = sbt("sgb", 512, BF16)
    ki = sbt("ki", 32, I32)
    lrow = [crep[0:1, 0:512], srep[0:1, 0:512]]
    ctr8 = [crep[0:8, :], srep[0:8, :]]

    qslab = [sha[:, 0:512], sha[:, 512:1024]]
    pt = [sha[:, 1024:1536], sha[:, 1536:2048], sha[:, 3072:3584], sha[:, 3584:4096]]
    qsq = sha[:, 2048:2560]
    outsb = sha[:, 2560:3072]
    stg = [sha[:, 4096:6144], sha[:, 6144:8192]]
    lhsb = [sha[:, 0:2048], sha[:, 2048:4096]]
    wd_lo = sha[:, 0:8192]
    kslab = [kvz[:, 0:8192], kvz[:, 16384:24576]]
    vslab = [kvz[:, 8192:16384], kvz[:, 24576:32768]]
    macc = kvz[:, 0:16384]
    actb = kvz[:, 16384:24576]
    wd_hi = kvz[:, 24576:32768]

    ps = [nc.alloc_psum_tensor(f"ps{i}", [128, 512], F32) for i in range(8)]

    _raw_vcopy = type(nc.vector).tensor_copy
    _raw_scopy = type(nc.scalar).copy
    _TRAILER.clear()
    _TRAILER["EngineType.DVE"] = (
        lambda h, k: _raw_vcopy(nc.vector, cons[:, 7:8], cons[:, 0:1]).then_inc(h, k))
    _TRAILER["EngineType.Activation"] = (
        lambda h, k: _raw_scopy(nc.scalar, cons[:, 7:8], cons[:, 0:1]).then_inc(h, k))

    sG = Sem(nc, "sG")
    sSy = Sem(nc, "sSy")
    sT = Sem(nc, "sT")
    sA = Sem(nc, "sA")
    sV = Sem(nc, "sV")
    sCC = Sem(nc, "sCC")
    sO = Sem(nc, "sO")
    sqk = Sem(nc, "sqk")
    sxp = Sem(nc, "sxp")
    smk = Sem(nc, "smk")
    spv = Sem(nc, "spv")
    sbsq = Sem(nc, "sbsq")
    sbcs = Sem(nc, "sbcs")
    sm2 = Sem(nc, "sm2")
    sbe = Sem(nc, "sbe")
    snb = Sem(nc, "snb")
    soe = Sem(nc, "soe")
    sGRD = Sem(nc, "sGRD")


    def _cut():
        with nc.Block("fin") as blk:
            @blk.gpsimd
            def _(g):
                g.wait_ge(sO.h, sO.n)
        return nc

    with nc.Block("p0") as blk:
        @blk.gpsimd
        def _(g):
            g.memset(ones_b[:, :], 1.0)
            g.memset(onesf[:, :], 1.0)
            g.memset(cons[:, 0:1], 0.0)
            g.memset(cons[:, 1:2], EPS_LN)
            g.memset(cons[:, 2:3], EPS_RMS)
            g.memset(cons[:, 3:4], math.log(SA))
            inst = g.memset(m_diag[:, :], 1.0)
            sG.inc(inst, 16) if False else inst.then_inc(sGRD.h, 1)
            sGRD.n += 1
            g.wait_ge(sGRD.h, sGRD.n)
            for a in range(4):
                g.affine_select(out=m_diag[:, 512 * a:512 * (a + 1)],
                                in_=m_diag[:, 512 * a:512 * (a + 1)],
                                compare_op=ALU.is_ge, fill=0.0, base=-128 * a,
                                pattern=[[1, 512]], channel_multiplier=-1)
            inst = g.memset(idt[:, :], 0.0)
            inst.then_inc(sGRD.h, 1)
            sGRD.n += 1
            g.wait_ge(sGRD.h, sGRD.n)
            inst = g.affine_select(out=idt[:, :], in_=idt[:, :],
                                   compare_op=ALU.not_equal, fill=1.0, base=0,
                                   pattern=[[-1, 128]], channel_multiplier=1)
            sV.inc(inst)
            for s in range(NSUB):
                inst = g.dma_start(out=poscol[:, s:s + 1],
                                   in_=pos_in[s * 128:(s + 1) * 128])
                sG.inc(inst, 16)
            inst = g.dma_start(out=iidx[:, :], in_=iidx_in)
            sG.inc(inst, 16)
            inst = g.dma_start(out=rb_rep[:, :], in_=rbias_r)
            sG.inc(inst, 16)

        @blk.scalar
        def _(sc):
            sc.wait_ge(sG.h, sG.n)
            sc.wait_ge(sV.h, sV.n)
            inst = sc.activation(ft[:, 0:32], iidx[:, :], AF.Exp,
                                 bias=cons[:, 0:1], scale=-math.log(1e6) / 32.0)
            sA.inc(inst)

        @blk.vector
        def _(v):
            v.wait_ge(sA.h, sA.n)
            for s in range(NSUB):
                v.tensor_scalar(gt[:, 0:32], ft[:, 0:32], poscol[:, s:s + 1], None, ALU.mult)
                for dst, shift in ((csc, 0.5 * math.pi), (ssc, 0.0)):
                    v.tensor_scalar(gt[:, 32:64], gt[:, 0:32], shift, 1.0 / TWO_PI,
                                    ALU.add, ALU.mult)
                    v.tensor_copy(ki[:, 0:32], gt[:, 32:64])
                    v.tensor_copy(gt[:, 64:96], ki[:, 0:32])
                    v.tensor_scalar(gt[:, 96:128], gt[:, 64:96], -TWO_PI, shift,
                                    ALU.mult, ALU.add)
                    dcol = dst[:, 32 * s:32 * (s + 1)]
                    v.tensor_add(dcol, gt[:, 0:32], gt[:, 96:128])
                    v.tensor_scalar(gt[:, 32:64], dcol, math.pi, None, ALU.is_gt)
                    v.scalar_tensor_tensor(dcol, gt[:, 32:64], -TWO_PI, dcol,
                                           ALU.mult, ALU.add)
                    v.tensor_scalar(gt[:, 32:64], dcol, -math.pi, None, ALU.is_lt)
                    inst = v.scalar_tensor_tensor(dcol, gt[:, 32:64], TWO_PI, dcol,
                                                  ALU.mult, ALU.add)
            sV.inc(inst)

        @blk.scalar
        def _(sc):
            sc.wait_ge(sV.h, sV.n)
            sc.activation(csc[:, :], csc[:, :], AF.Sin, bias=cons[:, 0:1], scale=1.0)
            inst = sc.activation(ssc[:, :], ssc[:, :], AF.Sin, bias=cons[:, 0:1], scale=1.0)
            sA.inc(inst)

    if upto <= 1:
        return _cut()

    def emit_ln_phase(blk, load_fn, store_fn):
        g0, a0, v0 = sG.n, sA.n, sV.n

        @blk.gpsimd
        def _(g):
            for s in range(NSUB):
                g.wait_ge(sV.h, v0 + 2 * s)
                load_fn(g, s)
                g.wait_ge(sV.h, v0 + 2 * s + 2)
                store_fn(g, s)

        @blk.scalar
        def _(sc):
            for s in range(NSUB):
                sc.wait_ge(sG.h, g0 + 32 * s + 16)
                inst = sc.activation(ft[:, :], xt[:, :], AF.Square, bias=cons[:, 0:1],
                                     scale=1.0, accum_out=stats[:, 1:2])
                sA.inc(inst)
                sc.wait_ge(sV.h, v0 + 2 * s + 1)
                sc.activation(stats[:, 6:7], stats[:, 4:5], AF.Ln,
                              bias=cons[:, 1:2], scale=1.0)
                inst = sc.activation(stats[:, 5:6], stats[:, 6:7], AF.Exp,
                                     bias=cons[:, 0:1], scale=-0.5)
                sA.inc(inst)

        @blk.vector
        def _(v):
            for s in range(NSUB):
                v.wait_ge(sG.h, g0 + 32 * s + 16)
                v.reduce_sum(stats[:, 0:1], xt[:, :], axis=AX)
                v.wait_ge(sA.h, a0 + 2 * s + 1)
                v.tensor_scalar(stats[:, 2:3], stats[:, 0:1], 1.0 / D, None, ALU.mult)
                v.tensor_scalar(stats[:, 3:4], stats[:, 1:2], 1.0 / D, None, ALU.mult)
                v.tensor_mul(stats[:, 4:5], stats[:, 2:3], stats[:, 2:3])
                inst = v.tensor_sub(stats[:, 4:5], stats[:, 3:4], stats[:, 4:5])
                sV.inc(inst)
                v.wait_ge(sA.h, a0 + 2 * s + 2)
                v.wait_ge(sG.h, g0 + 32 * s)
                inst = v.tensor_scalar(stg[0][:, :], xt[:, :], stats[:, 2:3],
                                       stats[:, 5:6], ALU.subtract, ALU.mult)
                sV.inc(inst)

    with nc.Block("p1") as blk:
        def p1_load(g, s):
            sG.inc(g.dma_start(out=xt[:, :], in_=x_sl[s * 128:(s + 1) * 128, :]), 16)

        def p1_store(g, s):
            sG.inc(g.dma_start(out=lnx_d[s * 128:(s + 1) * 128, :], in_=stg[0][:, :]), 16)
        emit_ln_phase(blk, p1_load, p1_store)

    with nc.Block("p1t") as blk:
        @blk.sync
        def _(s):
            s.wait_ge(sG.h, sG.n)
            for ch in range(16):
                inst = s.dma_start(out=act_fm[:, ch * TS:(ch + 1) * TS],
                                   in_=lnx_d[:, ch * 128:(ch + 1) * 128], transpose=True)
                sSy.inc(inst, 16)

    if upto <= 2:
        return _cut()

    stgq = [sha[:, 4096 + i * 512:4096 + (i + 1) * 512] for i in range(8)]

    def emit_stream_projs(specs, accum_ssq=False, name="proj", coll=None):
        with nc.Block(name) as blk:
            t0, a0, v0, g0 = sT.n, sA.n, sV.n, sG.n
            sy0, soe_snap = sSy.n, soe.n
            nq = 4 * len(specs)
            WL, TQ, ST = {}, {}, {}
            g_cnt = g0
            for kq in range(nq):
                g_cnt += 16 * (17 if specs[kq // 4][1] is not None else 16)
                WL[kq] = g_cnt
                TQ[kq] = t0 + 8 * (kq + 1)
            for it in range(8 * nq):
                ST[it] = sy0 + 16 * (it + 1)

            @blk.gpsimd
            def _(g):
                g.wait_ge(sT.h, t0)
                g.wait_ge(soe.h, soe_snap)
                for kq in range(nq):
                    w_ap, b_ap, _ = specs[kq // 4]
                    qu = kq % 4
                    if kq >= 2:
                        g.wait_ge(sT.h, TQ[kq - 2])
                    qb = (kq % 2) * 8192
                    for ch in range(16):
                        inst = g.dma_start(
                            out=wstage[:, qb + ch * 512:qb + (ch + 1) * 512],
                            in_=w_ap[ch * 128:(ch + 1) * 128, qu * 512:(qu + 1) * 512])
                        sG.inc(inst, 16)
                    if b_ap is not None:
                        inst = g.dma_start(out=bias_row[0:1, qu * 512:(qu + 1) * 512],
                                           in_=b_ap[qu * 512:(qu + 1) * 512])
                        sG.inc(inst, 16)
                if coll is not None:
                    need, fn = coll
                    g.wait_ge(sSy.h, sy0 + 16 * need)
                    fn(g)

            @blk.tensor
            def _(t):
                t.wait_ge(sSy.h, sy0)
                t.wait_ge(sV.h, v0)
                t.wait_ge(soe.h, soe_snap)
                it = 0
                for kq in range(nq):
                    b_ap = specs[kq // 4][1]
                    qu = kq % 4
                    t.wait_ge(sG.h, WL[kq])
                    qb = (kq % 2) * 8192
                    for s in range(NSUB):
                        bank = ps[it % 2]
                        if it >= 2:
                            t.wait_ge(sV.h, v0 + it - 1)
                        for ch in range(16):
                            inst = t.matmul(bank[:, :],
                                            act_fm[:, ch * TS + s * 128:ch * TS + (s + 1) * 128],
                                            wstage[:, qb + ch * 512:qb + (ch + 1) * 512],
                                            start=(ch == 0),
                                            stop=(b_ap is None and ch == 15))
                        if b_ap is not None:
                            inst = t.matmul(bank[:, :], ones_b[0:1, 0:128],
                                            bias_row[0:1, qu * 512:(qu + 1) * 512],
                                            start=False, stop=True)
                        sT.inc(inst)
                        it += 1

            if accum_ssq:
                @blk.scalar
                def _(sc):
                    for it in range(8 * nq):
                        sc.wait_ge(sT.h, t0 + it + 1)
                        if it >= 1:
                            sc.wait_ge(sV.h, v0 + it)
                        inst = sc.activation(qsq[:, :], ps[it % 2][:, :], AF.Square,
                                             bias=cons[:, 0:1], scale=1.0,
                                             accum_out=stats[:, 8:9])
                        sA.inc(inst)

            @blk.vector
            def _(v):
                for it in range(8 * nq):
                    kq, s = it // 8, it % 8
                    if accum_ssq:
                        v.wait_ge(sA.h, a0 + it + 1)
                        if kq == 0:
                            v.tensor_copy(ssqa[:, s:s + 1], stats[:, 8:9])
                        else:
                            v.tensor_add(ssqa[:, s:s + 1], ssqa[:, s:s + 1],
                                         stats[:, 8:9])
                    else:
                        v.wait_ge(sT.h, t0 + it + 1)
                    if it >= 8:
                        v.wait_ge(sSy.h, ST[it - 8])
                    inst = v.tensor_copy(stgq[it % 8][:, :], ps[it % 2][:, :])
                    sV.inc(inst)

            @blk.sync
            def _(sy):
                for it in range(8 * nq):
                    kq, s = it // 8, it % 8
                    dst_fn = specs[kq // 4][2]
                    sy.wait_ge(sV.h, v0 + it + 1)
                    dst_fn(sy, s, kq % 4, it % 8)

    def mk_dst(dst_d, col_off):
        def f(eng, s, qu, slot):
            inst = eng.dma_start(
                out=dst_d[s * 128:(s + 1) * 128,
                          col_off + qu * 512:col_off + (qu + 1) * 512],
                in_=stgq[slot][:, :])
            sSy.inc(inst, 16)
        return f

    def _agA_launch(g):
        inst = g.collective_compute("AllGather", ALU.bypass, replica_groups=RG8,
                                    ins=[agA_in], outs=[agA_out])
        sCC.inc(inst)

    emit_stream_projs([(Wk, bk, mk_dst(agA_in, 0)),
                       (Wv, bv, mk_dst(agA_in, D)),
                       (Wq, bq, mk_dst(q_d, 0))],
                      name="p2kvq", coll=(64, _agA_launch))

    if upto <= 3:
        return _cut()

    def emit_attention(tag, kv_body_fn, q_in_switch, q_tr_fn, out_ap):
        slabs = [(hb, sl) for hb in range(2) for sl in range(16)]
        M_of = {(hb, sl): 4 * (sl + 1) for (hb, sl) in slabs}

        QK, EX, MK, PV = {}, {}, {}, {}
        SQk, SQq, CSk, CSq = {}, {}, {}, {}
        KP, M2, BE, NB, OE, QTR, OSTORE = {}, {}, {}, {}, {}, {}, {}
        cqk, cex, cmk, cpv = sqk.n, sxp.n, smk.n, spv.n
        csq, ccs, ckp = sbsq.n, sbcs.n, sV.n
        cm2, cbe, cnb, coe = sm2.n, sbe.n, snb.n, soe.n
        nk_tr = 16 * 16
        qtr_base = sSy.n + nk_tr
        cg_store = sG.n + 16 * 16
        for idx, (hb, sl) in enumerate(slabs):
            M = M_of[(hb, sl)]
            if sl == 0:
                for i in range(16):
                    csq += 1; SQk[(hb, i)] = csq
                    ccs += 1; CSk[(hb, i)] = ccs
                    ckp += 1; KP[(hb, i)] = ckp
            csq += 1; SQq[idx] = csq
            ccs += 1; CSq[idx] = ccs
            cm2 += 1; M2[idx] = cm2
            cbe += 1; BE[idx] = cbe
            cnb += 1; NB[idx] = cnb
            QTR[idx] = qtr_base + 16 * (idx + 1)
            for m in range(M):
                cqk += 1; QK[(idx, m)] = cqk
                cex += 1; EX[(idx, m)] = cex
                cpv += 1; PV[(idx, m)] = cpv
                if m >= 4 * sl:
                    cmk += 1; MK[(idx, m)] = cmk
            coe += 1; OE[idx] = coe
            cg_store += 16
            OSTORE[idx] = cg_store
        kv_ready_g = sG.n + 16 * 16
        khead_sy = {0: sSy.n + 16 * 8, 1: sSy.n + 16 * 16}
        v_snap = sV.n

        rg = nc.gpsimd.compute_val(nc.partition_id([ET.Pool]))
        rsy = nc.sync.compute_val(nc.partition_id([ET.SP]))
        qd_snap = sG.n
        for i in nc.Switch(engines=[nc.gpsimd, nc.sync], index=[rg, rsy], n=NCO):
            nc.gpsimd.wait_ge(sCC.h, sCC.n)
            nc.sync.wait_ge(sCC.h, sCC.n)
            for hb in range(2):
                kv_body_fn(i, hb)
            if q_in_switch:
                for idx, (hb, sl) in enumerate(slabs):
                    if idx >= 2:
                        nc.sync.wait_ge(sqk.h, QK[(idx - 1, 0)] - 1)
                    nc.sync.dma_start(out=qslab[idx % 2][:, :],
                                      in_=q_tr_fn(i, hb, sl),
                                      transpose=True).then_inc(sSy.h, 16)
        sSy.n += nk_tr + (16 * 32 if q_in_switch else 0)
        sG.n += 16 * 16

        with nc.Block(f"{tag}_att") as blk:
            if not q_in_switch:
                @blk.sync
                def _(s):
                    s.wait_ge(sG.h, qd_snap)
                    for idx, (hb, sl) in enumerate(slabs):
                        if idx >= 2:
                            s.wait_ge(sqk.h, QK[(idx - 1, 0)] - 1)
                        inst = s.dma_start(out=qslab[idx % 2][:, :],
                                           in_=q_tr_fn(None, hb, sl), transpose=True)
                        sSy.inc(inst, 16)

            @blk.tensor
            def _(t):
                t.wait_ge(sG.h, kv_ready_g)
                t.wait_ge(sV.h, v_snap)
                for idx, (hb, sl) in enumerate(slabs):
                    M = M_of[(hb, sl)]
                    if sl == 0:
                        for i in range(16):
                            t.wait_ge(sbsq.h, SQk[(hb, i)])
                            if i >= 1:
                                t.wait_ge(sV.h, KP[(hb, i - 1)])
                            elif idx > 0:
                                t.wait_ge(soe.h, OE[idx - 1])
                            inst = t.matmul(ps[6][:, :], ones_b[:, 0:128], qsq[:, :],
                                            start=True, stop=True)
                            sbcs.inc(inst)
                    t.wait_ge(sbsq.h, SQq[idx])
                    if sl == 0:
                        t.wait_ge(sV.h, KP[(hb, 15)])
                    if idx > 0:
                        t.wait_ge(soe.h, OE[idx - 1])
                    inst = t.matmul(ps[6][:, :], ones_b[:, 0:128], qsq[:, :],
                                    start=True, stop=True)
                    sbcs.inc(inst)

                    def emit_pv(mp, M=M, sl=sl, idx=idx, hb=hb, t=t):
                        apv = mp - 4 * sl
                        jp = 128 * apv if apv > 0 else 0
                        t.wait_ge(sxp.h, EX[(idx, mp)])
                        if apv >= 0:
                            t.wait_ge(smk.h, MK[(idx, mp)])
                        t.matmul(ps[4][:, jp:512],
                                 vslab[hb][:, 128 * mp:128 * (mp + 1)],
                                 pt[mp % 4][:, jp:512],
                                 start=(mp == 0), stop=(mp == M - 1))
                        inst = t.matmul(ps[5][:, jp:512], ones_b[:, 0:128],
                                        pt[mp % 4][:, jp:512],
                                        start=(mp == 0), stop=(mp == M - 1))
                        spv.inc(inst)

                    for m in range(M):
                        a = m - 4 * sl
                        j0 = 128 * a if a > 0 else 0
                        if m >= 4:
                            t.wait_ge(sxp.h, EX[(idx, m - 4)])
                        inst = t.matmul(ps[m % 4][:, j0:512],
                                        kslab[hb][:, 128 * m:128 * (m + 1)],
                                        qslab[idx % 2][:, j0:512], start=True, stop=True)
                        sqk.inc(inst)
                        if m >= 2:
                            emit_pv(m - 2)
                    emit_pv(M - 2)
                    emit_pv(M - 1)

            @blk.scalar
            def _(sc):
                for idx, (hb, sl) in enumerate(slabs):
                    M = M_of[(hb, sl)]
                    if sl == 0:
                        sc.wait_ge(sSy.h, khead_sy[hb])
                        for i in range(16):
                            if i >= 1:
                                sc.wait_ge(sbcs.h, CSk[(hb, i - 1)])
                            inst = sc.activation(qsq[:, :],
                                                 kslab[hb][:, 512 * i:512 * (i + 1)],
                                                 AF.Square, bias=cons[:, 0:1], scale=1.0)
                            sbsq.inc(inst)
                    sc.wait_ge(sSy.h, QTR[idx])
                    sc.wait_ge(sbcs.h, CSk[(hb, 15)] if sl == 0 else CSq[idx] - 1)
                    inst = sc.activation(qsq[:, :], qslab[idx % 2][:, :], AF.Square,
                                         bias=cons[:, 0:1], scale=1.0)
                    sbsq.inc(inst)
                    sc.wait_ge(sm2.h, M2[idx])
                    sc.activation(stats[:, 9:10], stats[:, 8:9], AF.Ln,
                                  bias=cons[:, 0:1], scale=1.0)
                    inst = sc.activation(stats[:, 10:11], stats[:, 9:10], AF.Exp,
                                         bias=cons[:, 3:4], scale=0.5)
                    sbe.inc(inst)
                    for m in range(M):
                        a = m - 4 * sl
                        j0 = 128 * a if a > 0 else 0
                        sc.wait_ge(sqk.h, QK[(idx, m)])
                        sc.wait_ge(snb.h, NB[idx])
                        if m >= 4:
                            sc.wait_ge(spv.h, PV[(idx, m - 4)])
                        elif idx > 0:
                            sc.wait_ge(spv.h, PV[(idx - 1, M_of[slabs[idx - 1]] - 1)])
                        inst = sc.activation(pt[m % 4][:, j0:512], ps[m % 4][:, j0:512],
                                             AF.Exp, bias=stats[:, 11:12], scale=SA)
                        sxp.inc(inst)

            @blk.vector
            def _(v):
                for idx, (hb, sl) in enumerate(slabs):
                    M = M_of[(hb, sl)]
                    if sl == 0:
                        for i in range(16):
                            v.wait_ge(sbcs.h, CSk[(hb, i)])
                            if i == 0:
                                inst = v.reduce_max(stats[:, 7:8], ps[6][:, :], axis=AX)
                            else:
                                v.reduce_max(stats[:, 12:13], ps[6][:, :], axis=AX)
                                inst = v.tensor_max(stats[:, 7:8], stats[:, 7:8],
                                                    stats[:, 12:13])
                            sV.inc(inst)
                    v.wait_ge(sbcs.h, CSq[idx])
                    v.reduce_max(stats[:, 12:13], ps[6][:, :], axis=AX)
                    inst = v.tensor_mul(stats[:, 8:9], stats[:, 12:13], stats[:, 7:8])
                    sm2.inc(inst)
                    v.wait_ge(sbe.h, BE[idx])
                    inst = v.tensor_scalar(stats[:, 11:12], stats[:, 10:11], -1.0,
                                           None, ALU.mult)
                    snb.inc(inst)
                    for m in range(M):
                        a = m - 4 * sl
                        if a >= 0:
                            j0 = 128 * a if a > 0 else 0
                            v.wait_ge(sxp.h, EX[(idx, m)])
                            inst = v.tensor_mul(pt[m % 4][:, j0:512],
                                                pt[m % 4][:, j0:512],
                                                m_diag[:, 512 * a + j0:512 * a + 512])
                            smk.inc(inst)
                    v.wait_ge(spv.h, PV[(idx, M - 1)])
                    if idx >= 1:
                        v.wait_ge(sG.h, OSTORE[idx - 1])
                    v.reciprocal(linv[:, :], ps[5][:, :])
                    inst = v.tensor_mul(outsb[:, :], ps[4][:, :], linv[:, :])
                    soe.inc(inst)

            @blk.gpsimd
            def _(g):
                for idx, (hb, sl) in enumerate(slabs):
                    g.wait_ge(soe.h, OE[idx])
                    inst = g.dma_start(
                        out=out_ap[128 * hb:128 * (hb + 1), 512 * sl:512 * (sl + 1)],
                        in_=outsb[:, :])
                    sG.inc(inst, 16)

    def kv1_body(i, hb):
        h = 2 * i + hb
        for r in range(NCO):
            nc.sync.dma_start(out=kslab[hb][:, 1024 * r:1024 * (r + 1)],
                              in_=agA_out[r, :, 128 * h:128 * (h + 1)],
                              transpose=True).then_inc(sSy.h, 16)
        for r in range(NCO):
            src3 = agA_out[r, :, D + 128 * h:D + 128 * (h + 1)].rearrange(
                "(mm p) d -> p mm d", p=128)
            dst3 = vslab[hb][:, 1024 * r:1024 * (r + 1)].rearrange(
                "p (mm d) -> p mm d", d=128)
            nc.gpsimd.dma_start(out=dst3, in_=src3).then_inc(sG.h, 16)

    q_flat = q_d.rearrange("a (b c) -> (a b) c", c=128)

    def q1_tr(i, hb, sl):
        row0 = 8192 * hb + 512 * sl
        return q_flat[row0:row0 + 512, :]

    emit_attention("b1", kv1_body, False, q1_tr, agB_in)

    if upto <= 4:
        return _cut()

    with nc.Block("p5ag") as blk:
        @blk.gpsimd
        def _(g):
            g.wait_ge(sG.h, sG.n)
            inst = g.collective_compute("AllGather", ALU.bypass, replica_groups=RG8,
                                        ins=[agB_in], outs=[agB_out])
            sCC.inc(inst)

    def emit_wo(tag, src_out, w_ap, b_ap, res_ap, dst_d, res_kind):
        nvi = 2 if res_kind == "ln_res" else 1
        with nc.Block(f"{tag}_wow") as blk:
            t_snap = sT.n

            @blk.gpsimd
            def _(g):
                g.wait_ge(sT.h, t_snap)
                for ch in range(16):
                    inst = g.dma_start(out=wstage[:, ch * 1024:(ch + 1) * 1024],
                                       in_=w_ap[ch * 128:(ch + 1) * 128, 0:1024])
                    sG.inc(inst, 16)
                for ch in range(16):
                    inst = g.dma_start(out=act_fm[:, ch * 1024:(ch + 1) * 1024],
                                       in_=w_ap[ch * 128:(ch + 1) * 128, 1024:2048])
                    sG.inc(inst, 16)
                if b_ap is not None:
                    inst = g.dma_start(out=bias_row[0:1, :], in_=b_ap[:])
                    sG.inc(inst, 16)

        if upto <= 4.5:
            return
        rg = nc.gpsimd.compute_val(nc.partition_id([ET.Pool]))
        t_base = sT.n
        L = {}
        gl = sG.n
        for s in range(NSUB):
            gl += 16
            L[s] = gl
        for i in nc.Switch(engines=[nc.gpsimd], index=[rg], n=NCO):
            nc.gpsimd.wait_ge(sCC.h, sCC.n)
            for s in range(NSUB):
                if s >= 2 and upto > 4.6:
                    nc.gpsimd.wait_ge(sT.h, t_base + 4 * (s - 1))
                col0 = TS * i + 128 * s
                src4 = src_out[:, :, col0:col0 + 128].rearrange(
                    "r (b p) i -> p r b i", p=128)
                dst4 = lhsb[s % 2].rearrange("p (r b i) -> p r b i", r=8, b=2)
                nc.gpsimd.dma_start(out=dst4, in_=src4).then_inc(sG.h, 16)
        sG.n += 16 * NSUB

        if upto <= 4.6:
            return
        with nc.Block(f"{tag}_wo") as blk:
            t0, a0, v0, g0 = sT.n, sA.n, sV.n, sG.n
            soe_snap = soe.n

            @blk.tensor
            def _(t):
                t.wait_ge(soe.h, soe_snap)
                t.wait_ge(sV.h, v0)
                for s in range(NSUB):
                    t.wait_ge(sG.h, L[s])
                    if upto > 4.65:
                        t.wait_ge(sV.h, v0 + s * nvi)
                    for ob in range(4):
                        wsrc = wstage if ob < 2 else act_fm
                        o2 = ob % 2
                        for ch in range(16):
                            inst = t.matmul(ps[ob][:, :], lhsb[s % 2][:, 128 * ch:128 * (ch + 1)],
                                            wsrc[:, ch * 1024 + o2 * 512:ch * 1024 + (o2 + 1) * 512],
                                            start=(ch == 0), stop=(b_ap is None and ch == 15))
                        if b_ap is not None:
                            inst = t.matmul(ps[ob][:, :], ones_b[0:1, 0:128],
                                            bias_row[0:1, 512 * ob:512 * (ob + 1)],
                                            start=False, stop=True)
                        sT.inc(inst)

            if upto <= 4.65:
                return
            sy0 = sSy.n

            @blk.sync
            def _(sy):
                for s in range(NSUB):
                    sy.wait_ge(sV.h, v0 + s * nvi)
                    inst = sy.dma_start(out=xt[:, :], in_=res_ap[128 * s:128 * (s + 1), :])
                    sSy.inc(inst, 16)
                    sy.wait_ge(sV.h, v0 + (s + 1) * nvi)
                    inst = sy.dma_start(out=dst_d[128 * s:(s + 1) * 128, :], in_=gt[:, :])
                    sSy.inc(inst, 16)

            if res_kind == "ln_res":
                @blk.scalar
                def _(sc):
                    for s in range(NSUB):
                        sc.wait_ge(sT.h, t0 + 4 * s + 4)
                        sc.wait_ge(sV.h, v0 + s * nvi)
                        for ob in range(4):
                            sq_src = ps[ob][:, :] if upto > 4.8 else xt[:, 512 * ob:512 * (ob + 1)]
                            inst = sc.activation(ft[:, 512 * ob:512 * (ob + 1)], sq_src,
                                                 AF.Square, bias=cons[:, 0:1], scale=1.0,
                                                 accum_out=stats[:, ob:ob + 1])
                        sA.inc(inst)
                        sc.wait_ge(sV.h, v0 + s * nvi + 1)
                        sc.activation(stats[:, 6:7], stats[:, 4:5], AF.Ln,
                                      bias=cons[:, 1:2], scale=1.0)
                        inst = sc.activation(stats[:, 5:6], stats[:, 6:7], AF.Exp,
                                             bias=cons[:, 0:1], scale=-0.5)
                        sA.inc(inst)

            @blk.vector
            def _(v):
                for s in range(NSUB):
                    v.wait_ge(sT.h, t0 + 4 * s + 4)
                    v.wait_ge(sSy.h, sy0 + 16 * (2 * s + 1))
                    if s >= 1:
                        v.wait_ge(sSy.h, sy0 + 16 * 2 * s)
                    if res_kind == "ln_res":
                        v.wait_ge(sA.h, a0 + s * 2 + 1)
                        if upto > 4.75:
                            for ob in range(4):
                                if ob == 0:
                                    v.reduce_sum(stats[:, 12:13], ps[ob][:, :], axis=AX)
                                else:
                                    v.reduce_sum(stats[:, 13:14], ps[ob][:, :], axis=AX)
                                    v.tensor_add(stats[:, 12:13], stats[:, 12:13],
                                                 stats[:, 13:14])
                            v.tensor_add(stats[:, 14:15], stats[:, 0:1], stats[:, 1:2])
                            v.tensor_add(stats[:, 15:16], stats[:, 2:3], stats[:, 3:4])
                            v.tensor_add(stats[:, 14:15], stats[:, 14:15], stats[:, 15:16])
                            v.tensor_scalar(stats[:, 2:3], stats[:, 12:13], 1.0 / D, None, ALU.mult)
                            v.tensor_scalar(stats[:, 3:4], stats[:, 14:15], 1.0 / D, None, ALU.mult)
                            v.tensor_mul(stats[:, 4:5], stats[:, 2:3], stats[:, 2:3])
                            inst = v.tensor_sub(stats[:, 4:5], stats[:, 3:4], stats[:, 4:5])
                        else:
                            inst = v.tensor_copy(stats[:, 4:5], cons[:, 1:2])
                        sV.inc(inst)
                        v.wait_ge(sA.h, a0 + s * 2 + 2)
                        for ob in range(4):
                            if upto > 4.77:
                                v.tensor_scalar(ft[:, 512 * ob:512 * (ob + 1)], ps[ob][:, :],
                                                stats[:, 2:3], stats[:, 5:6],
                                                ALU.subtract, ALU.mult)
                            else:
                                v.tensor_copy(ft[:, 512 * ob:512 * (ob + 1)], ps[ob][:, :])
                        inst = v.tensor_add(gt[:, :], xt[:, :], ft[:, :])
                        sV.inc(inst)
                    else:
                        for ob in range(4):
                            v.tensor_copy(ft[:, 512 * ob:512 * (ob + 1)], ps[ob][:, :])
                        inst = v.tensor_add(gt[:, :], xt[:, :], ft[:, :])
                        sV.inc(inst)

    if upto <= 4.3:
        return _cut()
    emit_wo("b1", agB_out, Wo, bo, x_sl, x2_d,
            "add" if upto <= 4.7 else "ln_res")
    if upto <= 4.9:
        return _cut()

    if upto <= 5:
        return _cut()

    with nc.Block("p6w") as blk:
        @blk.gpsimd
        def _(g):
            sG.inc(g.dma_start(out=wrep[:, :], in_=ln1w_r), 16)

    with nc.Block("p6") as blk:
        g0, a0, v0 = sG.n, sA.n, sV.n

        @blk.gpsimd
        def _(g):
            for s in range(NSUB):
                g.wait_ge(sV.h, v0 + 3 * s)
                inst = g.dma_start(out=xt[:, :], in_=x2_d[128 * s:(s + 1) * 128, :])
                sG.inc(inst, 16)
                g.wait_ge(sV.h, v0 + 3 * s + 2)
                inst = g.dma_start(out=li2_d[128 * s:(s + 1) * 128, :], in_=gt[:, :])
                sG.inc(inst, 16)
                g.wait_ge(sV.h, v0 + 3 * s + 3)
                inst = g.dma_start(out=h_d[128 * s:(s + 1) * 128, :], in_=stg[0][:, :])
                sG.inc(inst, 16)

        @blk.scalar
        def _(sc):
            for s in range(NSUB):
                sc.wait_ge(sG.h, g0 + 48 * s + 16)
                inst = sc.activation(ft[:, :], xt[:, :], AF.Square, bias=cons[:, 0:1],
                                     scale=1.0, accum_out=stats[:, 1:2])
                sA.inc(inst)
                sc.wait_ge(sV.h, v0 + 3 * s + 1)
                sc.activation(stats[:, 6:7], stats[:, 4:5], AF.Ln,
                              bias=cons[:, 1:2], scale=1.0)
                inst = sc.activation(stats[:, 5:6], stats[:, 6:7], AF.Exp,
                                     bias=cons[:, 0:1], scale=-0.5)
                sA.inc(inst)
                sc.wait_ge(sV.h, v0 + 3 * s + 2)
                sc.activation(stats[:, 8:9], stats[:, 7:8], AF.Ln,
                              bias=cons[:, 2:3], scale=1.0)
                inst = sc.activation(stats[:, 9:10], stats[:, 8:9], AF.Exp,
                                     bias=cons[:, 0:1], scale=-0.5)
                sA.inc(inst)

        @blk.vector
        def _(v):
            for s in range(NSUB):
                v.wait_ge(sG.h, g0 + 48 * s + 16)
                v.reduce_sum(stats[:, 0:1], xt[:, :], axis=AX)
                v.wait_ge(sA.h, a0 + 3 * s + 1)
                v.tensor_scalar(stats[:, 2:3], stats[:, 0:1], 1.0 / D, None, ALU.mult)
                v.tensor_scalar(stats[:, 3:4], stats[:, 1:2], 1.0 / D, None, ALU.mult)
                v.tensor_mul(stats[:, 4:5], stats[:, 2:3], stats[:, 2:3])
                inst = v.tensor_sub(stats[:, 4:5], stats[:, 3:4], stats[:, 4:5])
                sV.inc(inst)
                v.wait_ge(sA.h, a0 + 3 * s + 2)
                v.tensor_mul(stats[:, 7:8], stats[:, 5:6], stats[:, 5:6])
                v.tensor_mul(stats[:, 7:8], stats[:, 7:8], stats[:, 4:5])
                if s >= 1:
                    v.wait_ge(sG.h, g0 + 48 * s - 16)
                inst = v.tensor_scalar(gt[:, :], xt[:, :], stats[:, 2:3],
                                       stats[:, 5:6], ALU.subtract, ALU.mult)
                sV.inc(inst)
                v.wait_ge(sA.h, a0 + 3 * s + 3)
                if s >= 1:
                    v.wait_ge(sG.h, g0 + 48 * s)
                v.tensor_scalar(ft[:, :], gt[:, :], stats[:, 9:10], None, ALU.mult)
                inst = v.tensor_mul(stg[0][:, :], ft[:, :], wrep[:, :])
                sV.inc(inst)

    with nc.Block("p6t") as blk:
        @blk.sync
        def _(s):
            s.wait_ge(sG.h, sG.n)
            for ch in range(16):
                inst = s.dma_start(out=act_fm[:, ch * TS:(ch + 1) * TS],
                                   in_=h_d[:, ch * 128:(ch + 1) * 128], transpose=True)
                sSy.inc(inst, 16)

    if upto <= 6:
        return _cut()

    emit_stream_projs([(fwq, None, mk_dst(qraw_d, 0))], accum_ssq=True, name="p7q")

    def emit_kv512(w_ap, dst_col, do_knorm, name):
        with nc.Block(name) as blk:
            @blk.gpsimd
            def _(g):
                g.wait_ge(sT.h, sT.n)
                for ch in range(16):
                    inst = g.dma_start(out=wstage[:, ch * 512:(ch + 1) * 512],
                                       in_=w_ap[ch * 128:(ch + 1) * 128, :])
                    sG.inc(inst, 16)
                if do_knorm:
                    inst = g.dma_start(out=wrep[:, 0:512], in_=knw_r)
                    sG.inc(inst, 16)

            t0, a0, v0, g0 = sT.n, sA.n, sV.n, sG.n

            @blk.tensor
            def _(t):
                t.wait_ge(sG.h, g0)
                t.wait_ge(sSy.h, sSy.n)
                t.wait_ge(sV.h, v0)
                for s in range(NSUB):
                    if s >= 2:
                        t.wait_ge(sV.h, v0 + s - 1)
                    for ch in range(16):
                        inst = t.matmul(ps[s % 2][:, :],
                                        act_fm[:, ch * TS + s * 128:ch * TS + (s + 1) * 128],
                                        wstage[:, ch * 512:(ch + 1) * 512],
                                        start=(ch == 0), stop=(ch == 15))
                    sT.inc(inst)

            if do_knorm:
                @blk.scalar
                def _(sc):
                    for s in range(NSUB):
                        sc.wait_ge(sT.h, t0 + s + 1)
                        if s >= 1:
                            sc.wait_ge(sV.h, v0 + s)
                        sc.activation(qsq[:, :], ps[s % 2][:, :], AF.Square,
                                      bias=cons[:, 0:1], scale=1.0,
                                      accum_out=stats[:, 0:1])
                        sc.activation(stats[:, 6:7], stats[:, 0:1], AF.Ln,
                                      bias=cons[:, 2:3], scale=1.0 / 512.0)
                        inst = sc.activation(stats[:, 5:6], stats[:, 6:7], AF.Exp,
                                             bias=cons[:, 0:1], scale=-0.5)
                        sA.inc(inst)

                @blk.vector
                def _(v):
                    for s in range(NSUB):
                        v.wait_ge(sA.h, a0 + s + 1)
                        if s >= 1:
                            v.wait_ge(sG.h, g0 + 16 * s)
                        v.tensor_scalar(rt[:, :], ps[s % 2][:, :], stats[:, 5:6],
                                        None, ALU.mult)
                        v.tensor_mul(rt[:, :], rt[:, :], wrep[:, 0:512])
                        for hh in range(4):
                            v.tensor_copy(crep[:, 32 * hh:32 * hh + 32],
                                          csc[:, 32 * s:32 * s + 32])
                            v.tensor_copy(srep[:, 32 * hh:32 * hh + 32],
                                          ssc[:, 32 * s:32 * s + 32])
                        a_ap = rt[:, :].rearrange("p (h d) -> p h d", d=128)[:, :, 0:32]
                        b_ap2 = rt[:, :].rearrange("p (h d) -> p h d", d=128)[:, :, 32:64]
                        c_ap = crep[:, 0:128].rearrange("p (h d) -> p h d", d=32)
                        s_ap = srep[:, 0:128].rearrange("p (h d) -> p h d", d=32)
                        t1 = gt[:, 0:128].rearrange("p (h d) -> p h d", d=32)
                        t2 = gt[:, 128:256].rearrange("p (h d) -> p h d", d=32)
                        t3 = gt[:, 256:384].rearrange("p (h d) -> p h d", d=32)
                        t4 = gt[:, 384:512].rearrange("p (h d) -> p h d", d=32)
                        v.tensor_mul(t1, a_ap, c_ap)
                        v.tensor_mul(t2, b_ap2, s_ap)
                        v.tensor_mul(t3, b_ap2, c_ap)
                        v.tensor_mul(t4, a_ap, s_ap)
                        v.tensor_sub(a_ap, t1, t2)
                        v.tensor_add(b_ap2, t3, t4)
                        inst = v.tensor_copy(stg[0][:, 0:512], rt[:, :])
                        sV.inc(inst)
            else:
                @blk.vector
                def _(v):
                    for s in range(NSUB):
                        v.wait_ge(sT.h, t0 + s + 1)
                        if s >= 1:
                            v.wait_ge(sG.h, g0 + 16 * s)
                        inst = v.tensor_copy(stg[0][:, 0:512], ps[s % 2][:, :])
                        sV.inc(inst)

            @blk.gpsimd
            def _(g):
                for s in range(NSUB):
                    g.wait_ge(sV.h, v0 + s + 1)
                    inst = g.dma_start(
                        out=agCkv_in[128 * s:128 * (s + 1), dst_col:dst_col + 512],
                        in_=stg[0][:, 0:512])
                    sG.inc(inst, 16)

    with nc.Block("p7qp") as blk:
        @blk.gpsimd
        def _(g):
            sG.inc(g.dma_start(out=wrep[:, :], in_=qnw_r), 16)

        g0, a0, v0 = sG.n, sA.n, sV.n
        sy_snap = sSy.n

        @blk.gpsimd
        def _(g):
            g.wait_ge(sSy.h, sy_snap)
            for s in range(NSUB):
                g.wait_ge(sV.h, v0 + 2 * s)
                inst = g.dma_start(out=stg[0][:, :], in_=qraw_d[128 * s:128 * (s + 1), :])
                sG.inc(inst, 16)
                g.wait_ge(sV.h, v0 + 2 * s + 2)
                inst = g.dma_start(out=agCq_in[128 * s:128 * (s + 1), :],
                                   in_=stg[1][:, :])
                sG.inc(inst, 16)
            g.wait_ge(sG.h, g0 + 32 * NSUB)
            inst = g.collective_compute("AllGather", ALU.bypass, replica_groups=RG8,
                                        ins=[agCq_in], outs=[agCq_out])
            sCC.inc(inst)

        @blk.scalar
        def _(sc):
            for s in range(NSUB):
                sc.wait_ge(sG.h, g0 + 32 * s + 16)
                sc.activation(stats[:, 6:7], ssqa[:, s:s + 1], AF.Ln,
                              bias=cons[:, 2:3], scale=1.0 / D)
                inst = sc.activation(stats[:, 5:6], stats[:, 6:7], AF.Exp,
                                     bias=cons[:, 0:1], scale=-0.5)
                sA.inc(inst)

        @blk.vector
        def _(v):
            for s in range(NSUB):
                v.wait_ge(sG.h, g0 + 32 * s + 16)
                v.wait_ge(sA.h, a0 + s + 1)
                v.tensor_scalar(ft[:, :], stg[0][:, :], stats[:, 5:6], None, ALU.mult)
                inst = v.tensor_mul(ft[:, :], ft[:, :], wrep[:, :])
                sV.inc(inst)
                for hh in range(16):
                    v.tensor_copy(crep[:, 32 * hh:32 * hh + 32], csc[:, 32 * s:32 * s + 32])
                    v.tensor_copy(srep[:, 32 * hh:32 * hh + 32], ssc[:, 32 * s:32 * s + 32])
                a_ap = ft[:, :].rearrange("p (h d) -> p h d", d=128)[:, :, 0:32]
                b_ap2 = ft[:, :].rearrange("p (h d) -> p h d", d=128)[:, :, 32:64]
                c_ap = crep[:, :].rearrange("p (h d) -> p h d", d=32)
                s_ap = srep[:, :].rearrange("p (h d) -> p h d", d=32)
                t1 = gt[:, 0:512].rearrange("p (h d) -> p h d", d=32)
                t2 = gt[:, 512:1024].rearrange("p (h d) -> p h d", d=32)
                t3 = gt[:, 1024:1536].rearrange("p (h d) -> p h d", d=32)
                t4 = gt[:, 1536:2048].rearrange("p (h d) -> p h d", d=32)
                v.tensor_mul(t1, a_ap, c_ap)
                v.tensor_mul(t2, b_ap2, s_ap)
                v.tensor_mul(t3, b_ap2, c_ap)
                v.tensor_mul(t4, a_ap, s_ap)
                v.tensor_sub(a_ap, t1, t2)
                v.tensor_add(b_ap2, t3, t4)
                inst = v.tensor_copy(stg[1][:, :], ft[:, :])
                sV.inc(inst)


    emit_kv512(fwk, 0, True, "p7k")
    emit_kv512(fwv, 512, False, "p7v")

    with nc.Block("p8ag") as blk:
        @blk.gpsimd
        def _(g):
            g.wait_ge(sG.h, sG.n)
            inst = g.collective_compute("AllGather", ALU.bypass, replica_groups=RG8,
                                        ins=[agCkv_in], outs=[agCkv_out])
            sCC.inc(inst)

    if upto <= 7:
        return _cut()

    def kv2_body(i, hb):
        gidx = i // 2
        for r in range(NCO):
            nc.sync.dma_start(out=kslab[hb][:, 1024 * r:1024 * (r + 1)],
                              in_=agCkv_out[r, :, 128 * gidx:128 * (gidx + 1)],
                              transpose=True).then_inc(sSy.h, 16)
        for r in range(NCO):
            src3 = agCkv_out[r, :, 512 + 128 * gidx:512 + 128 * (gidx + 1)].rearrange(
                "(mm p) d -> p mm d", p=128)
            dst3 = vslab[hb][:, 1024 * r:1024 * (r + 1)].rearrange(
                "p (mm d) -> p mm d", d=128)
            nc.gpsimd.dma_start(out=dst3, in_=src3).then_inc(sG.h, 16)

    def q2_tr(i, hb, sl):
        h = 2 * i + hb
        r = sl // 2
        row0 = 512 * (sl % 2)
        return agCq_out[r, row0:row0 + 512, 128 * h:128 * (h + 1)]

    emit_attention("b2", kv2_body, True, q2_tr, agD_in)

    if upto <= 8:
        return _cut()

    with nc.Block("p9ag") as blk:
        @blk.gpsimd
        def _(g):
            g.wait_ge(sG.h, sG.n)
            inst = g.collective_compute("AllGather", ALU.bypass, replica_groups=RG8,
                                        ins=[agD_in], outs=[agD_out])
            sCC.inc(inst)

    emit_wo("b2", agD_out, fwo, None, li2_d, x1_d, "add")

    if upto <= 9:
        return _cut()

    with nc.Block("p9w") as blk:
        @blk.gpsimd
        def _(g):
            sG.inc(g.dma_start(out=wrep[:, :], in_=ln2w_r), 16)
            src3 = rw.rearrange("(c p) e -> p c e", p=128)
            dst3 = rwb[:, :].rearrange("p (c e) -> p c e", e=8)
            sG.inc(g.dma_start(out=dst3, in_=src3), 16)

    with nc.Block("p9h") as blk:
        g0, a0, v0 = sG.n, sA.n, sV.n

        @blk.gpsimd
        def _(g):
            for s in range(NSUB):
                g.wait_ge(sV.h, v0 + s)
                inst = g.dma_start(out=xt[:, :], in_=x1_d[128 * s:(s + 1) * 128, :])
                sG.inc(inst, 16)
                g.wait_ge(sV.h, v0 + s + 1)
                inst = g.dma_start(out=h2_d[128 * s:(s + 1) * 128, :], in_=stg[0][:, :])
                sG.inc(inst, 16)

        @blk.scalar
        def _(sc):
            for s in range(NSUB):
                sc.wait_ge(sG.h, g0 + 32 * s + 16)
                sc.activation(ft[:, :], xt[:, :], AF.Square, bias=cons[:, 0:1],
                              scale=1.0, accum_out=stats[:, 0:1])
                sc.activation(stats[:, 6:7], stats[:, 0:1], AF.Ln,
                              bias=cons[:, 2:3], scale=1.0 / D)
                inst = sc.activation(stats[:, 5:6], stats[:, 6:7], AF.Exp,
                                     bias=cons[:, 0:1], scale=-0.5)
                sA.inc(inst)

        @blk.vector
        def _(v):
            for s in range(NSUB):
                v.wait_ge(sA.h, a0 + s + 1)
                if s >= 1:
                    v.wait_ge(sG.h, g0 + 32 * s)
                v.tensor_scalar(gt[:, :], xt[:, :], stats[:, 5:6], None, ALU.mult)
                inst = v.tensor_mul(stg[0][:, :], gt[:, :], wrep[:, :])
                sV.inc(inst)

    with nc.Block("p9t") as blk:
        @blk.sync
        def _(s):
            s.wait_ge(sG.h, sG.n)
            for ch in range(16):
                inst = s.dma_start(out=act_fm[:, ch * TS:(ch + 1) * TS],
                                   in_=h2_d[:, ch * 128:(ch + 1) * 128], transpose=True)
                sSy.inc(inst, 16)

    with nc.Block("p9r") as blk:
        t0, a0, v0 = sT.n, sA.n, sV.n

        @blk.tensor
        def _(t):
            t.wait_ge(sSy.h, sSy.n)
            t.wait_ge(sG.h, sG.n)
            t.wait_ge(sV.h, v0)
            for s in range(NSUB):
                if s >= 2:
                    t.wait_ge(sA.h, a0 + s - 1)
                for ch in range(16):
                    inst = t.matmul(ps[s % 2][:, 0:8],
                                    act_fm[:, ch * TS + s * 128:ch * TS + (s + 1) * 128],
                                    rwb[:, 8 * ch:8 * (ch + 1)],
                                    start=(ch == 0), stop=(ch == 15))
                sT.inc(inst)
                t.wait_ge(sV.h, v0 + 2 * s + 1)
                inst = t.transpose(ps[2][0:8, 0:128], rout[:, 0:8], idt[:, :])
                sT.inc(inst)

        @blk.scalar
        def _(sc):
            for s in range(NSUB):
                sc.wait_ge(sT.h, t0 + 2 * s + 1)
                inst = sc.activation(rout[:, 8:16], ps[s % 2][:, 0:8], AF.Exp,
                                     bias=cons[:, 0:1], scale=-1.0)
                sA.inc(inst)

        @blk.vector
        def _(v):
            for s in range(NSUB):
                v.wait_ge(sA.h, a0 + s + 1)
                v.tensor_scalar(rout[:, 8:16], rout[:, 8:16], 1.0, None, ALU.add)
                v.reciprocal(rout[:, 16:24], rout[:, 8:16])
                v.tensor_add(rout[:, 24:32], rout[:, 16:24], rb_rep[:, :])
                v.reduce_max(stats[:, 0:1], rout[:, 24:32], axis=AX)
                v.tensor_scalar(rout[:, 32:40], rout[:, 24:32], stats[:, 0:1],
                                None, ALU.is_equal)
                v.scalar_tensor_tensor(rout[:, 40:48], rout[:, 32:40], -1e30,
                                       rout[:, 24:32], ALU.mult, ALU.add)
                v.reduce_max(stats[:, 1:2], rout[:, 40:48], axis=AX)
                v.tensor_scalar(rout[:, 48:56], rout[:, 40:48], stats[:, 1:2],
                                None, ALU.is_equal)
                v.tensor_add(rout[:, 32:40], rout[:, 32:40], rout[:, 48:56])
                v.tensor_mul(rout[:, 56:64], rout[:, 16:24], rout[:, 32:40])
                v.reduce_sum(stats[:, 2:3], rout[:, 56:64], axis=AX)
                v.reciprocal(stats[:, 3:4], stats[:, 2:3])
                inst = v.tensor_scalar(rout[:, 0:8], rout[:, 56:64], stats[:, 3:4],
                                       None, ALU.mult)
                sV.inc(inst)
                v.wait_ge(sT.h, t0 + 2 * s + 2)
                dstc = ctr8[0] if s < 4 else ctr8[1]
                inst = v.tensor_copy(dstc[:, 128 * (s % 4):128 * (s % 4 + 1)],
                                     ps[2][0:8, 0:128])
                sV.inc(inst)

        @blk.gpsimd
        def _(g):
            g.wait_ge(sV.h, sV.n)
            sG.inc(g.dma_start(out=ctr_d[:, 0:512], in_=ctr8[0]), 16)
            sG.inc(g.dma_start(out=ctr_d[:, 512:1024], in_=ctr8[1]), 16)

    if upto <= 10:
        return _cut()

    with nc.Block("p10") as blk:
        t0, a0, v0, g0 = sT.n, sA.n, sV.n, sG.n
        soe_snap, spv_snap, sqk_snap = soe.n, spv.n, sqk.n
        GW, TW, VW, AW = {}, {}, {}, {}
        g_cnt, t_cnt, v_cnt, a_cnt = g0, t0, v0, a0
        for e in range(8):
            g_cnt += 32
            GW[(e, 'ctr')] = g_cnt
            for qu in range(4):
                g_cnt += 16 * 32
                GW[(e, qu)] = g_cnt
                if qu == 2:
                    g_cnt += 16 * 4
                    GW[(e, 'wdlo')] = g_cnt
            g_cnt += 16 * 4
            GW[(e, 'wd')] = g_cnt
            t_cnt += 2
            TW[(e, 'cbc')] = t_cnt
            v_cnt += 1
            VW[(e, 'cbc')] = v_cnt
            for qu in range(4):
                for pr in range(4):
                    t_cnt += 2
                    TW[(e, qu, pr)] = t_cnt
                    a_cnt += 1
                    AW[(e, qu, pr)] = a_cnt
                    v_cnt += 1
                    VW[(e, qu, pr)] = v_cnt
            for dt_i in range(32):
                t_cnt += 1
                TW[(e, 'd', dt_i)] = t_cnt
                v_cnt += 1
                VW[(e, 'd', dt_i)] = v_cnt

        def ld_quarter(g, e, qu):
            qb = (qu % 2) * 8192
            for ch in range(16):
                inst = g.dma_start(out=wstage[:, qb + ch * 256:qb + (ch + 1) * 256],
                                   in_=wg[e, ch * 128:(ch + 1) * 128,
                                          256 * qu:256 * (qu + 1)])
                sG.inc(inst, 16)
            for ch in range(16):
                inst = g.dma_start(
                    out=wstage[:, qb + 4096 + ch * 256:qb + 4096 + (ch + 1) * 256],
                    in_=wu[e, ch * 128:(ch + 1) * 128, 256 * qu:256 * (qu + 1)])
                sG.inc(inst, 16)

        def ld_wd(g, e, lo):
            for chi in (range(4) if lo else range(4, 8)):
                dst = wd_lo if chi < 4 else wd_hi
                co = (chi % 4) * 2048
                inst = g.dma_start(out=dst[:, co:co + 2048],
                                   in_=wd[e, chi * 128:(chi + 1) * 128, :])
                sG.inc(inst, 16)

        @blk.gpsimd
        def _(g):
            g.wait_ge(soe.h, soe_snap)
            g.wait_ge(spv.h, spv_snap)
            g.wait_ge(sqk.h, sqk_snap)
            g.wait_ge(sT.h, t0)
            g.wait_ge(sV.h, v0)
            g.wait_ge(sA.h, a0)
            for e in range(8):
                if e >= 1:
                    g.wait_ge(sT.h, TW[(e - 1, 'cbc')])
                inst = g.dma_start(out=lrow[0], in_=ctr_d[e, 0:512])
                sG.inc(inst, 16)
                inst = g.dma_start(out=lrow[1], in_=ctr_d[e, 512:1024])
                sG.inc(inst, 16)
                for qu in range(4):
                    if qu >= 2:
                        g.wait_ge(sT.h, TW[(e, qu - 2, 3)])
                    elif e >= 1:
                        g.wait_ge(sT.h, TW[(e - 1, qu + 2, 3)])
                    ld_quarter(g, e, qu)
                    if qu == 2:
                        ld_wd(g, e, True)
                ld_wd(g, e, False)

        @blk.tensor
        def _(t):
            t.wait_ge(sV.h, v0)
            for e in range(8):
                t.wait_ge(sG.h, GW[(e, 'ctr')])
                if e >= 1:
                    t.wait_ge(sV.h, VW[(e - 1, 'cbc')])
                t.matmul(ps[4][:, :], onesf[0:1, 0:128], lrow[0],
                         start=True, stop=True)
                inst = t.matmul(ps[5][:, :], onesf[0:1, 0:128], lrow[1],
                                start=True, stop=True)
                sT.inc(inst, 2)
                for qu in range(4):
                    t.wait_ge(sG.h, GW[(e, qu)])
                    qb = (qu % 2) * 8192
                    for pr in range(4):
                        ic2, th = pr // 2, pr % 2
                        bg, bu = (0, 1) if th == 0 else (2, 3)
                        if pr >= 2:
                            t.wait_ge(sV.h, VW[(e, qu, pr - 2)])
                        elif qu >= 1:
                            t.wait_ge(sV.h, VW[(e, qu - 1, 2 + pr)])
                        elif e >= 1:
                            t.wait_ge(sV.h, VW[(e - 1, 'd', 31)])
                        for ch in range(16):
                            inst = t.matmul(ps[bg][:, :],
                                            wstage[:, qb + ch * 256 + ic2 * 128:qb + ch * 256 + (ic2 + 1) * 128],
                                            act_fm[:, ch * TS + th * 512:ch * TS + (th + 1) * 512],
                                            start=(ch == 0), stop=(ch == 15))
                        sT.inc(inst)
                        for ch in range(16):
                            inst = t.matmul(ps[bu][:, :],
                                            wstage[:, qb + 4096 + ch * 256 + ic2 * 128:qb + 4096 + ch * 256 + (ic2 + 1) * 128],
                                            act_fm[:, ch * TS + th * 512:ch * TS + (th + 1) * 512],
                                            start=(ch == 0), stop=(ch == 15))
                        sT.inc(inst)
                t.wait_ge(sG.h, GW[(e, 'wd')])
                t.wait_ge(sV.h, VW[(e, 3, 3)])
                dt_i = 0
                for s in range(NSUB):
                    for ob in range(4):
                        bank = ps[dt_i % 2]
                        if dt_i >= 2:
                            t.wait_ge(sV.h, VW[(e, 'd', dt_i - 2)])
                        for chi in range(8):
                            wsrc = wd_lo if chi < 4 else wd_hi
                            co = (chi % 4) * 2048
                            inst = t.matmul(bank[:, :],
                                            actb[:, chi * 1024 + s * 128:chi * 1024 + (s + 1) * 128],
                                            wsrc[:, co + ob * 512:co + (ob + 1) * 512],
                                            start=(chi == 0), stop=(chi == 7))
                        sT.inc(inst)
                        dt_i += 1

        @blk.scalar
        def _(sc):
            for e in range(8):
                for qu in range(4):
                    for pr in range(4):
                        bg = 0 if pr % 2 == 0 else 2
                        sc.wait_ge(sT.h, TW[(e, qu, pr)] - 1)
                        if pr >= 1:
                            sc.wait_ge(sV.h, VW[(e, qu, pr - 1)])
                        elif qu >= 1:
                            sc.wait_ge(sV.h, VW[(e, qu - 1, 3)])
                        elif e >= 1:
                            sc.wait_ge(sV.h, VW[(e - 1, 3, 3)])
                        inst = sc.activation(sgb[:, :], ps[bg][:, :], AF.Silu,
                                             bias=cons[:, 0:1], scale=1.0)
                        sA.inc(inst)

        @blk.vector
        def _(v):
            for e in range(8):
                v.wait_ge(sT.h, TW[(e, 'cbc')])
                v.tensor_copy(cbc[:, 0:512], ps[4][:, :])
                inst = v.tensor_copy(cbc[:, 512:1024], ps[5][:, :])
                sV.inc(inst)
                for qu in range(4):
                    for pr in range(4):
                        ic2, th = pr // 2, pr % 2
                        bu = 1 if th == 0 else 3
                        gI = qu * 2 + ic2
                        v.wait_ge(sT.h, TW[(e, qu, pr)])
                        v.wait_ge(sA.h, AW[(e, qu, pr)])
                        v.tensor_mul(rt[:, :], ps[bu][:, :], cbc[:, 512 * th:512 * (th + 1)])
                        inst = v.tensor_mul(
                            actb[:, gI * 1024 + th * 512:gI * 1024 + (th + 1) * 512],
                            sgb[:, :], rt[:, :])
                        sV.inc(inst)
                dt_i = 0
                for s in range(NSUB):
                    for ob in range(4):
                        v.wait_ge(sT.h, TW[(e, 'd', dt_i)])
                        mslice = macc[:, s * 2048 + ob * 512:s * 2048 + (ob + 1) * 512]
                        if e == 0:
                            inst = v.tensor_copy(mslice, ps[dt_i % 2][:, :])
                        else:
                            inst = v.tensor_add(mslice, mslice, ps[dt_i % 2][:, :])
                        sV.inc(inst)
                        dt_i += 1

    if upto <= 11:
        return _cut()

    with nc.Block("p11") as blk:
        g0, a0, v0, o0 = sG.n, sA.n, sV.n, sO.n

        @blk.gpsimd
        def _(g):
            for s in range(NSUB):
                g.wait_ge(sV.h, v0 + 3 * s)
                inst = g.dma_start(out=xt[:, :], in_=x1_d[128 * s:(s + 1) * 128, :])
                sG.inc(inst, 16)
                inst = g.dma_start(out=gt[:, :], in_=x2_d[128 * s:(s + 1) * 128, :])
                sG.inc(inst, 16)
                g.wait_ge(sV.h, v0 + 3 * s + 3)
                inst = g.dma_start(out=y_out[128 * s:(s + 1) * 128, :], in_=ft[:, :])
                sO.inc(inst, 16)

        @blk.scalar
        def _(sc):
            for s in range(NSUB):
                sc.wait_ge(sV.h, v0 + 3 * s + 1)
                inst = sc.activation(xt[:, :], ft[:, :], AF.Square, bias=cons[:, 0:1],
                                     scale=1.0, accum_out=stats[:, 1:2])
                sA.inc(inst)
                sc.wait_ge(sV.h, v0 + 3 * s + 2)
                sc.activation(stats[:, 6:7], stats[:, 4:5], AF.Ln,
                              bias=cons[:, 1:2], scale=1.0)
                inst = sc.activation(stats[:, 5:6], stats[:, 6:7], AF.Exp,
                                     bias=cons[:, 0:1], scale=-0.5)
                sA.inc(inst)

        @blk.vector
        def _(v):
            for s in range(NSUB):
                v.wait_ge(sG.h, g0 + 32 * (s + 1))
                if s >= 1:
                    v.wait_ge(sO.h, o0 + 16 * s)
                inst = v.tensor_add(ft[:, :], xt[:, :], macc[:, 2048 * s:2048 * (s + 1)])
                sV.inc(inst)
                v.reduce_sum(stats[:, 0:1], ft[:, :], axis=AX)
                v.wait_ge(sA.h, a0 + 2 * s + 1)
                v.tensor_scalar(stats[:, 2:3], stats[:, 0:1], 1.0 / D, None, ALU.mult)
                v.tensor_scalar(stats[:, 3:4], stats[:, 1:2], 1.0 / D, None, ALU.mult)
                v.tensor_mul(stats[:, 4:5], stats[:, 2:3], stats[:, 2:3])
                inst = v.tensor_sub(stats[:, 4:5], stats[:, 3:4], stats[:, 4:5])
                sV.inc(inst)
                v.wait_ge(sA.h, a0 + 2 * s + 2)
                v.tensor_scalar(xt[:, :], ft[:, :], stats[:, 2:3], stats[:, 5:6],
                                ALU.subtract, ALU.mult)
                inst = v.tensor_add(ft[:, :], gt[:, :], xt[:, :])
                sV.inc(inst)

    if DEBUG_TAPS:
        with nc.Block("taps") as blk:
            @blk.gpsimd
            def _(g):
                g.wait_ge(sO.h, sO.n)
                for nm, src in [("tap_lnx", lnx_d), ("tap_qd", q_d), ("tap_agA", agA_in),
                                ("tap_agB", agB_in), ("tap_x2", x2_d), ("tap_h", h_d),
                                ("tap_agC", agCq_in), ("tap_agD", agD_in),
                                ("tap_x1", x1_d), ("tap_h2", h2_d),
                                ("tap_ctr", ctr_d), ("tap_li2", li2_d)]:
                    inst = g.dma_start(out=taps[nm], in_=src)
                    sO.inc(inst, 16)

    with nc.Block("fin") as blk:
        @blk.gpsimd
        def _(g):
            g.wait_ge(sO.h, sO.n)
    return nc


_NC_CACHE = [None]


def kernel(**inputs):
    i = {k: np.ascontiguousarray(np.asarray(v, dtype=np.float32))
         for k, v in inputs.items()}
    x = i["L0_attn_x"].reshape(S_FULL, D)
    if _NC_CACHE[0] is None:
        _NC_CACHE[0] = build_program()
    nc = _NC_CACHE[0]

    shared = {
        "Wq": i["L0_attn_Wq"], "Wk": i["L0_attn_Wk"], "Wv": i["L0_attn_Wv"],
        "Wo": i["L0_attn_Wo"], "bq": i["L0_attn_bq"], "bk": i["L0_attn_bk"],
        "bv": i["L0_attn_bv"], "bo": i["L0_attn_bo"],
        "fwq": i["L0_ffn_wq"], "fwk": i["L0_ffn_wk"], "fwv": i["L0_ffn_wv"],
        "fwo": i["L0_ffn_wo"], "rw": i["L0_ffn_router_w"],
        "wg": i["L0_ffn_wg"], "wu": i["L0_ffn_wu"], "wd": i["L0_ffn_wd"],
        "ln1w_r": np.ascontiguousarray(np.tile(i["L0_ffn_ln1_w"][None, :], (128, 1))),
        "ln2w_r": np.ascontiguousarray(np.tile(i["L0_ffn_ln2_w"][None, :], (128, 1))),
        "qnw_r": np.ascontiguousarray(np.tile(i["L0_ffn_qnorm_w"][None, :], (128, 1))),
        "knw_r": np.ascontiguousarray(np.tile(i["L0_ffn_knorm_w"][None, :], (128, 1))),
        "rbias_r": np.ascontiguousarray(np.tile(i["L0_ffn_router_bias"][None, :], (128, 1))),
        "iidx_in": np.ascontiguousarray(
            np.tile(np.arange(32, dtype=np.float32)[None, :], (128, 1))),
    }
    in_maps = []
    for c in range(NCO):
        m = dict(shared)
        m["x_sl"] = np.ascontiguousarray(x[c * TS:(c + 1) * TS])
        m["pos_in"] = np.arange(c * TS, (c + 1) * TS, dtype=np.float32)
        in_maps.append(m)

    res = run_bass_kernel_spmd(nc, in_maps, core_ids=list(range(NCO)))
    y = np.concatenate([res.results[c]["y"] for c in range(NCO)], axis=0)
    return y.reshape(1, S_FULL, D)
